# version 02
# baseline (speedup 1.0000x reference)
import numpy as np
import ml_dtypes
from contextlib import ExitStack
import concourse.bass as bass
import concourse.mybir as mybir
from concourse.bass_utils import run_bass_kernel_spmd

F32 = mybir.dt.float32
BF16 = mybir.dt.bfloat16
I32 = mybir.dt.int32
AF = mybir.ActivationFunctionType
ALU = mybir.AluOpType
AX = mybir.AxisListType
BF = ml_dtypes.bfloat16

D = 1024
KC = 8
T = 2048
NCORE = 8


class Sem:
    __slots__ = ("h", "n", "name")

    def __init__(self, h, name):
        self.h = h
        self.n = 0
        self.name = name


class Prog:

    def __init__(self, nc, es):
        self.nc = nc
        self.es = es
        self.q = {k: [] for k in "tvcgy"}

    def sem(self, name):
        s = Sem(self.es.enter_context(self.nc.semaphore(name)), name)
        if not hasattr(self, "sem_handles"):
            self.sem_handles = []
        self.sem_handles.append(s.h)
        return s

    def sbuf(self, name, shape, dtype):
        return self.es.enter_context(self.nc.sbuf_tensor(name, shape, dtype))

    def psum(self, name, shape, dtype=F32):
        return self.es.enter_context(self.nc.psum_tensor(name, shape, dtype))

    def step(self, eng, build, waits=(), inc=None, k=1):
        wl = [(s.h, v) for (s, v) in waits]
        ih = inc.h if inc is not None else None

        def fn(e):
            for h, v in wl:
                e.wait_ge(h, v)
            last = build(e)
            if ih is not None:
                last.then_inc(ih, k)

        self.q[eng].append(fn)
        if inc is not None:
            inc.n += k
            return inc.n
        return None

    def run_block(self):
        self.nc.all_engine_barrier()
        for h in getattr(self, "sem_handles", []):
            self.nc.gpsimd.sem_clear(h)
        self.nc.all_engine_barrier()
        blk = self.es.enter_context(self.nc.Block())

        def body(key):
            def f(eng):
                for fn in self.q[key]:
                    fn(eng)
            return f

        blk.tensor(body("t"))
        blk.vector(body("v"))
        blk.scalar(body("c"))
        blk.gpsimd(body("g"))
        blk.sync(body("y"))


def emit_ln_stats(p, E, X, xh, TH, eps_ap, wait_first):
    if not isinstance(X, (list, tuple)):
        X = [X[:, k * TH:(k + 1) * TH] for k in range(KC)]
    if not isinstance(xh, (list, tuple)):
        xh = [xh[:, k * TH:(k + 1) * TH] for k in range(KC)]
    nTC = TH // 512
    s_t, s_v, s_c = E["s_t"], E["s_v"], E["s_c"]
    SQ, P1, P2, P3, P4 = E["SQ"], E["P1"], E["P2"], E["P3"], E["P4"]
    ONES = E["ONES"]
    R_MU, R_VAR, R_RSTD, R_TMP = E["R_MU"], E["R_VAR"], E["R_RSTD"], E["R_TMP"]
    R16_MU, R16_RSTD = E["R16_MU"], E["R16_RSTD"]
    MU_B, RSTD_B = E["MU_B"], E["RSTD_B"]
    mm_prev = s_t.n
    for k in range(KC):
        w = list(wait_first(k))
        if k >= 1:
            w.append((s_t, mm_prev))

        def sq_cast(e, k=k):
            e.activation(SQ[:, 0:TH], X[k], AF.Square)
            return e.activation(xh[k], X[k], AF.Copy)
        p.step("c", sq_cast, waits=w, inc=s_c)
        cw = s_c.n

        def mm(e, k=k):
            last = None
            for tc in range(nTC):
                sl = slice(tc * 512, (tc + 1) * 512)
                last = e.matmul(P1[0:1, sl], ONES[:, 0:1], xh[k][:, sl],
                                start=(k == 0), stop=(k == KC - 1))
                last = e.matmul(P2[0:1, sl], ONES[:, 0:1], SQ[:, sl],
                                start=(k == 0), stop=(k == KC - 1))
            return last
        mm_prev = p.step("t", mm, waits=[(s_c, cw)], inc=s_t)
    tmm = s_t.n

    def row_ops(e):
        e.tensor_single_scalar(R_MU[0:1, 0:TH], P1[0:1, 0:TH], 1.0 / D,
                               ALU.mult)
        e.tensor_tensor(R_TMP[0:1, 0:TH], R_MU[0:1, 0:TH], R_MU[0:1, 0:TH],
                        ALU.mult)
        return e.scalar_tensor_tensor(
            R_VAR[0:1, 0:TH], P2[0:1, 0:TH], 1.0 / D, R_TMP[0:1, 0:TH],
            ALU.mult, ALU.subtract)
    p.step("v", row_ops, waits=[(s_t, tmm)], inc=s_v)
    vr = s_v.n
    p.step("c", lambda e: e.activation(R_TMP[0:1, 0:TH], R_VAR[0:1, 0:TH],
                                       AF.Sqrt, bias=eps_ap),
           waits=[(s_v, vr)], inc=s_c)
    cr = s_c.n

    p.step("v", lambda e: e.reciprocal(R_RSTD[0:1, 0:TH], R_TMP[0:1, 0:TH]),
           waits=[(s_c, cr)], inc=s_v)
    vrc = s_v.n

    def r16(e):
        e.activation(R16_MU[0:1, 0:TH], R_MU[0:1, 0:TH], AF.Copy)
        return e.activation(R16_RSTD[0:1, 0:TH], R_RSTD[0:1, 0:TH], AF.Copy)
    p.step("c", r16, waits=[(s_v, vrc)], inc=s_c)
    vb = s_c.n

    def bcast(e):
        last = None
        for tc in range(nTC):
            sl = slice(tc * 512, (tc + 1) * 512)
            last = e.matmul(P3[:, sl], ONES[0:1, :], R16_MU[0:1, sl],
                            start=True, stop=True)
            last = e.matmul(P4[:, sl], ONES[0:1, :], R16_RSTD[0:1, sl],
                            start=True, stop=True)
        return last
    p.step("t", bcast, waits=[(s_c, vb)], inc=s_t)
    tb = s_t.n
    p.step("c", lambda e: e.activation(MU_B[:, 0:TH], P3[:, 0:TH], AF.Copy),
           waits=[(s_t, tb)], inc=s_c)
    p.step("c", lambda e: e.activation(RSTD_B[:, 0:TH], P4[:, 0:TH], AF.Copy),
           inc=s_c)
    return s_c.n


def build_l0ffn(T_=T, debug=False, router_passes=3):
    TH = T_ // 2
    nTC = TH // 512
    NN = 32
    nMT = TH // 128
    nc = bass.Bass("TRN2", target_bir_lowering=False)
    xT = nc.dram_tensor("xT", [D, T_], F32, kind="ExternalInput").ap()
    w1r = nc.dram_tensor("w1r", [4, NN, 128, KC, 128], F32,
                         kind="ExternalInput").ap()
    b1t = nc.dram_tensor("b1t", [128, 4 * NN], F32, kind="ExternalInput").ap()
    wor = nc.dram_tensor("wor", [KC, 128, NN, 128], F32,
                         kind="ExternalInput").ap()
    bo_c = nc.dram_tensor("bo_c", [128, KC], F32, kind="ExternalInput").ap()
    lng_c = nc.dram_tensor("lng_c", [128, KC], F32, kind="ExternalInput").ap()
    lnb_c = nc.dram_tensor("lnb_c", [128, KC], F32, kind="ExternalInput").ap()
    wrp = nc.dram_tensor("wr", [128, KC * 4], F32, kind="ExternalInput").ap()
    ident = nc.dram_tensor("ident", [128, 128], F32, kind="ExternalInput").ap()
    yT = nc.dram_tensor("yT", [D, T_], F32, kind="ExternalOutput").ap()
    if debug:
        TH_ = T_ // 2
        dbg_xlh = nc.dram_tensor("dbg_xlh", [D, TH_], F32, kind="ExternalOutput").ap()
        dbg_mask = nc.dram_tensor("dbg_mask", [128, 32], F32, kind="ExternalOutput").ap()
        dbg_maskb = nc.dram_tensor("dbg_maskb", [128, 4 * TH_], F32, kind="ExternalOutput").ap()
        dbg_inter = nc.dram_tensor("dbg_inter", [128, 32 * TH_], F32, kind="ExternalOutput").ap()
        dbg_y = nc.dram_tensor("dbg_y", [D, TH_], F32, kind="ExternalOutput").ap()
        dbg_mu = nc.dram_tensor("dbg_mu", [128, TH_], F32, kind="ExternalOutput").ap()
        dbg_rstd = nc.dram_tensor("dbg_rstd", [128, TH_], F32, kind="ExternalOutput").ap()
        dbg_x0 = nc.dram_tensor("dbg_x0", [D, TH_], F32, kind="ExternalOutput").ap()
        dbg_lg = nc.dram_tensor("dbg_lg", [128, 32], F32, kind="ExternalOutput").ap()
        dbg_xll = nc.dram_tensor("dbg_xll", [128, TH_], F32, kind="ExternalOutput").ap()
        dbg_wrl = nc.dram_tensor("dbg_wrl", [128, 64], F32, kind="ExternalOutput").ap()

    es = ExitStack()
    with es:
        p = Prog(nc, es)
        X = p.sbuf("X", [128, KC * TH], F32)
        XRE = p.sbuf("XRE", [128, 2 * TH], F32)
        XL1 = p.sbuf("XL1", [128, TH], F32)
        XLH = p.sbuf("XLH", [128, KC * TH], BF16)
        XLL = p.sbuf("XLL", [128, KC * TH], BF16)
        INTER = p.sbuf("INTER", [128, NN * TH], BF16)
        MASKB = p.sbuf("MASKB", [128, 4 * TH], BF16)
        MU_B = p.sbuf("MU_B", [128, TH], F32)
        RSTD_B = p.sbuf("RSTD_B", [128, TH], F32)
        SQ = p.sbuf("SQ", [128, TH], BF16)
        W1B = [p.sbuf(f"W1B{i}", [128, KC * 128], BF16) for i in range(2)]
        GE = [p.sbuf(f"GE{i}", [128, TH], BF16) for i in range(2)]
        GM = p.sbuf("GM", [128, 512], BF16)
        B1 = p.sbuf("B1", [128, 4 * NN], F32)
        BO = p.sbuf("BO", [128, KC], F32)
        LNG = p.sbuf("LNG", [128, KC], F32)
        LNB = p.sbuf("LNB", [128, KC], F32)
        WRH = p.sbuf("WRH", [128, KC * 4], BF16)
        WRL = p.sbuf("WRL", [128, KC * 4], BF16)
        WR32 = p.sbuf("WR32", [128, KC * 4], F32)
        ID16 = p.sbuf("ID16", [128, 128], BF16)
        ONES = p.sbuf("ONES", [128, 128], BF16)
        R_MU = p.sbuf("R_MU", [1, TH], F32)
        R_VAR = p.sbuf("R_VAR", [1, TH], F32)
        R_RSTD = p.sbuf("R_RSTD", [1, TH], F32)
        R_TMP = p.sbuf("R_TMP", [1, TH], F32)
        R16_MU = p.sbuf("R16_MU", [1, TH], BF16)
        R16_RSTD = p.sbuf("R16_RSTD", [1, TH], BF16)
        MROW = p.sbuf("MROW", [1, TH], BF16)
        MASK_TM = p.sbuf("MASK_TM", [128, nMT * 4], F32)
        MCOL16 = p.sbuf("MCOL16", [128, nMT * 4], BF16)
        LGS = p.sbuf("LGS", [128, 32], F32)
        RMX = p.sbuf("RMX", [128, 8], F32)
        WSEL = p.sbuf("WSEL", [128, 8], F32)
        EPS5 = p.sbuf("EPS5", [1, 1], F32)
        EPS12 = p.sbuf("EPS12", [1, 1], F32)
        WOB = [XLL[:, 0:4096], XLL[:, 4096:8192]]
        OUT32 = XLL[:].bitcast(F32)[:, 0:TH]

        P1 = p.psum("P1", [128, TH])
        P2 = p.psum("P2", [128, TH])
        P3 = p.psum("P3", [128, TH])
        P4 = p.psum("P4", [128, TH])

        s_t = p.sem("s_t")
        s_v = p.sem("s_v")
        s_c = p.sem("s_c")
        s_x = p.sem("s_x")
        s_w = p.sem("s_w")
        s_st = p.sem("s_st")

        E = dict(s_t=s_t, s_v=s_v, s_c=s_c, SQ=SQ, P1=P1, P2=P2, P3=P3,
                 P4=P4, ONES=ONES, R_MU=R_MU, R_VAR=R_VAR, R_RSTD=R_RSTD,
                 R_TMP=R_TMP, R16_MU=R16_MU, R16_RSTD=R16_RSTD,
                 MU_B=MU_B, RSTD_B=RSTD_B)

        p.step("y", lambda e: e.dma_start(out=B1[:], in_=b1t), inc=s_x, k=16)
        p.step("y", lambda e: e.dma_start(out=BO[:], in_=bo_c), inc=s_x, k=16)
        p.step("y", lambda e: e.dma_start(out=LNG[:], in_=lng_c), inc=s_x, k=16)
        p.step("y", lambda e: e.dma_start(out=LNB[:], in_=lnb_c), inc=s_x, k=16)
        p.step("y", lambda e: e.dma_start(out=WR32[:], in_=wrp),
               inc=s_x, k=16)
        p.step("g", lambda e: e.dma_start(out=ID16[:], in_=ident),
               inc=s_w, k=16)
        cl = s_x.n
        def const_g(e):
            e.memset(ONES[:], 1.0)
            e.memset(EPS5[:], 1e-5)
            return e.memset(EPS12[:], 1e-12)
        p.step("g", const_g, inc=s_w)
        gmemset = s_w.n
        p.step("c", lambda e: e.activation(WRH[:], WR32[:], AF.Copy),
               waits=[(s_x, cl)], inc=s_c)
        cwrh = s_c.n
        p.step("v", lambda e: e.scalar_tensor_tensor(
            WRL[:], WRH[:], -1.0, WR32[:], ALU.mult, ALU.add),
            waits=[(s_c, cwrh)], inc=s_v)
        const_v = s_v.n
        const_c = s_c.n
        const_g = gmemset

        for half in range(2):
            t0 = half * TH
            xloads = []
            for k in range(KC):
                m = p.step("y", (lambda k, t0: lambda e: e.dma_start(
                    out=X[:, k * TH:(k + 1) * TH],
                    in_=xT[k * 128:(k + 1) * 128, t0:t0 + TH]))(k, t0),
                    waits=[(s_st, 16 * KC * half)], inc=s_x, k=16)
                xloads.append(m)

            def wait_first(k, xloads=xloads):
                w = [(s_x, xloads[k])]
                if k == 0:
                    w.append((s_v, const_v))
                    w.append((s_w, const_g))
                return w
            cmu = emit_ln_stats(p, E, X, XLH, TH, EPS5[0:1, 0:1], wait_first)
            for k in range(KC):
                sl = slice(k * TH, (k + 1) * TH)
                x1 = XL1[:, (k % 2) * 0:]

                def applya(e, sl=sl, k=k):
                    e.tensor_tensor(XL1[:], X[:, sl], MU_B[:], ALU.subtract)
                    return e.tensor_tensor(XL1[:], XL1[:], RSTD_B[:], ALU.mult)
                w = [(s_c, cmu)] if k == 0 else [(s_c, s_c.n)]
                p.step("v", applya, waits=w, inc=s_v)
                va = s_v.n
                p.step("c", (lambda sl: lambda e: e.activation(
                    XLH[:, sl], XL1[:], AF.Copy))(sl),
                    waits=[(s_v, va)], inc=s_c)
                ca = s_c.n
                p.step("v", (lambda sl: lambda e: e.scalar_tensor_tensor(
                    XLL[:, sl], XLH[:, sl], -1.0, XL1[:],
                    ALU.mult, ALU.add))(sl), waits=[(s_c, ca)], inc=s_v)
            vln = s_v.n
            cln = s_c.n
            if debug and half == 0:
                p.step("g", lambda e: e.dma_start(out=dbg_mu, in_=MU_B[:]),
                       waits=[(s_v, vln)], inc=s_w, k=16)
                p.step("g", lambda e: e.dma_start(out=dbg_rstd, in_=RSTD_B[:]),
                       waits=[(s_v, vln)], inc=s_w, k=16)
                for k in range(KC):
                    p.step("g", (lambda k: lambda e: e.dma_start(
                        out=dbg_x0[k * 128:(k + 1) * 128, :],
                        in_=X[:, k * TH:(k + 1) * TH]))(k),
                        waits=[(s_v, vln)], inc=s_w, k=16)
                for k in range(KC):
                    p.step("g", (lambda k: lambda e: e.dma_start(
                        out=dbg_xlh[k * 128:(k + 1) * 128, :],
                        in_=XLH[:, k * TH:(k + 1) * TH]))(k),
                        waits=[(s_v, vln)], inc=s_w, k=16)

            def router_mm(e):
                last = None
                for mt in range(nMT):
                    psl = P1[:, mt * 4:(mt + 1) * 4]
                    passes = ((XLH, WRH, True, False),
                              (XLH, WRL, False, False),
                              (XLL, WRH, False, True))[:router_passes]
                    passes = tuple(
                        (a, b, st, (i == len(passes) - 1))
                        for i, (a, b, st, _) in enumerate(passes))
                    for (lh, rh, st, fi) in passes:
                        for k in range(KC):
                            last = e.matmul(
                                psl,
                                lh[:, k * TH + mt * 128:k * TH + (mt + 1) * 128],
                                rh[:, k * 4:(k + 1) * 4],
                                start=(st and k == 0),
                                stop=(fi and k == KC - 1))
                return last
            p.step("t", router_mm, waits=[(s_v, vln), (s_c, cln)], inc=s_t)
            trt = s_t.n
            clg = p.step("c", lambda e: e.activation(LGS[:], P1[:, 0:32],
                                                     AF.Copy),
                         waits=[(s_t, trt)], inc=s_c)
            if debug and half == 0:
                p.step("g", lambda e: e.dma_start(out=dbg_lg, in_=LGS[:]),
                       waits=[(s_c, clg)], inc=s_w, k=16)
                p.step("g", lambda e: e.dma_start(out=dbg_xll, in_=XLL[:, 0:TH]),
                       waits=[(s_c, clg)], inc=s_w, k=16)
                p.step("g", lambda e: e.dma_start(out=dbg_wrl[:, 0:32], in_=WRH[:]),
                       waits=[(s_c, clg)], inc=s_w, k=16)
                p.step("g", lambda e: e.dma_start(out=dbg_wrl[:, 32:64], in_=WRL[:]),
                       waits=[(s_c, clg)], inc=s_w, k=16)
            def mask_ops(e):
                lg3 = LGS[:].rearrange("p (m e) -> p m e", e=4)
                e.tensor_reduce(RMX[:], lg3, AX.X, ALU.max)
                e.tensor_scalar(WSEL[:], RMX[:], 0.0, 1.0, ALU.mult, ALU.add)
                last = None
                for ex in range(4):
                    for mt in range(nMT):
                        e.tensor_tensor(MASK_TM[:, mt * 4 + ex:mt * 4 + ex + 1],
                                        LGS[:, mt * 4 + ex:mt * 4 + ex + 1],
                                        RMX[:, mt:mt + 1], ALU.is_ge)
                    for mt in range(nMT):
                        e.tensor_tensor(MASK_TM[:, mt * 4 + ex:mt * 4 + ex + 1],
                                        MASK_TM[:, mt * 4 + ex:mt * 4 + ex + 1],
                                        WSEL[:, mt:mt + 1], ALU.mult)
                    for mt in range(nMT):
                        last = e.tensor_tensor(
                            WSEL[:, mt:mt + 1], WSEL[:, mt:mt + 1],
                            MASK_TM[:, mt * 4 + ex:mt * 4 + ex + 1],
                            ALU.subtract)
                return last
            p.step("v", mask_ops, waits=[(s_c, clg)], inc=s_v)
            vmk0 = s_v.n
            p.step("c", lambda e: e.activation(MCOL16[:], MASK_TM[:], AF.Copy),
                   waits=[(s_v, vmk0)], inc=s_c)
            vmk_c = s_c.n
            vmk = s_v.n
            if debug and half == 0:
                p.step("g", lambda e: e.dma_start(out=dbg_mask, in_=MASK_TM[:]),
                       waits=[(s_v, vmk)], inc=s_w, k=16)
            cmask = s_c.n
            for ex in range(4):
                def mrow_mm(e, ex=ex):
                    last = None
                    for mt in range(nMT):
                        last = e.matmul(
                            P2[0:1, mt * 128:(mt + 1) * 128],
                            MCOL16[:, mt * 4 + ex:mt * 4 + ex + 1],
                            ID16[:], start=True, stop=True)
                    return last
                w = [(s_c, vmk_c)] if ex == 0 else [(s_c, cmask)]
                p.step("t", mrow_mm, waits=w, inc=s_t)
                tmr = s_t.n
                p.step("c", lambda e: e.activation(MROW[0:1, :],
                                                   P2[0:1, 0:TH], AF.Copy),
                       waits=[(s_t, tmr)], inc=s_c)
                vmr = s_c.n

                def mb_mm(e):
                    last = None
                    for tc in range(nTC):
                        sl = slice(tc * 512, (tc + 1) * 512)
                        last = e.matmul(P3[:, sl], ONES[0:1, :],
                                        MROW[0:1, sl], start=True, stop=True)
                    return last
                p.step("t", mb_mm, waits=[(s_c, vmr)], inc=s_t)
                tmb = s_t.n
                cmask = p.step("c", (lambda ex: lambda e: e.activation(
                    MASKB[:, ex * TH:(ex + 1) * TH], P3[:, 0:TH],
                    AF.Copy))(ex), waits=[(s_t, tmb)], inc=s_c)

            order = [(n, ex) for n in range(NN) for ex in range(4)]
            w1_load = [None] * len(order)
            w1_mm = [None] * len(order)
            gelu_m = [None] * len(order)
            merge_m = [None] * len(order)
            for i, (n, ex) in enumerate(order):
                buf = W1B[i % 2]
                w = []
                if i >= 2:
                    w.append((s_t, w1_mm[i - 2]))
                w1_load[i] = p.step("g", (lambda n, ex, buf: lambda e:
                    e.dma_start(out=buf[:],
                                in_=w1r[ex, n].rearrange("p k i -> p (k i)")))(
                        n, ex, buf), waits=w, inc=s_w, k=16)
                hp = P1 if (i % 2 == 0) else P2

                def w1mm(e, buf=buf, hp=hp):
                    last = None
                    for k in range(KC):
                        for tc in range(nTC):
                            last = e.matmul(
                                hp[:, tc * 512:(tc + 1) * 512],
                                buf[:, k * 128:(k + 1) * 128],
                                XLH[:, k * TH + tc * 512:k * TH + (tc + 1) * 512],
                                start=(k == 0), stop=(k == KC - 1))
                    return last
                w = [(s_w, w1_load[i]), (s_c, cmask)]
                if i >= 2:
                    w.append((s_c, gelu_m[i - 2]))
                w1_mm[i] = p.step("t", w1mm, waits=w, inc=s_t)
                w = [(s_t, w1_mm[i])]
                if i >= 2:
                    w.append((s_v, merge_m[i - 2]))
                gelu_m[i] = p.step("c", (lambda n, ex, hp, i: lambda e:
                    e.activation(GE[i % 2][:], hp[:, 0:TH], AF.Gelu,
                                 bias=B1[:, ex * NN + n:ex * NN + n + 1]))(
                        n, ex, hp, i), waits=w, inc=s_c)

                def merge(e, n=n, ex=ex, i=i):
                    last = None
                    for tc in range(nTC):
                        sl = slice(tc * 512, (tc + 1) * 512)
                        dst = INTER[:, n * TH + tc * 512:n * TH + (tc + 1) * 512]
                        msk = MASKB[:, ex * TH + tc * 512:
                                    ex * TH + (tc + 1) * 512]
                        if ex == 0:
                            last = e.tensor_tensor(dst, GE[i % 2][:, sl], msk,
                                                   ALU.mult)
                        else:
                            e.tensor_tensor(GM[:], GE[i % 2][:, sl], msk,
                                            ALU.mult)
                            last = e.tensor_tensor(dst, dst, GM[:], ALU.add)
                    return last
                merge_m[i] = p.step("v", merge, waits=[(s_c, gelu_m[i])],
                                    inc=s_v)
            vint = s_v.n
            if debug and half == 0:
                p.step("g", lambda e: e.dma_start(out=dbg_maskb, in_=MASKB[:]),
                       waits=[(s_c, cmask)], inc=s_w, k=16)
                p.step("g", lambda e: e.dma_start(out=dbg_inter, in_=INTER[:]),
                       waits=[(s_v, vint)], inc=s_w, k=16)

            wo_load = [None] * KC
            wo_mm = [None] * KC
            ep_m = [None] * KC
            for m in range(KC):
                w = [(s_t, trt), (s_v, vmk)]
                if m >= 2:
                    w.append((s_t, wo_mm[m - 2]))
                wo_load[m] = p.step("g", (lambda m: lambda e: e.dma_start(
                    out=WOB[m % 2], in_=wor[m].rearrange("p k c -> p (k c)")))(
                        m), waits=w, inc=s_w, k=16)
                po = P3 if (m % 2 == 0) else P4

                def womm(e, m=m, po=po):
                    last = None
                    buf = WOB[m % 2]
                    for k in range(NN):
                        for tc in range(nTC):
                            last = e.matmul(
                                po[:, tc * 512:(tc + 1) * 512],
                                buf[:, k * 128:(k + 1) * 128],
                                INTER[:, k * TH + tc * 512:k * TH + (tc + 1) * 512],
                                start=(k == 0), stop=(k == NN - 1))
                    return last
                w = [(s_w, wo_load[m]), (s_v, vint)]
                if m >= 2:
                    w.append((s_v, ep_m[m - 2]))
                wo_mm[m] = p.step("t", womm, waits=w, inc=s_t)

                def ep1(e, m=m, po=po):
                    sl = slice(m * TH, (m + 1) * TH)
                    e.tensor_tensor(XL1[:], X[:, sl], MU_B[:], ALU.subtract)
                    e.tensor_tensor(XL1[:], XL1[:], RSTD_B[:], ALU.mult)
                    return e.scalar_tensor_tensor(
                        X[:, sl], po[:, 0:TH], BO[:, m:m + 1], XL1[:],
                        ALU.add, ALU.add)
                ep_m[m] = p.step("v", ep1, waits=[(s_t, wo_mm[m])], inc=s_v)
            vy = s_v.n
            if debug and half == 0:
                for k in range(KC):
                    p.step("g", (lambda k: lambda e: e.dma_start(
                        out=dbg_y[k * 128:(k + 1) * 128, :],
                        in_=X[:, k * TH:(k + 1) * TH]))(k),
                        waits=[(s_v, vy)], inc=s_w, k=16)

            def wait_first2(k, vy=vy):
                return [(s_v, vy)]
            cmu2 = emit_ln_stats(p, E, X, XLH, TH, EPS12[0:1, 0:1],
                                 wait_first2)
            fin_marks = []
            for k in range(KC):
                xw = [(s_v, vy)] if k < 2 else [(s_v, fin_marks[k - 2])]
                xr = p.step("y", (lambda k, t0: lambda e: e.dma_start(
                    out=XRE[:, (k % 2) * TH:(k % 2 + 1) * TH],
                    in_=xT[k * 128:(k + 1) * 128, t0:t0 + TH]))(k, t0),
                    waits=xw, inc=s_x, k=16)

                def fin(e, k=k):
                    sl = slice(k * TH, (k + 1) * TH)
                    e.tensor_tensor(OUT32[:, 0:TH], X[:, sl], MU_B[:],
                                    ALU.subtract)
                    e.tensor_tensor(OUT32[:, 0:TH], OUT32[:, 0:TH],
                                    RSTD_B[:], ALU.mult)
                    e.tensor_scalar(OUT32[:, 0:TH], OUT32[:, 0:TH],
                                    LNG[:, k:k + 1], LNB[:, k:k + 1],
                                    ALU.mult, ALU.add)
                    e.tensor_single_scalar(OUT32[:, 0:TH], OUT32[:, 0:TH],
                                           0.5, ALU.mult)
                    return e.tensor_tensor(
                        OUT32[:, 0:TH], OUT32[:, 0:TH],
                        XRE[:, (k % 2) * TH:(k % 2 + 1) * TH], ALU.add)
                vout = p.step("v", fin,
                              waits=[(s_c, cmu2), (s_x, xr),
                                     (s_st, 16 * (KC * half + k))], inc=s_v)
                fin_marks.append(vout)
                p.step("y", (lambda k, t0: lambda e: e.dma_start(
                    out=yT[k * 128:(k + 1) * 128, t0:t0 + TH],
                    in_=OUT32[:, 0:TH]))(k, t0),
                    waits=[(s_v, vout)], inc=s_st, k=16)

        def final(e):
            e.wait_ge(s_st.h, s_st.n)
        p.q["g"].append(final)
        p.run_block()
    return nc


def l0ffn_host_inputs(W1, b1, Wo, bo, lng, lnb, Wr):
    w1r = np.ascontiguousarray(
        W1.reshape(4, 8, 128, 32, 128).transpose(0, 3, 2, 1, 4))
    b1t = np.ascontiguousarray(
        b1.reshape(4, 32, 128).transpose(2, 0, 1).reshape(128, 128))
    wor = np.ascontiguousarray(
        Wo.reshape(32, 128, 8, 128).transpose(2, 1, 0, 3))
    bo_c = np.ascontiguousarray(bo.reshape(8, 128).T)
    lng_c = np.ascontiguousarray(lng.reshape(8, 128).T)
    lnb_c = np.ascontiguousarray(lnb.reshape(8, 128).T)
    ident = np.eye(128, dtype=np.float32)
    return dict(w1r=w1r, b1t=b1t, wor=wor, bo_c=bo_c, lng_c=lng_c,
                lnb_c=lnb_c,
                wr=np.ascontiguousarray(
                    Wr.reshape(8, 128, 4).transpose(1, 0, 2).reshape(128, 32)),
                ident=ident)




def build_l0attn(T_=T, upto=None, ngroups=KC, nsets=5):
    TH = T_ // 2
    nc = bass.Bass("TRN2", target_bir_lowering=False)
    xT = nc.dram_tensor("xT", [D, T_], F32, kind="ExternalInput").ap()
    xpT = nc.dram_tensor("xpT", [D, T_], F32, kind="ExternalInput").ap()
    wq_r = nc.dram_tensor("wq_r", [24, 128, KC, 128], F32,
                          kind="ExternalInput").ap()
    wo_r = nc.dram_tensor("wo_r", [KC, 128, KC, 128], F32,
                          kind="ExternalInput").ap()
    dmask = nc.dram_tensor("dmask", [128, 4 * 512], F32,
                           kind="ExternalInput").ap()
    flag = nc.dram_tensor("flag", [1, 1], F32, kind="ExternalInput").ap()
    ident = nc.dram_tensor("ident", [128, 128], F32,
                           kind="ExternalInput").ap()
    yT = nc.dram_tensor("yT", [D, T_], F32, kind="ExternalOutput").ap()

    es = ExitStack()
    with es:
        p = Prog(nc, es)
        YATT = p.sbuf("YATT", [128, KC * T_], BF16)
        X = YATT[:].bitcast(F32)
        XLH = p.sbuf("XLH", [128, KC * T_], BF16)
        XPH = p.sbuf("XPH", [128, KC * T_], BF16)
        MU_B = p.sbuf("MU_B", [128, TH], F32)
        RSTD_B = p.sbuf("RSTD_B", [128, TH], F32)
        SQ = p.sbuf("SQ", [128, TH], BF16)
        XL1 = p.sbuf("XL1", [128, TH], F32)
        QA = [p.sbuf(f"QA{h}", [65, T_], BF16) for h in range(2)]
        KA = [p.sbuf(f"KA{h}", [65, T_], BF16) for h in range(2)]
        KP = [p.sbuf(f"KP{h}", [65, T_], BF16) for h in range(2)]
        VT = [p.sbuf(f"VT{h}", [128, 16 * 65], BF16) for h in range(2)]
        VPT = [p.sbuf(f"VPT{h}", [128, 16 * 65], BF16) for h in range(2)]
        VSTG = p.sbuf("VSTG", [128, T_], F32)
        PT = [p.sbuf(f"PT{i}", [128, 512], BF16) for i in range(2)]
        WQB = [p.sbuf(f"WQB{i}", [128, 3 * KC * 128], BF16) for i in range(2)]
        WOB = [p.sbuf(f"WOB{i}", [128, KC * 128], BF16) for i in range(2)]
        AB = p.sbuf("AB", [128, 4], F32)
        M3 = p.sbuf("M3", [4, 1], F32)
        R1 = p.sbuf("R1", [1, 4], F32)
        M1 = p.sbuf("M1", [1, 1], F32)
        CS = p.sbuf("CS", [1, 4], F32)
        RL32 = p.sbuf("RL32", [1, 512], F32)
        RL16 = p.sbuf("RL16", [1, 512], BF16)
        RBS = p.sbuf("RBS", [64, 512], F32)
        DM = p.sbuf("DM", [128, 4 * 512], F32)
        FLG = p.sbuf("FLG", [1, 1], F32)
        ONEROW = p.sbuf("ONEROW", [1, T_], BF16)
        ID32 = p.sbuf("ID32", [128, 128], F32)
        ID16 = p.sbuf("ID16", [128, 128], BF16)
        ONES = p.sbuf("ONES", [128, 128], BF16)
        XRE = XPH[:].bitcast(F32)[:, 0:2 * T_]
        OUT32 = XPH[:].bitcast(F32)[:, 2 * T_:3 * T_]
        R_MU = p.sbuf("R_MU", [1, TH], F32)
        R_VAR = p.sbuf("R_VAR", [1, TH], F32)
        R_RSTD = p.sbuf("R_RSTD", [1, TH], F32)
        R_TMP = p.sbuf("R_TMP", [1, TH], F32)
        R16_MU = p.sbuf("R16_MU", [1, TH], BF16)
        R16_RSTD = p.sbuf("R16_RSTD", [1, TH], BF16)
        EPS5 = p.sbuf("EPS5", [1, 1], F32)

        PRJ = p.psum("PRJ", [128, 2048])
        SC1 = p.psum("SC1", [128, 512])
        SC2 = p.psum("SC2", [128, 512])
        PO = p.psum("PO", [128, 512])
        RB = p.psum("RB", [128, 512])

        s_t = p.sem("s_t")
        s_v = p.sem("s_v")
        s_c = p.sem("s_c")
        s_x = p.sem("s_x")
        s_w = p.sem("s_w")
        s_st = p.sem("s_st")

        E = dict(s_t=s_t, s_v=s_v, s_c=s_c, SQ=SQ,
                 P1=PRJ[:, 0:1024], P2=PRJ[:, 1024:2048],
                 P3=SC1, P4=SC2, ONES=ONES, R_MU=R_MU, R_VAR=R_VAR,
                 R_RSTD=R_RSTD, R_TMP=R_TMP, R16_MU=R16_MU,
                 R16_RSTD=R16_RSTD, MU_B=MU_B, RSTD_B=RSTD_B)
        E["P1"] = SC1
        E["P2"] = SC2
        E["P3"] = PRJ[:, 0:1024]
        E["P4"] = PRJ[:, 1024:2048]
        E["P1"] = PRJ[:, 0:1024]
        E["P2"] = PRJ[:, 1024:2048]
        E["P3"] = PRJ[:, 0:1024]
        E["P4"] = PRJ[:, 1024:2048]

        p.step("y", lambda e: e.dma_start(out=DM[:], in_=dmask),
               inc=s_x, k=16)
        p.step("y", lambda e: e.dma_start(out=FLG[:], in_=flag), inc=s_x, k=16)
        p.step("y", lambda e: e.dma_start(out=ID32[:], in_=ident),
               inc=s_x, k=16)
        p.step("g", lambda e: e.dma_start(out=ID16[:], in_=ident),
               inc=s_w, k=16)

        def const_g(e):
            e.memset(ONES[:], 1.0)
            e.memset(ONEROW[:], 1.0)
            e.memset(EPS5[:], 1e-5)
            for h in range(2):
                e.memset(QA[h][64:65, :], 1.0)
                e.memset(VT[h][:], 1.0)
                e.memset(VPT[h][:], 1.0)
            return e.memset(XL1[:, 0:1], 0.0)
        p.step("g", const_g, inc=s_w)
        const_w = s_w.n
        cl = s_x.n

        def do_ln(src_ap, dst, base_st):
            for th in range(2):
                t0 = th * TH
                vcur = s_v.n
                ccur = s_c.n
                xloads = []
                for k in range(KC):
                    m = p.step("y", (lambda k, t0: lambda e: e.dma_start(
                        out=X[:, k * TH:(k + 1) * TH],
                        in_=src_ap[k * 128:(k + 1) * 128, t0:t0 + TH]))(k, t0),
                        waits=[(s_v, vcur), (s_c, ccur)], inc=s_x, k=16)
                    xloads.append(m)

                def wait_first(k, xloads=xloads):
                    w = [(s_x, xloads[k])]
                    if k == 0:
                        w.append((s_w, const_w))
                    return w
                xh_chunks = [dst[:, k * T_ + t0: k * T_ + t0 + TH]
                             for k in range(KC)]
                cmu = emit_ln_stats(p, E, X, xh_chunks, TH, EPS5[0:1, 0:1],
                                    wait_first)
                for k in range(KC):
                    def applya(e, k=k, t0=t0):
                        e.tensor_tensor(XL1[:], X[:, k * TH:(k + 1) * TH],
                                        MU_B[:], ALU.subtract)
                        return e.tensor_tensor(XL1[:], XL1[:], RSTD_B[:],
                                               ALU.mult)
                    w = [(s_c, cmu)] if k == 0 else [(s_c, s_c.n)]
                    p.step("v", applya, waits=w, inc=s_v)
                    va = s_v.n
                    p.step("c", (lambda k, t0: lambda e: e.activation(
                        dst[:, k * T_ + t0: k * T_ + t0 + TH], XL1[:],
                        AF.Copy))(k, t0), waits=[(s_v, va)], inc=s_c)
            return s_c.n, s_v.n
        cln1, vln1 = do_ln(xT, XLH, 0)
        cln2, vln2 = do_ln(xpT, XPH, s_v.n)
        cln = s_c.n
        if upto == "ln":
            def final0(e):
                e.wait_ge(s_c.h, cln)
            p.q["g"].append(final0)
            p.run_block()
            return nc

        nKT = T_ // 128
        exp_marks = {}
        yatt_marks = []
        prj_read_m = [0]
        prev_cv = (s_c.n, s_v.n)
        for g in range(ngroups):
            sets = [("q", g, XLH), ("k", 8 + g, XLH), ("v", 16 + g, XLH),
                    ("kp", 8 + g, XPH), ("vp", 16 + g, XPH)][:nsets]
            wb = WQB[g % 2]
            w = [] if g < 2 else [(s_t, prj_read_m[0])]
            wl = p.step("g", (lambda g, wb: lambda e: e.dma_start(
                out=wb[:, 0:KC * 128],
                in_=wq_r[g, :, :, :].rearrange("p k c -> p (k c)")))(g, wb),
                waits=w, inc=s_w, k=16)
            wl2 = p.step("g", (lambda g, wb: lambda e: e.dma_start(
                out=wb[:, KC * 128:2 * KC * 128],
                in_=wq_r[8 + g, :, :, :].rearrange("p k c -> p (k c)")))(
                    g, wb), inc=s_w, k=16)
            wl3 = p.step("g", (lambda g, wb: lambda e: e.dma_start(
                out=wb[:, 2 * KC * 128:3 * KC * 128],
                in_=wq_r[16 + g, :, :, :].rearrange("p k c -> p (k c)")))(
                    g, wb), inc=s_w, k=16)
            wq_done = wl3
            if upto == "projw":
                continue
            prj_marks = []
            for si, (kind, mchunk, src) in enumerate(sets):
                woff = (0 if kind in ("q",) else
                        KC * 128 if kind in ("k", "kp") else 2 * KC * 128)

                def prjmm(e, woff=woff, src=src, wb=wb):
                    last = None
                    for k in range(KC):
                        lh32 = wb[:, woff + k * 128: woff + (k + 1) * 128]
                        for tc in range(4):
                            last = e.matmul(
                                PRJ[:, tc * 512:(tc + 1) * 512],
                                lh32, src[:, k * T_ + tc * 512:
                                          k * T_ + (tc + 1) * 512],
                                start=(k == 0), stop=(k == KC - 1))
                    return last
                w = [(s_w, wq_done), (s_c, cln)]
                if si > 0:
                    w.append((s_c, prj_marks[-1][0]))
                    w.append((s_v, prj_marks[-1][1]))
                else:
                    w.append((s_c, prev_cv[0]))
                    w.append((s_v, prev_cv[1]))
                tm = p.step("t", prjmm, waits=w, inc=s_t)
                if upto == "projm":
                    prj_marks.append((s_c.n, s_v.n, s_t.n))
                    continue
                vm = s_v.n
                if upto != "proje" and kind in ("q", "k", "kp"):
                    col = {"q": 0, "k": 1, "kp": 2}[kind]
                    vm = p.step("v", (lambda col: lambda e: e.tensor_reduce(
                        AB[:, col:col + 1], PRJ[:, 0:2048], AX.X, ALU.max,
                        apply_absolute_value=True))(col),
                        waits=[(s_t, tm)], inc=s_v)
                if upto == "projr":
                    prj_marks.append((s_c.n, s_v.n, s_t.n))
                    continue
                if kind in ("q", "k", "kp"):
                    dst = {"q": QA, "k": KA, "kp": KP}[kind]
                    qs = 0.125 if kind == "q" else 1.0

                    def evac(e, dst=dst, qs=qs):
                        e.activation(dst[0][0:64, :], PRJ[0:64, 0:2048],
                                     AF.Copy, scale=qs)
                        return e.activation(dst[1][0:64, :],
                                            PRJ[64:128, 0:2048], AF.Copy,
                                            scale=qs)
                    cm = p.step("c", evac, waits=[(s_t, tm), (s_v, vm)],
                                inc=s_c)
                    tm2 = s_t.n
                else:
                    cm = p.step("c", lambda e: e.activation(
                        VSTG[:], PRJ[:, 0:2048], AF.Copy),
                        waits=[(s_t, tm), (s_v, vm)], inc=s_c)
                    dstv = VT if kind == "v" else VPT

                    def vtr(e, dstv=dstv):
                        last = None
                        for i in range(16):
                            for h in range(2):
                                last = e.transpose(
                                    RB[:, 0:64].bitcast(BF16)[:, 0:64],
                                    VSTG[h * 64:(h + 1) * 64,
                                         i * 128:(i + 1) * 128], ID16[:])
                        return last
                    cm2 = cm
                    for i in range(16):
                        for h in range(2):
                            tmx = p.step("t", (lambda i, h: lambda e:
                                e.transpose(RB[:, 0:64],
                                            VSTG[h * 64:(h + 1) * 64,
                                                 i * 128:(i + 1) * 128],
                                            ID32[h * 64:(h + 1) * 64,
                                                 h * 64:(h + 1) * 64]))(i, h),
                                waits=[(s_c, cm2)] if (i == 0 and h == 0)
                                else [(s_c, s_c.n)], inc=s_t)
                            p.step("c", (lambda i, h, dstv: lambda e:
                                e.activation(
                                    dstv[h][:, i * 65:i * 65 + 64],
                                    RB[:, 0:64], AF.Copy))(i, h, dstv),
                                waits=[(s_t, tmx)], inc=s_c)
                    cm = s_c.n
                prj_marks.append((s_c.n, s_v.n, s_t.n))
            prj_read_m[0] = s_t.n
            prev_cv = (s_c.n, s_v.n)
            if upto in ("proj", "projm", "proje", "projr"):
                continue
            vab = s_v.n

            def abred(e):
                e.tensor_scalar(AB[:, 3:4], AB[:, 0:1], 0.0, 0.0,
                                ALU.mult, ALU.add)
                return e.engine_nop() if False else e.tensor_reduce(
                    AB[:, 3:4], AB[:, 0:3], AX.X, ALU.max)
            p.step("v", lambda e: e.tensor_reduce(
                AB[:, 3:4], AB[:, 0:3], AX.X, ALU.max),
                waits=[(s_v, vab)], inc=s_v)
            vab2 = s_v.n
            ttr = p.step("t", lambda e: e.transpose(
                RB[0:1, 0:128], AB[:, 3:4], ID32[:]),
                waits=[(s_v, vab2), (s_c, s_c.n)], inc=s_t)
            cr1 = p.step("c", lambda e: e.activation(
                ONEROW[0:1, 0:128] if False else R_TMP[0:1, 0:128],
                RB[0:1, 0:128], AF.Copy), waits=[(s_t, ttr)], inc=s_c)
            vmx = p.step("v", lambda e: e.tensor_reduce(
                M1[:], R_TMP[0:1, 0:128], AX.X, ALU.max),
                waits=[(s_c, cr1)], inc=s_v)

            csq = p.step("c", lambda e: e.activation(
                CS[0:1, 0:1], M1[:], AF.Square, scale=0.3535533906),
                waits=[(s_v, vmx)], inc=s_c)

            def crow_v(e):
                e.tensor_single_scalar(CS[0:1, 1:2], CS[0:1, 0:1], -1.0,
                                       ALU.mult)
                e.scalar_tensor_tensor(CS[0:1, 2:3], CS[0:1, 0:1], -1.0,
                                       FLG[0:1, 0:1], ALU.mult, ALU.add)
                e.tensor_single_scalar(KA[0][64:65, :], ONEROW[:],
                                       CS[0:1, 1:2], ALU.mult)
                e.tensor_single_scalar(KA[1][64:65, :], ONEROW[:],
                                       CS[0:1, 1:2], ALU.mult)
                e.tensor_single_scalar(KP[0][64:65, :], ONEROW[:],
                                       CS[0:1, 2:3], ALU.mult)
                return e.tensor_single_scalar(KP[1][64:65, :], ONEROW[:],
                                              CS[0:1, 2:3], ALU.mult)
            cc = p.step("v", crow_v, waits=[(s_c, csq), (s_x, cl)], inc=s_v)
            if upto == "bound":
                continue

            for h in range(2):
                for qc in range(4):
                    kts = [("p", kt) for kt in range(nKT)] +                           [("o", kt) for kt in range(4 * qc + 4)]
                    av_marks = []
                    for idx, (reg, kt) in enumerate(kts):
                        sc = SC1 if idx % 2 == 0 else SC2
                        ksrc = KP[h] if reg == "p" else KA[h]
                        vsrc = VPT[h] if reg == "p" else VT[h]

                        def smm(e, sc=sc, ksrc=ksrc, kt=kt, h=h, qc=qc):
                            return e.matmul(
                                sc[:], ksrc[:, kt * 128:(kt + 1) * 128]
                                if False else ksrc[0:65,
                                                   kt * 128:(kt + 1) * 128],
                                QA[h][0:65, qc * 512:(qc + 1) * 512],
                                start=True, stop=True)
                        w = [(s_v, cc)]
                        if idx >= 2:
                            w.append((s_c, exp_marks[(g, h, qc, idx - 2)]))
                        tsm = p.step("t", smm, waits=w, inc=s_t)
                        diag = (reg == "o" and kt >= 4 * qc)
                        vm = None
                        if diag:
                            j = kt - 4 * qc
                            vm = p.step("v", (lambda sc, j: lambda e:
                                e.tensor_tensor(
                                    sc[:], sc[:],
                                    DM[:, j * 512:(j + 1) * 512], ALU.add))(
                                        sc, j), waits=[(s_t, tsm)], inc=s_v)
                        wex = [(s_t, tsm)] if vm is None else [(s_v, vm)]
                        if idx >= 2:
                            wex.append((s_t, av_marks[idx - 2]))
                        cex = p.step("c", (lambda sc, pt: lambda e:
                            e.activation(PT[pt][:], sc[:], AF.Exp))(
                                sc, idx % 2), waits=wex, inc=s_c)
                        exp_marks[(g, h, qc, idx)] = cex

                        def avmm(e, vsrc=vsrc, kt=kt, first=(idx == 0),
                                 last_=(idx == len(kts) - 1), pt=idx % 2):
                            return e.matmul(
                                PO[0:65, :], vsrc[:, kt * 65:(kt + 1) * 65],
                                PT[pt][:], start=first, stop=last_)
                        wav = [(s_c, cex)]
                        if idx == 0 and yatt_marks:
                            wav.append((s_v, yatt_marks[-1]))
                        tav = p.step("t", avmm, waits=wav, inc=s_t)
                        av_marks.append(tav)
                    vrl = p.step("v", lambda e: e.reciprocal(
                        RL32[:], PO[64:65, 0:512]),
                        waits=[(s_t, av_marks[-1])], inc=s_v)
                    crl = p.step("c", lambda e: e.activation(
                        RL16[:], RL32[:], AF.Copy), waits=[(s_v, vrl)],
                        inc=s_c)
                    trb = p.step("t", lambda e: e.matmul(
                        RB[0:64, :], ONES[0:1, 0:64], RL16[:],
                        start=True, stop=True), waits=[(s_c, crl)], inc=s_t)
                    crb = p.step("c", lambda e: e.activation(
                        RBS[:], RB[0:64, :], AF.Copy), waits=[(s_t, trb)],
                        inc=s_c)
                    ym = p.step("v", (lambda g, h, qc: lambda e:
                        e.tensor_tensor(
                            YATT[h * 64:(h + 1) * 64 + 0,
                                 g * T_ + qc * 512: g * T_ + (qc + 1) * 512]
                            if h == 0 else
                            YATT[64:128,
                                 g * T_ + qc * 512: g * T_ + (qc + 1) * 512],
                            PO[0:64, :], RBS[:], ALU.mult))(g, h, qc),
                        waits=[(s_c, crb)], inc=s_v)
                    yatt_marks.append(ym)
        vint = s_v.n
        if upto in ("ln", "proj", "bound", "attn", "projw", "projm", "proje", "projr"):
            def final1(e):
                e.wait_ge(s_v.h, s_v.n)
            p.q["g"].append(final1)
            p.run_block()
            return nc

        wo_load = [None] * KC
        wo_mm = [None] * KC
        ep_m = [None] * KC
        for m in range(KC):
            w = []
            if m >= 2:
                w.append((s_t, wo_mm[m - 2]))
            wo_load[m] = p.step("g", (lambda m: lambda e: e.dma_start(
                out=WOB[m % 2][:], in_=wo_r[m].rearrange("p k c -> p (k c)")))(
                    m), waits=w, inc=s_w, k=16)
            pos_ = ([PRJ[:, 0:512], PRJ[:, 512:1024], PRJ[:, 1024:1536],
                     PRJ[:, 1536:2048]] if m % 2 == 0 else
                    [SC1[:], SC2[:], PO[:], RB[:]])

            def womm(e, m=m, pos_=pos_):
                last = None
                buf = WOB[m % 2]
                for k in range(KC):
                    for tc in range(4):
                        last = e.matmul(
                            pos_[tc], buf[:, k * 128:(k + 1) * 128],
                            YATT[:, k * T_ + tc * 512:k * T_ + (tc + 1) * 512],
                            start=(k == 0), stop=(k == KC - 1))
                return last
            w = [(s_w, wo_load[m]), (s_v, vint)]
            if m >= 2:
                w.append((s_v, ep_m[m - 2]))
            wo_mm[m] = p.step("t", womm, waits=w, inc=s_t)
            xr = p.step("y", (lambda m: lambda e: e.dma_start(
                out=XRE[:, (m % 2) * T_:(m % 2 + 1) * T_],
                in_=xT[m * 128:(m + 1) * 128, :]))(m),
                waits=[(s_v, vint)] if m < 2 else [(s_v, ep_m[m - 2])],
                inc=s_x, k=16)

            def ep1(e, m=m, pos_=pos_):
                last = None
                for tc in range(4):
                    last = e.tensor_tensor(
                        OUT32[:, tc * 512:(tc + 1) * 512], pos_[tc],
                        XRE[:, (m % 2) * T_ + tc * 512:
                            (m % 2) * T_ + (tc + 1) * 512], ALU.add)
                return last
            ep_m[m] = p.step("v", ep1,
                             waits=[(s_t, wo_mm[m]), (s_x, xr),
                                    (s_st, 16 * m)], inc=s_v)
            p.step("y", (lambda m: lambda e: e.dma_start(
                out=yT[m * 128:(m + 1) * 128, :], in_=OUT32[:]))(m),
                waits=[(s_v, ep_m[m])], inc=s_st, k=16)

        def final(e):
            e.wait_ge(s_st.h, s_st.n)
        p.q["g"].append(final)
        p.run_block()
    return nc


def l0attn_host_inputs(W_qkv, W_o):
    wqT = np.ascontiguousarray(W_qkv.T)
    wq_r = np.ascontiguousarray(
        wqT.reshape(8, 128, 24, 128).transpose(2, 1, 0, 3))
    woT = np.ascontiguousarray(W_o.T)
    wo_r = np.ascontiguousarray(
        woT.reshape(8, 128, 8, 128).transpose(2, 1, 0, 3))
    dm = np.full((4, 128, 512), -1.0e38, np.float32)
    for j in range(4):
        for kk in range(128):
            dm[j, kk, j * 128 + kk:] = 0.0
    ident = np.eye(128, dtype=np.float32)
    dm = np.ascontiguousarray(dm.transpose(1, 0, 2).reshape(128, 2048))
    return dict(wq_r=wq_r, wo_r=wo_r, dmask=dm, ident=ident)




def emit_quant(p, E2, XIN, QOUT, W, s_v, waits):
    BM, RS = E2["BM"], E2["RS"]
    XN, GS, ZR = E2["XN"][:, 0:W], E2["GS"][:, 0:W], E2["ZR"][:, 0:W]
    SGN = E2["SGN"][:, 0:W]
    nb = W // 16

    def q(e):
        x3 = XIN.rearrange("p (w k) -> p w k", k=16)
        e.tensor_reduce(BM[:, 0:nb], x3, AX.X, ALU.max,
                        apply_absolute_value=True)
        e.tensor_scalar(RS[:, 0:nb], BM[:, 0:nb], 1.0 / 6.0, 1e-12,
                        ALU.mult, ALU.max)
        e.reciprocal(BM[:, 0:nb], RS[:, 0:nb])
        rrep = BM[:, 0:nb].unsqueeze(2).broadcast_to((128, nb, 16))
        e.tensor_tensor(XN.rearrange("p (w k) -> p w k", k=16),
                        XIN.rearrange("p (w k) -> p w k", k=16), rrep,
                        ALU.mult)
        e.tensor_single_scalar(ZR.bitcast(I32), XN.bitcast(I32), 0x7FFFFFFF,
                               ALU.bitwise_and)
        e.tensor_single_scalar(GS, ZR, 0.25, ALU.is_gt)
        e.tensor_single_scalar(SGN, ZR, 0.75, ALU.is_gt)
        e.tensor_scalar(ZR.bitcast(I32), ZR.bitcast(I32),
                        0x3F800000, 0x1FFFFF, ALU.max, ALU.add)
        e.tensor_single_scalar(ZR.bitcast(I32), ZR.bitcast(I32), -0x400000,
                               ALU.bitwise_and)
        e.scalar_tensor_tensor(ZR, ZR, -0.5, SGN, ALU.add, ALU.mult)
        e.scalar_tensor_tensor(ZR, GS, 0.5, ZR, ALU.mult, ALU.add)
        e.tensor_single_scalar(SGN.bitcast(I32), XN.bitcast(I32),
                               -0x80000000, ALU.bitwise_and)
        e.tensor_tensor(ZR.bitcast(I32), ZR.bitcast(I32), SGN.bitcast(I32),
                        ALU.bitwise_or)
        srep = RS[:, 0:nb].unsqueeze(2).broadcast_to((128, nb, 16))
        return e.tensor_tensor(
            ZR.rearrange("p (w k) -> p w k", k=16),
            ZR.rearrange("p (w k) -> p w k", k=16), srep, ALU.mult)
    return p.step("v", q, waits=waits, inc=s_v)


def build_l1ffn(T_=T, debug=False):
    TH = T_ // 2
    NI = 2816
    NIC = NI // 128
    NCH = 6
    nch_w = [512] * 5 + [256]
    nch_off = [0, 512, 1024, 1536, 2048, 2560]
    nc = bass.Bass("TRN2", target_bir_lowering=False)
    xT = nc.dram_tensor("xT", [D, T_], F32, kind="ExternalInput").ap()
    wg = nc.dram_tensor("wg", [NI, D], F32, kind="ExternalInput").ap()
    wu = nc.dram_tensor("wu", [NI, D], F32, kind="ExternalInput").ap()
    wd = nc.dram_tensor("wd", [D, NI], F32, kind="ExternalInput").ap()
    ident = nc.dram_tensor("ident", [128, 128], F32,
                           kind="ExternalInput").ap()
    yT = nc.dram_tensor("yT", [D, T_], F32, kind="ExternalOutput").ap()
    wgq_t = nc.dram_tensor("wgq_t", [128, KC, NI], BF16, kind="Internal").ap()
    wuq_t = nc.dram_tensor("wuq_t", [128, KC, NI], BF16, kind="Internal").ap()
    wdq_t = nc.dram_tensor("wdq_t", [KC, 128, NIC, 128], BF16,
                           kind="Internal").ap()
    h2q_d = nc.dram_tensor("h2q_d", [128, NIC * TH], BF16,
                           kind="Internal").ap()
    if debug:
        dbg_xqt = nc.dram_tensor("dbg_xqt", [128, KC * (T_ // 2)], F32,
                                 kind="ExternalOutput").ap()
        dbg_h2f = nc.dram_tensor("dbg_h2f", [128, 22 * (T_ // 2)], F32,
                                 kind="ExternalOutput").ap()
        dbg_wg = nc.dram_tensor("dbg_wg", [128, KC * 2816], F32,
                                kind="ExternalOutput").ap()
        dbg_wmv = nc.dram_tensor("dbg_wmv", [128, KC * 512], F32,
                                 kind="ExternalOutput").ap()
        dbg_gate = nc.dram_tensor("dbg_gate", [128, 512], F32,
                                  kind="ExternalOutput").ap()

    es = ExitStack()
    with es:
        p = Prog(nc, es)
        X = p.sbuf("X", [128, KC * TH], F32)
        XQT = p.sbuf("XQT", [128, KC * TH], BF16)
        H2F = p.sbuf("H2F", [128, NIC * TH], BF16)
        WSTG = p.sbuf("WSTG", [128, NI], F32)
        QS = p.sbuf("QS", [128, NI], BF16)
        XTM = p.sbuf("XTM", [128, 1024], F32)
        XLN = p.sbuf("XLN", [128, 1024], F32)
        BM = p.sbuf("BM", [128, NI // 16], F32)
        RS = p.sbuf("RS", [128, NI // 16], F32)
        H2F32 = H2F[:].bitcast(F32)
        XN = H2F32[:, 0:NI]
        GS = H2F32[:, NI:2 * NI]
        ZR = H2F32[:, 2 * NI:3 * NI]
        SGN = H2F32[:, 3 * NI:4 * NI]
        MUC = p.sbuf("MUC", [128, 16], F32)
        RSC = p.sbuf("RSC", [128, 16], F32)
        SG = p.sbuf("SG", [128, 512], F32)
        H2 = p.sbuf("H2", [128, 1024], F32)
        WMV = [p.sbuf(f"WMV{i}", [128, KC * 512], BF16) for i in range(2)]
        WDB = [p.sbuf(f"WDB{i}", [128, NIC * 128], BF16) for i in range(2)]
        STG = p.sbuf("STG", [128, 256], BF16)
        ID32 = p.sbuf("ID32", [128, 128], F32)
        ID16 = p.sbuf("ID16", [128, 128], BF16)
        ONES = p.sbuf("ONES", [128, 128], BF16)
        R_MU = p.sbuf("R_MU", [1, TH], F32)
        R_VAR = p.sbuf("R_VAR", [1, TH], F32)
        R_RSTD = p.sbuf("R_RSTD", [1, TH], F32)
        R_TMP = p.sbuf("R_TMP", [1, TH], F32)
        R16_MU = p.sbuf("R16_MU", [1, TH], BF16)
        R16_RSTD = p.sbuf("R16_RSTD", [1, TH], BF16)
        SQ = p.sbuf("SQ", [128, TH], BF16)
        XH1 = p.sbuf("XH1", [128, TH], BF16)
        MU_B = p.sbuf("MU_B", [128, TH], F32)
        RSTD_B = p.sbuf("RSTD_B", [128, TH], F32)
        EPS5 = p.sbuf("EPS5", [1, 1], F32)

        P1 = p.psum("P1", [128, 1024])
        P2 = p.psum("P2", [128, 1024])
        P3 = p.psum("P3", [128, 1024])
        P4 = p.psum("P4", [128, 1024])

        s_t = p.sem("s_t")
        s_v = p.sem("s_v")
        s_c = p.sem("s_c")
        s_x = p.sem("s_x")
        s_w = p.sem("s_w")
        s_st = p.sem("s_st")

        E = dict(s_t=s_t, s_v=s_v, s_c=s_c, SQ=SQ, P1=P1, P2=P2, P3=P3,
                 P4=P4, ONES=ONES, R_MU=R_MU, R_VAR=R_VAR, R_RSTD=R_RSTD,
                 R_TMP=R_TMP, R16_MU=R16_MU, R16_RSTD=R16_RSTD,
                 MU_B=MU_B, RSTD_B=RSTD_B)
        E2 = dict(BM=BM, RS=RS, XN=XN, GS=GS, ZR=ZR, SGN=SGN)

        p.step("y", lambda e: e.dma_start(out=ID32[:], in_=ident),
               inc=s_x, k=16)
        p.step("g", lambda e: e.dma_start(out=ID16[:], in_=ident),
               inc=s_w, k=16)

        def const_g(e):
            e.memset(ONES[:], 1.0)
            e.memset(H2[:], 1.0)
            return e.memset(EPS5[:], 1e-5)
        p.step("g", const_g, inc=s_w)
        const_w = s_w.n

        def quant_weight(wsrc, nrows, freew, dst_kind):
            ntile = nrows // 128
            for wt in range(ntile):
                lw = p.step("y", (lambda wt: lambda e: e.dma_start(
                    out=WSTG[:, 0:freew],
                    in_=wsrc[wt * 128:(wt + 1) * 128, :]))(wt),
                    waits=[(s_v, s_v.n)], inc=s_x, k=16)
                qm = emit_quant(p, E2, WSTG[:, 0:freew], None,
                                freew, s_v,
                                [(s_x, lw), (s_w, const_w), (s_t, s_t.n)])
                ZRv = E2["ZR"]
                for j in range(freew // 128):
                    st_sl = STG[:, (j % 2) * 128:(j % 2) * 128 + 128]
                    w = [(s_v, qm)] if j < 2 else [(s_w, s_w.n)]
                    tmark = p.step("t", (lambda j, ZRv: lambda e: e.transpose(
                        P1[:, 0:128],
                        ZRv[:, j * 128:(j + 1) * 128], ID32[:]))(j, ZRv),
                        waits=w + ([(s_c, s_c.n)] if j > 0 else []),
                        inc=s_t)
                    cm = p.step("c", (lambda st_sl: lambda e: e.activation(
                        st_sl, P1[:, 0:128], AF.Copy))(
                            st_sl), waits=[(s_t, tmark)], inc=s_c)
                    if dst_kind == "g":
                        dap = wgq_t[:, j, wt * 128:(wt + 1) * 128]
                    elif dst_kind == "u":
                        dap = wuq_t[:, j, wt * 128:(wt + 1) * 128]
                    else:
                        dap = wdq_t[wt, :, j, :]
                    p.step("g", (lambda dap, st_sl: lambda e: e.dma_start(
                        out=dap, in_=st_sl))(dap, st_sl),
                        waits=[(s_c, cm)], inc=s_w, k=16)
        quant_weight(wg, NI, 1024, "g")
        quant_weight(wu, NI, 1024, "u")
        quant_weight(wd, D, NI, "d")
        w_spill = s_w.n

        for half in range(2):
            t0 = half * TH
            xloads = []
            for k in range(KC):
                m = p.step("y", (lambda k, t0: lambda e: e.dma_start(
                    out=X[:, k * TH:(k + 1) * TH],
                    in_=xT[k * 128:(k + 1) * 128, t0:t0 + TH]))(k, t0),
                    waits=[(s_st, 16 * KC * half), (s_v, s_v.n)],
                    inc=s_x, k=16)
                xloads.append(m)
            X_chunks = [X[:, k * TH:(k + 1) * TH] for k in range(KC)]
            xh_chunks = [XH1[:]] * KC

            def wait_first(k, xloads=xloads):
                w = [(s_x, xloads[k])]
                if k == 0:
                    w.append((s_w, const_w))
                return w
            cmu = emit_ln_stats(p, E, X_chunks, xh_chunks, TH,
                                EPS5[0:1, 0:1], wait_first)
            for tt in range(TH // 128):
                tm1 = p.step("t", (lambda tt: lambda e: e.transpose(
                    P2[:, 0:128], _row(MU_B, tt), ID32[0:1, 0:128]))(tt),
                    waits=[(s_c, cmu), (s_c, s_c.n)], inc=s_t)
                cm1 = p.step("c", (lambda tt: lambda e: e.activation(
                    MUC[:, tt:tt + 1], P2[:, 0:1], AF.Copy))(tt),
                    waits=[(s_t, tm1)], inc=s_c)
                tm2 = p.step("t", (lambda tt: lambda e: e.transpose(
                    P2[:, 0:128], _row(RSTD_B, tt), ID32[0:1, 0:128]))(tt),
                    waits=[(s_c, cm1)], inc=s_t)
                cm2 = p.step("c", (lambda tt: lambda e: e.activation(
                    RSC[:, tt:tt + 1], P2[:, 0:1], AF.Copy))(tt),
                    waits=[(s_t, tm2)], inc=s_c)
            cmurs = s_c.n

            for tt in range(TH // 128):
                evs = []
                for k in range(KC):
                    w = ([(s_c, cmurs)] if (tt == 0 and k == 0)
                         else [(s_v, s_v.n), (s_c, s_c.n)])
                    tmx = p.step("t", (lambda k, tt: lambda e:
                        e.transpose(P3[:, 0:128],
                                    X[:, k * TH + tt * 128:
                                      k * TH + (tt + 1) * 128],
                                    ID32[:]))(k, tt), waits=w, inc=s_t)
                    cmx = p.step("c", (lambda k: lambda e: e.activation(
                        XTM[:, k * 128:(k + 1) * 128], P3[:, 0:128],
                        AF.Copy))(k), waits=[(s_t, tmx)], inc=s_c)
                    evs.append(cmx)

                def applyq(e, tt=tt):
                    return e.tensor_scalar(
                        XLN[:], XTM[:], MUC[:, tt:tt + 1], RSC[:, tt:tt + 1],
                        ALU.subtract, ALU.mult)
                va = p.step("v", applyq, waits=[(s_c, evs[-1])], inc=s_v)
                qm = emit_quant(p, E2, XLN[:], None, 1024, s_v,
                                [(s_v, va), (s_t, s_t.n)])
                ZRv = E2["ZR"]
                for k in range(KC):
                    w = [(s_v, qm)] if k == 0 else [(s_c, s_c.n)]
                    tmy = p.step("t", (lambda k, ZRv: lambda e: e.transpose(
                        P3[:, 0:128],
                        ZRv[:, k * 128:(k + 1) * 128], ID32[:]))(k, ZRv),
                        waits=w, inc=s_t)
                    p.step("c", (lambda k, tt: lambda e: e.activation(
                        XQT[:, k * TH + tt * 128:k * TH + (tt + 1) * 128],
                        P3[:, 0:128], AF.Copy))(k, tt),
                        waits=[(s_t, tmy)], inc=s_c)
            cxq = s_c.n
            vxq = s_v.n
            if debug and half == 0:
                p.step("g", lambda e: e.dma_start(out=dbg_xqt, in_=XQT[:]),
                       waits=[(s_c, cxq)], inc=s_w, k=16)
                p.step("g", lambda e: e.dma_start(
                    out=dbg_wg, in_=wgq_t[:, :, :]), waits=[(s_w, w_spill)],
                    inc=s_w, k=16)

            spill_marks = []
            for nch in range(NCH):
                wnn = nch_w[nch]
                lg_ = p.step("g", (lambda nch, wnn: lambda e: e.dma_start(
                    out=WMV[0][:, 0:KC * wnn],
                    in_=wgq_t[:, :, nch_off[nch]:nch_off[nch] + wnn]))(
                        nch, wnn),
                    waits=[(s_t, s_t.n), (s_w, w_spill)], inc=s_w, k=16)
                lu_ = p.step("g", (lambda nch, wnn: lambda e: e.dma_start(
                    out=WMV[1][:, 0:KC * wnn],
                    in_=wuq_t[:, :, nch_off[nch]:nch_off[nch] + wnn]))(
                        nch, wnn),
                    inc=s_w, k=16)
                if debug and half == 0 and nch == 0:
                    p.step("g", lambda e: e.dma_start(
                        out=dbg_wmv, in_=WMV[0][:]),
                        waits=[(s_w, lg_)], inc=s_w, k=16)
                for mt in range(TH // 128):
                    def gmm(e, mt=mt, wnn=wnn):
                        last = None
                        for k in range(KC):
                            last = e.matmul(
                                P1[:, 0:wnn],
                                XQT[:, k * TH + mt * 128:
                                    k * TH + (mt + 1) * 128],
                                WMV[0][:, k * wnn:(k + 1) * wnn],
                                start=(k == 0), stop=(k == KC - 1))
                        return last
                    w = [(s_w, lg_), (s_c, cxq), (s_v, vxq)]
                    if spill_marks:
                        w.append((s_w, spill_marks[-1]))
                    tg = p.step("t", gmm, waits=w, inc=s_t)

                    def umm(e, mt=mt, wnn=wnn):
                        last = None
                        for k in range(KC):
                            last = e.matmul(
                                P2[:, 0:wnn],
                                XQT[:, k * TH + mt * 128:
                                    k * TH + (mt + 1) * 128],
                                WMV[1][:, k * wnn:(k + 1) * wnn],
                                start=(k == 0), stop=(k == KC - 1))
                        return last
                    tu = p.step("t", umm, waits=[(s_w, lu_)], inc=s_t)
                    cs = p.step("c", (lambda wnn: lambda e: e.activation(
                        SG[:, 0:wnn], P1[:, 0:wnn], AF.Silu))(wnn),
                        waits=[(s_t, tg)], inc=s_c)
                    if debug and half == 0 and nch == 0 and mt == 0:
                        cdg = p.step("c", lambda e: e.activation(
                            XTM[:, 0:512], P1[:, 0:512], AF.Copy),
                            waits=[(s_t, tg)], inc=s_c)
                        p.step("g", lambda e: e.dma_start(
                            out=dbg_gate, in_=XTM[:, 0:512]),
                            waits=[(s_c, cdg)], inc=s_w, k=16)
                    vh2 = p.step("v", (lambda wnn: lambda e: e.tensor_tensor(
                        H2[:, 0:wnn], SG[:, 0:wnn], P2[:, 0:wnn],
                        ALU.mult))(wnn), waits=[(s_c, cs), (s_t, tu)],
                        inc=s_v)
                    qm2 = emit_quant(p, E2, H2[:, 0:1024], None,
                                     1024, s_v, [(s_v, vh2), (s_t, s_t.n)])
                    ZRv = E2["ZR"]
                    for j in range(wnn // 128):
                        st_sl = STG[:, (j % 2) * 128:(j % 2) * 128 + 128]
                        w2 = [(s_v, qm2)] if j < 2 else [(s_w, s_w.n)]
                        tmz = p.step("t", (lambda j, ZRv: lambda e:
                            e.transpose(P3[:, 0:128],
                                        ZRv[:, j * 128:(j + 1) * 128],
                                        ID32[:]))(j, ZRv),
                            waits=w2 + ([(s_c, s_c.n)] if j > 0 else []),
                            inc=s_t)
                        cmz = p.step("c", (lambda st_sl: lambda e:
                            e.activation(st_sl, P3[:, 0:128],
                                         AF.Copy))(st_sl),
                            waits=[(s_t, tmz)], inc=s_c)
                        kti = (nch_off[nch] + j * 128) // 128
                        hm = p.step("g", (lambda kti, mt, st_sl: lambda e:
                            e.dma_start(
                                out=h2q_d[:, kti * TH + mt * 128:
                                          kti * TH + (mt + 1) * 128],
                                in_=st_sl))(kti, mt, st_sl),
                            waits=[(s_c, cmz)], inc=s_w, k=16)
                        spill_marks.append(hm)
            h2_done = s_w.n

            h2f_m = p.step("y", lambda e: e.dma_start(out=H2F[:], in_=h2q_d),
                           waits=[(s_w, h2_done)], inc=s_x, k=16)
            if debug and half == 0:
                p.step("g", lambda e: e.dma_start(out=dbg_h2f, in_=H2F[:]),
                       waits=[(s_x, h2f_m)], inc=s_w, k=16)
            ep_marks = []
            for m in range(KC):
                w = [(s_w, w_spill)]
                if m >= 2:
                    w.append((s_t, ep_marks[m - 2][1]))
                lw = p.step("g", (lambda m: lambda e: e.dma_start(
                    out=WDB[m % 2][:],
                    in_=wdq_t[m].rearrange("p k c -> p (k c)")))(m),
                    waits=w, inc=s_w, k=16)
                po = P3 if m % 2 == 0 else P4

                def dmm(e, m=m, po=po):
                    last = None
                    buf = WDB[m % 2]
                    for k in range(NIC):
                        for tc in range(TH // 512):
                            last = e.matmul(
                                po[:, tc * 512:(tc + 1) * 512],
                                buf[:, k * 128:(k + 1) * 128],
                                H2F[:, k * TH + tc * 512:
                                    k * TH + (tc + 1) * 512],
                                start=(k == 0), stop=(k == NIC - 1))
                    return last
                w = [(s_w, lw), (s_x, h2f_m)]
                if m >= 2:
                    w.append((s_v, ep_marks[m - 2][0]))
                tdm = p.step("t", dmm, waits=w, inc=s_t)

                def ep(e, m=m, po=po):
                    sl = slice(m * TH, (m + 1) * TH)
                    return e.scalar_tensor_tensor(
                        X[:, sl], po[:, 0:TH], 0.5, X[:, sl],
                        ALU.mult, ALU.add)
                em = p.step("v", ep, waits=[(s_t, tdm)], inc=s_v)
                ep_marks.append((em, tdm))
                p.step("y", (lambda m, t0: lambda e: e.dma_start(
                    out=yT[m * 128:(m + 1) * 128, t0:t0 + TH],
                    in_=X[:, m * TH:(m + 1) * TH]))(m, t0),
                    waits=[(s_v, em)], inc=s_st, k=16)

        def final(e):
            e.wait_ge(s_st.h, s_st.n)
        p.q["g"].append(final)
        p.run_block()
    return nc


def _row(t, tt):
    return t[0:1, tt * 128:(tt + 1) * 128]


def _h2ap(h):
    return h.rearrange("k p t -> p (k t)")




def build_l1attn(T_=T):
    TH = T_ // 2
    nKT = T_ // 128
    nc = bass.Bass("TRN2", target_bir_lowering=False)
    xT = nc.dram_tensor("xT", [D, T_], F32, kind="ExternalInput").ap()
    xpT = nc.dram_tensor("xpT", [D, T_], F32, kind="ExternalInput").ap()
    wq_r = nc.dram_tensor("wq_r", [24, 128, KC, 128], F32,
                          kind="ExternalInput").ap()
    wo_r = nc.dram_tensor("wo_r", [KC, 128, KC, 128], F32,
                          kind="ExternalInput").ap()
    bqkv_c = nc.dram_tensor("bqkv_c", [128, 24], F32,
                            kind="ExternalInput").ap()
    gq_c = nc.dram_tensor("gq_c", [128, KC], F32, kind="ExternalInput").ap()
    bq_c = nc.dram_tensor("bq_c", [128, KC], F32, kind="ExternalInput").ap()
    gk_c = nc.dram_tensor("gk_c", [128, KC], F32, kind="ExternalInput").ap()
    bk_c = nc.dram_tensor("bk_c", [128, KC], F32, kind="ExternalInput").ap()
    bo_c = nc.dram_tensor("bo_c", [128, KC], F32, kind="ExternalInput").ap()
    utri = nc.dram_tensor("utri", [128, 128], F32, kind="ExternalInput").ap()
    flag = nc.dram_tensor("flag", [1, 1], F32, kind="ExternalInput").ap()
    ident = nc.dram_tensor("ident", [128, 128], F32,
                           kind="ExternalInput").ap()
    yT = nc.dram_tensor("yT", [D, T_], F32, kind="ExternalOutput").ap()

    es = ExitStack()
    with es:
        p = Prog(nc, es)
        YATT = p.sbuf("YATT", [128, KC * T_], BF16)
        X = YATT[:].bitcast(F32)
        XLH = p.sbuf("XLH", [128, KC * T_], BF16)
        XPH = p.sbuf("XPH", [128, KC * T_], BF16)
        XRE = XPH[:].bitcast(F32)[:, 0:2 * T_]
        OUT32 = XPH[:].bitcast(F32)[:, 2 * T_:3 * T_]
        MUBC = p.sbuf("MUBC", [128, T_], F32)
        WBC = p.sbuf("WBC", [128, T_], F32)
        KERQ = p.sbuf("KERQ", [128, T_], BF16)
        KERK = p.sbuf("KERK", [128, T_], BF16)
        KERKF = p.sbuf("KERKF", [128, T_], F32)
        VSTG = p.sbuf("VSTG", [128, T_], F32)
        KTOK = [p.sbuf(f"KTOK{h}", [128, nKT * 64], BF16) for h in range(2)]
        KPTOK = [p.sbuf(f"KPTOK{h}", [128, nKT * 64], BF16) for h in range(2)]
        VT = [p.sbuf(f"VT{h}", [128, nKT * 65], BF16) for h in range(2)]
        VPT = [p.sbuf(f"VPT{h}", [128, nKT * 65], BF16) for h in range(2)]
        KVB = p.sbuf("KVB", [128, 65], BF16)
        YU = [p.sbuf(f"YU{h}", [65, T_], BF16) for h in range(2)]
        PTB = p.sbuf("PTB", [128, 128], BF16)
        RBS = p.sbuf("RBS", [64, 512], BF16)
        UTRI = p.sbuf("UTRI", [128, 128], F32)
        FLG16 = p.sbuf("FLG16", [1, 1], BF16)
        FLGC = p.sbuf("FLGC", [128, 1], F32)
        WQB = [p.sbuf(f"WQB{i}", [128, 3 * KC * 128], BF16) for i in range(2)]
        BQKV = p.sbuf("BQKV", [128, 24], F32)
        BQ16 = p.sbuf("BQ16", [128, 24], BF16)
        BSR = p.sbuf("BSR", [1, 24], F32)
        BSQ = p.sbuf("BSQ", [1, 1], F32)
        BSK = p.sbuf("BSK", [1, 1], F32)
        GQ = p.sbuf("GQ", [128, KC], F32)
        BQ = p.sbuf("BQ", [128, KC], F32)
        GK = p.sbuf("GK", [128, KC], F32)
        BK = p.sbuf("BK", [128, KC], F32)
        BO = p.sbuf("BO", [128, KC], F32)
        ID32 = p.sbuf("ID32", [128, 128], F32)
        ONES = p.sbuf("ONES", [128, 128], BF16)
        ONES32 = p.sbuf("ONES32", [1, 128], F32)
        EPS5 = p.sbuf("EPS5", [1, 1], F32)
        EPS6 = p.sbuf("EPS6", [1, 1], F32)
        MQ = [p.sbuf(f"MQ{i}", [1, T_], BF16) for i in range(2)]
        MK = [p.sbuf(f"MK{i}", [1, T_], BF16) for i in range(2)]
        MP = [p.sbuf(f"MP{i}", [1, T_], BF16) for i in range(2)]
        SQ = KERK[:, 0:TH]
        XH1 = KERK[:, TH:2 * TH]
        MU_B = KERQ[:].bitcast(F32)[:, 0:TH]
        RSTD_B = VSTG[:, 0:TH]
        R_MU = MUBC[0:1, 0:TH]
        R_VAR = MUBC[0:1, TH:2 * TH]
        R_RSTD = WBC[0:1, 0:TH]
        R_TMP = WBC[0:1, TH:2 * TH]
        R16_MU = KTOK[0][0:1, 0:TH]
        R16_RSTD = KTOK[1][0:1, 0:TH]
        XL1 = KERKF[:, 0:TH]
        QSC = MUBC[:].bitcast(BF16)[:, 0:T_]
        SACC = WBC[0:1, 0:T_]
        RL32 = KERKF[0:1, 0:T_]
        TMPR = VSTG[0:1, 0:T_]
        WOB = [VSTG[:].bitcast(BF16)[:, i * KC * 128:(i + 1) * KC * 128]
               for i in range(2)]

        PRJ = p.psum("PRJ", [128, 2048])
        SC1 = p.psum("SC1", [128, 512])
        SC2 = p.psum("SC2", [128, 512])
        PO = p.psum("PO", [128, 512])
        RB = p.psum("RB", [128, 512])

        s_t = p.sem("s_t")
        s_v = p.sem("s_v")
        s_c = p.sem("s_c")
        s_x = p.sem("s_x")
        s_w = p.sem("s_w")
        s_st = p.sem("s_st")

        E = dict(s_t=s_t, s_v=s_v, s_c=s_c, SQ=SQ,
                 P1=PRJ[:, 0:1024], P2=PRJ[:, 1024:2048],
                 P3=PRJ[:, 0:1024], P4=PRJ[:, 1024:2048],
                 ONES=ONES, R_MU=R_MU, R_VAR=R_VAR, R_RSTD=R_RSTD,
                 R_TMP=R_TMP, R16_MU=R16_MU, R16_RSTD=R16_RSTD,
                 MU_B=MU_B, RSTD_B=RSTD_B)

        p.step("y", lambda e: e.dma_start(out=UTRI[:], in_=utri), inc=s_x, k=16)
        p.step("y", lambda e: e.dma_start(out=ID32[:], in_=ident), inc=s_x, k=16)
        p.step("y", lambda e: e.dma_start(out=BQKV[:], in_=bqkv_c), inc=s_x, k=16)
        p.step("y", lambda e: e.dma_start(out=GQ[:], in_=gq_c), inc=s_x, k=16)
        p.step("y", lambda e: e.dma_start(out=BQ[:], in_=bq_c), inc=s_x, k=16)
        p.step("y", lambda e: e.dma_start(out=GK[:], in_=gk_c), inc=s_x, k=16)
        p.step("y", lambda e: e.dma_start(out=BK[:], in_=bk_c), inc=s_x, k=16)
        p.step("y", lambda e: e.dma_start(out=BO[:], in_=bo_c), inc=s_x, k=16)
        p.step("g", lambda e: e.dma_start(out=FLG16[:], in_=flag), inc=s_w, k=16)

        def const_g(e):
            e.memset(ONES[:], 1.0)
            e.memset(ONES32[:], 1.0)
            e.memset(EPS5[:], 1e-5)
            for h in range(2):
                e.memset(VT[h][:], 1.0)
                e.memset(VPT[h][:], 1.0)
            return e.memset(EPS6[:], 1e-6)
        p.step("g", const_g, inc=s_w)
        const_w = s_w.n
        cl = s_x.n
        tf = p.step("t", lambda e: e.matmul(RB[:, 0:1], ONES[0:1, :],
                                            FLG16[0:1, 0:1], start=True,
                                            stop=True),
                    waits=[(s_w, const_w)], inc=s_t)
        cf = p.step("c", lambda e: e.activation(FLGC[:], RB[:, 0:1], AF.Copy),
                    waits=[(s_t, tf)], inc=s_c)
        cb16 = p.step("c", lambda e: e.activation(BQ16[:], BQKV[:], AF.Copy),
                      waits=[(s_x, cl)], inc=s_c)
        tbs = p.step("t", lambda e: e.matmul(RB[0:1, 0:24], ONES[:, 0:1],
                                             BQ16[:, 0:24], start=True,
                                             stop=True),
                     waits=[(s_c, cb16), (s_c, cf)], inc=s_t)
        cbs = p.step("c", lambda e: e.activation(BSR[0:1, 0:24],
                                                 RB[0:1, 0:24], AF.Copy),
                     waits=[(s_t, tbs)], inc=s_c)
        vbs = p.step("v", lambda e: (
            e.tensor_reduce(BSQ[0:1, 0:1], BSR[0:1, 0:8], AX.X, ALU.add),
            e.tensor_reduce(BSK[0:1, 0:1], BSR[0:1, 8:16], AX.X,
                            ALU.add))[-1], waits=[(s_c, cbs)], inc=s_v)

        def do_ln(src_ap, dst):
            for th in range(2):
                t0 = th * TH
                vcur, ccur = s_v.n, s_c.n
                xloads = []
                for k in range(KC):
                    m = p.step("y", (lambda k, t0: lambda e: e.dma_start(
                        out=X[:, k * TH:(k + 1) * TH],
                        in_=src_ap[k * 128:(k + 1) * 128, t0:t0 + TH]))(k, t0),
                        waits=[(s_v, vcur), (s_c, ccur)], inc=s_x, k=16)
                    xloads.append(m)

                def wait_first(k, xloads=xloads):
                    w = [(s_x, xloads[k])]
                    if k == 0:
                        w.append((s_w, const_w))
                    return w
                X_chunks = [X[:, k * TH:(k + 1) * TH] for k in range(KC)]
                xh_chunks = [XH1[:]] * KC
                cmu = emit_ln_stats(p, E, X_chunks, xh_chunks, TH,
                                    EPS5[0:1, 0:1], wait_first)
                for k in range(KC):
                    def applya(e, k=k):
                        e.tensor_tensor(XL1[:], X[:, k * TH:(k + 1) * TH],
                                        MU_B[:], ALU.subtract)
                        return e.tensor_tensor(XL1[:], XL1[:], RSTD_B[:],
                                               ALU.mult)
                    w = [(s_c, cmu)] if k == 0 else [(s_c, s_c.n)]
                    p.step("v", applya, waits=w, inc=s_v)
                    va = s_v.n
                    p.step("c", (lambda k, t0: lambda e: e.activation(
                        dst[:, k * T_ + t0:k * T_ + t0 + TH], XL1[:],
                        AF.Copy))(k, t0), waits=[(s_v, va)], inc=s_c)
        do_ln(xT, XLH)
        do_ln(xpT, XPH)
        cln = s_c.n

        def qkv_weight_load(g, wb, extra_w):
            wl = p.step("g", (lambda g, wb: lambda e: e.dma_start(
                out=wb[:, 0:KC * 128],
                in_=wq_r[g, :, :, :].rearrange("p k c -> p (k c)")))(g, wb),
                waits=extra_w, inc=s_w, k=16)
            p.step("g", (lambda g, wb: lambda e: e.dma_start(
                out=wb[:, KC * 128:2 * KC * 128],
                in_=wq_r[8 + g, :, :, :].rearrange("p k c -> p (k c)")))(
                    g, wb), inc=s_w, k=16)
            return p.step("g", (lambda g, wb: lambda e: e.dma_start(
                out=wb[:, 2 * KC * 128:3 * KC * 128],
                in_=wq_r[16 + g, :, :, :].rearrange("p k c -> p (k c)")))(
                    g, wb), inc=s_w, k=16)

        def stat_pass(which, src, dst_rows):
            base = 0 if which == 0 else 8
            pss = [SC1, SC2, PO, RB]
            for mc in range(KC):
                wb = WQB[mc % 2]
                wl = p.step("g", (lambda mc, wb: lambda e: e.dma_start(
                    out=wb[:, 0:KC * 128],
                    in_=wq_r[base + mc, :, :, :]
                    .rearrange("p k c -> p (k c)")))(mc, wb),
                    waits=[(s_t, s_t.n)], inc=s_w, k=16)

                def prj(e, wb=wb, src=src):
                    last = None
                    for k in range(KC):
                        for tc in range(4):
                            last = e.matmul(
                                PRJ[:, tc * 512:(tc + 1) * 512],
                                wb[:, k * 128:(k + 1) * 128],
                                src[:, k * T_ + tc * 512:
                                    k * T_ + (tc + 1) * 512],
                                start=(k == 0), stop=(k == KC - 1))
                    return last
                w = [(s_w, wl), (s_c, cln), (s_c, s_c.n), (s_v, s_v.n)]
                tm = p.step("t", prj, waits=w, inc=s_t)
                cm1 = p.step("c", lambda e: e.activation(
                    QSC[:], PRJ[:, 0:2048], AF.Copy),
                    waits=[(s_t, tm)], inc=s_c)
                cm2 = p.step("c", (lambda mc, base: lambda e: e.activation(
                    KERQ[:], PRJ[:, 0:2048], AF.Square,
                    bias=BQKV[:, base + mc:base + mc + 1]))(mc, base),
                    waits=[(s_c, cm1)], inc=s_c)

                def summs(e):
                    last = None
                    for tc in range(4):
                        sl = slice(tc * 512, (tc + 1) * 512)
                        last = e.matmul(pss[tc][0:1, :], ONES[:, 0:1],
                                        QSC[:, sl], start=True, stop=True)
                    return last
                ts1 = p.step("t", summs, waits=[(s_c, cm2)], inc=s_t)

                def acc_sum(e, mc=mc):
                    last = None
                    for tc in range(4):
                        sl = slice(tc * 512, (tc + 1) * 512)
                        if mc == 0:
                            last = e.tensor_single_scalar(
                                SACC[0:1, sl], pss[tc][0:1, :],
                                1.0, ALU.mult)
                        else:
                            last = e.tensor_tensor(
                                SACC[0:1, sl], SACC[0:1, sl],
                                pss[tc][0:1, :], ALU.add)
                    return last
                va1 = p.step("v", acc_sum, waits=[(s_t, ts1)], inc=s_v)

                def sqmms(e):
                    last = None
                    for tc in range(4):
                        sl = slice(tc * 512, (tc + 1) * 512)
                        last = e.matmul(pss[tc][0:1, :], ONES[:, 0:1],
                                        KERQ[:, sl], start=True, stop=True)
                    return last
                ts2 = p.step("t", sqmms, waits=[(s_v, va1)], inc=s_t)

                def acc_sq(e, mc=mc):
                    last = None
                    for tc in range(4):
                        sl = slice(tc * 512, (tc + 1) * 512)
                        if mc == 0:
                            last = e.tensor_single_scalar(
                                RL32[0:1, sl], pss[tc][0:1, :], 1.0,
                                ALU.mult)
                        else:
                            last = e.tensor_tensor(
                                RL32[0:1, sl], RL32[0:1, sl],
                                pss[tc][0:1, :], ALU.add)
                    return last
                p.step("v", acc_sq, waits=[(s_t, ts2)], inc=s_v)
            bs = BSQ if which == 0 else BSK

            def rows5(e, bs=bs):
                e.tensor_single_scalar(SACC[0:1, :], SACC[0:1, :],
                                       bs[0:1, 0:1], ALU.add)
                e.tensor_single_scalar(SACC[0:1, :], SACC[0:1, :], 1.0 / D,
                                       ALU.mult)
                e.tensor_tensor(TMPR[0:1, :], SACC[0:1, :], SACC[0:1, :],
                                ALU.mult)
                e.scalar_tensor_tensor(RL32[0:1, :], RL32[0:1, :], 1.0 / D,
                                       TMPR[0:1, :], ALU.mult, ALU.subtract)
                return e.tensor_single_scalar(RL32[0:1, :], RL32[0:1, :],
                                              1024.0 / 1023.0, ALU.mult)
            vr = p.step("v", rows5, waits=[(s_v, s_v.n)], inc=s_v)
            cq = p.step("c", lambda e: e.activation(
                RL32[0:1, :], RL32[0:1, :], AF.Sqrt), waits=[(s_v, vr)],
                inc=s_c)

            def fin_w(e):
                e.tensor_single_scalar(RL32[0:1, :], RL32[0:1, :], 1e-6,
                                       ALU.add)
                return e.reciprocal(RL32[0:1, :], RL32[0:1, :])
            vf = p.step("v", fin_w, waits=[(s_c, cq)], inc=s_v)

            def cast16(e, dst_rows=dst_rows):
                e.activation(dst_rows[0][0:1, :], SACC[0:1, :], AF.Copy)
                return e.activation(dst_rows[1][0:1, :], RL32[0:1, :],
                                    AF.Copy)
            return p.step("c", cast16, waits=[(s_v, vf)], inc=s_c)

        sq_m = stat_pass(0, XLH, MQ)
        sk_m = stat_pass(1, XLH, MK)
        sp_m = stat_pass(1, XPH, MP)
        stats_done = (s_c.n, s_v.n, s_t.n)

        yatt_marks = []
        prev_cv = (s_c.n, s_v.n)
        for g in range(KC):
            wb = WQB[g % 2]
            w = [(s_c, sp_m)]
            if g >= 2:
                w.append((s_t, prev_prj[0]))
            wq_done = qkv_weight_load(g, wb, w)
            sets = [("q", 0, XLH, MQ, GQ, BQ), ("k", KC * 128, XLH, MK, GK, BK),
                    ("v", 2 * KC * 128, XLH, None, None, None),
                    ("kp", KC * 128, XPH, MP, GK, BK),
                    ("vp", 2 * KC * 128, XPH, None, None, None)]
            prj_marks = []
            for si, (kind, woff, src, rows, gam, bet) in enumerate(sets):
                def prjmm(e, woff=woff, src=src, wb=wb):
                    last = None
                    for k in range(KC):
                        for tc in range(4):
                            last = e.matmul(
                                PRJ[:, tc * 512:(tc + 1) * 512],
                                wb[:, woff + k * 128:woff + (k + 1) * 128],
                                src[:, k * T_ + tc * 512:
                                    k * T_ + (tc + 1) * 512],
                                start=(k == 0), stop=(k == KC - 1))
                    return last
                w = [(s_w, wq_done), (s_c, s_c.n), (s_v, s_v.n)]
                tm = p.step("t", prjmm, waits=w, inc=s_t)
                bcol = {"q": g, "k": 8 + g, "v": 16 + g,
                        "kp": 8 + g, "vp": 16 + g}[kind]
                if kind in ("v", "vp"):
                    vm = p.step("v", (lambda bcol: lambda e:
                        e.tensor_single_scalar(
                            VSTG[:], PRJ[:, 0:2048],
                            BQKV[:, bcol:bcol + 1], ALU.add))(bcol),
                        waits=[(s_t, tm), (s_c, s_c.n)], inc=s_v)
                    dstv = VT if kind == "v" else VPT
                    for i in range(nKT):
                        for h in range(2):
                            tmx = p.step("t", (lambda i, h: lambda e:
                                e.transpose(RB[:, 0:64],
                                            VSTG[h * 64:(h + 1) * 64,
                                                 i * 128:(i + 1) * 128],
                                            ID32[h * 64:(h + 1) * 64,
                                                 h * 64:(h + 1) * 64]))(i, h),
                                waits=[(s_v, vm)] if (i == 0 and h == 0)
                                else [(s_c, s_c.n)], inc=s_t)
                            p.step("c", (lambda i, h, dstv: lambda e:
                                e.activation(dstv[h][:, i * 65:i * 65 + 64],
                                             RB[:, 0:64], AF.Copy))(
                                    i, h, dstv), waits=[(s_t, tmx)], inc=s_c)
                else:
                    def bc_mu(e, rows=rows):
                        last = None
                        pss = [SC1, SC2, PO, RB]
                        for tc in range(4):
                            sl = slice(tc * 512, (tc + 1) * 512)
                            last = e.matmul(pss[tc][:], ONES[0:1, :],
                                            rows[0][0:1, sl], start=True,
                                            stop=True)
                        return last
                    tb1 = p.step("t", bc_mu,
                                 waits=[(s_c, s_c.n), (s_v, s_v.n)], inc=s_t)

                    def ev_mu(e):
                        pss = [SC1, SC2, PO, RB]
                        last = None
                        for tc in range(4):
                            sl = slice(tc * 512, (tc + 1) * 512)
                            last = e.activation(MUBC[:, sl], pss[tc][:],
                                                AF.Copy)
                        return last
                    cb1 = p.step("c", ev_mu, waits=[(s_t, tb1)], inc=s_c)

                    def bc_w(e, rows=rows):
                        last = None
                        pss = [SC1, SC2, PO, RB]
                        for tc in range(4):
                            sl = slice(tc * 512, (tc + 1) * 512)
                            last = e.matmul(pss[tc][:], ONES[0:1, :],
                                            rows[1][0:1, sl], start=True,
                                            stop=True)
                        return last
                    tb2 = p.step("t", bc_w, waits=[(s_c, cb1)], inc=s_t)

                    def ev_w(e):
                        pss = [SC1, SC2, PO, RB]
                        last = None
                        for tc in range(4):
                            sl = slice(tc * 512, (tc + 1) * 512)
                            last = e.activation(WBC[:, sl], pss[tc][:],
                                                AF.Copy)
                        return last
                    cb2 = p.step("c", ev_w, waits=[(s_t, tb2)], inc=s_c)

                    def appk(e, bcol=bcol):
                        e.tensor_single_scalar(KERKF[:], PRJ[:, 0:2048],
                                               BQKV[:, bcol:bcol + 1],
                                               ALU.add)
                        e.tensor_tensor(KERKF[:], KERKF[:], MUBC[:],
                                        ALU.subtract)
                        return e.tensor_tensor(KERKF[:], KERKF[:], WBC[:],
                                               ALU.mult)
                    vap = p.step("v", appk, waits=[(s_c, cb2), (s_t, tm)],
                                 inc=s_v)

                    def sq2(e, gam=gam, bet=bet, kind=kind, g=g):
                        if kind == "q":
                            return e.activation(KERQ[:], KERKF[:], AF.Square,
                                                bias=_gb(bet, g),
                                                scale=_gb(gam, g))
                        if kind == "k":
                            e.activation(KERK[:], KERKF[:], AF.Square,
                                         bias=_gb(bet, g), scale=_gb(gam, g))
                        return e.activation(VSTG[:], KERKF[:], AF.Square,
                                            bias=_gb(bet, g),
                                            scale=_gb(gam, g))
                    csq = p.step("c", sq2, waits=[(s_v, vap)], inc=s_c)
                    if kind == "kp":
                        def gate(e):
                            return e.tensor_single_scalar(
                                VSTG[:], VSTG[:], FLGC[:, 0:1], ALU.mult)
                        csq_v = p.step("v", gate,
                                       waits=[(s_c, csq), (s_c, cf)],
                                       inc=s_v)
                    if kind in ("k", "kp"):
                        dstt = KTOK if kind == "k" else KPTOK
                        for i in range(nKT):
                            for h in range(2):
                                wtt = [(s_c, s_c.n)]
                                if i == 0 and h == 0:
                                    wtt = ([(s_v, csq_v)] if kind == "kp"
                                           else [(s_c, csq)])
                                tmx = p.step("t", (lambda i, h: lambda e:
                                    e.transpose(
                                        RB[:, 0:64],
                                        VSTG[h * 64:(h + 1) * 64,
                                             i * 128:(i + 1) * 128],
                                        ID32[h * 64:(h + 1) * 64,
                                             h * 64:(h + 1) * 64]))(i, h),
                                    waits=wtt, inc=s_t)
                                p.step("c", (lambda i, h, dstt: lambda e:
                                    e.activation(
                                        dstt[h][:, i * 64:(i + 1) * 64],
                                        RB[:, 0:64], AF.Copy))(i, h, dstt),
                                    waits=[(s_t, tmx)], inc=s_c)
                prj_marks.append((s_c.n, s_v.n))
            prev_prj = (s_t.n,)
            prev_cv = (s_c.n, s_v.n)

            for h in range(2):
                def kvinit(e, h=h):
                    last = None
                    for i in range(nKT):
                        last = e.matmul(
                            SC1[0:64, 0:65],
                            KPTOK[h][:, i * 64:(i + 1) * 64],
                            VPT[h][:, i * 65:(i + 1) * 65],
                            start=(i == 0), stop=(i == nKT - 1))
                    return last
                tki = p.step("t", kvinit, waits=[(s_c, s_c.n), (s_v, s_v.n)],
                             inc=s_t)
                ckv = p.step("c", (lambda h: lambda e: e.activation(
                    KVB[h * 64:(h + 1) * 64, :], SC1[0:64, 0:65],
                    AF.Copy))(h), waits=[(s_t, tki)], inc=s_c)
                for c_ in range(nKT):
                    def amm(e, h=h, c_=c_):
                        return e.matmul(
                            SC2[:, 0:128],
                            KERK[h * 64:(h + 1) * 64,
                                 c_ * 128:(c_ + 1) * 128],
                            KERQ[h * 64:(h + 1) * 64,
                                 c_ * 128:(c_ + 1) * 128],
                            start=True, stop=True)
                    w = [(s_c, s_c.n), (s_v, s_v.n)]
                    tam = p.step("t", amm, waits=w, inc=s_t)
                    vmk = p.step("v", (lambda: lambda e: e.tensor_tensor(
                        SC2[:, 0:128], SC2[:, 0:128], UTRI[:], ALU.mult))(),
                        waits=[(s_t, tam), (s_x, cl)], inc=s_v)
                    cpt = p.step("c", lambda e: e.activation(
                        PTB[:], SC2[:, 0:128], AF.Copy),
                        waits=[(s_v, vmk)], inc=s_c)
                    def outc(e, h=h, c_=c_):
                        e.matmul(PO[0:65, 0:128],
                                 VT[h][:, c_ * 65:(c_ + 1) * 65], PTB[:],
                                 start=True, stop=False)
                        return e.matmul(
                            PO[0:65, 0:128], KVB[h * 64:(h + 1) * 64, :],
                            KERQ[h * 64:(h + 1) * 64,
                                 c_ * 128:(c_ + 1) * 128],
                            start=False, stop=True)
                    toc = p.step("t", outc, waits=[(s_c, cpt), (s_c, ckv)],
                                 inc=s_t)
                    cev = p.step("c", (lambda h, c_: lambda e: e.activation(
                        YU[h][0:65, c_ * 128:(c_ + 1) * 128],
                        PO[0:65, 0:128], AF.Copy))(h, c_),
                        waits=[(s_t, toc)], inc=s_c)
                    if c_ < nKT - 1:
                        def kvup(e, h=h, c_=c_):
                            return e.matmul(
                                SC1[0:64, 0:65],
                                KTOK[h][:, c_ * 64:(c_ + 1) * 64],
                                VT[h][:, c_ * 65:(c_ + 1) * 65],
                                start=False, stop=True)
                        tku = p.step("t", kvup, waits=[(s_t, toc)], inc=s_t)
                        ckv = p.step("c", (lambda h: lambda e: e.activation(
                            KVB[h * 64:(h + 1) * 64, :], SC1[0:64, 0:65],
                            AF.Copy))(h), waits=[(s_t, tku)], inc=s_c)
                cve = p.step("c", (lambda h: lambda e: e.activation(
                    RL32[0:1, :], YU[h][64:65, :], AF.Copy))(h),
                    waits=[(s_t, s_t.n)], inc=s_c)
                vrc = p.step("v", lambda e: (
                    e.tensor_single_scalar(RL32[0:1, :], RL32[0:1, :], 1e-6,
                                           ALU.add),
                    e.reciprocal(RL32[0:1, :], RL32[0:1, :]))[-1],
                    waits=[(s_c, cve)], inc=s_v)
                for qc in range(4):
                    trb = p.step("t", (lambda qc: lambda e: e.matmul(
                        RB[0:64, :], ONES32[0:1, 0:64],
                        RL32[0:1, qc * 512:(qc + 1) * 512], start=True,
                        stop=True))(qc), waits=[(s_v, s_v.n)],
                        inc=s_t)
                    crb = p.step("c", lambda e: e.activation(
                        RBS[:], RB[0:64, :], AF.Copy), waits=[(s_t, trb)],
                        inc=s_c)
                    ym = p.step("v", (lambda g, h, qc: lambda e:
                        e.tensor_tensor(
                            YATT[h * 64:(h + 1) * 64 if h == 0 else 128,
                                 g * T_ + qc * 512:g * T_ + (qc + 1) * 512],
                            YU[h][0:64, qc * 512:(qc + 1) * 512], RBS[:],
                            ALU.mult))(g, h, qc), waits=[(s_c, crb)],
                        inc=s_v)
                    yatt_marks.append(ym)
        vint = s_v.n

        wo_load = [None] * KC
        wo_mm = [None] * KC
        ep_m = [None] * KC
        for m in range(KC):
            w = [(s_t, prev_prj[0])] if m < 2 else [(s_t, wo_mm[m - 2])]
            wo_load[m] = p.step("g", (lambda m: lambda e: e.dma_start(
                out=WOB[m % 2], in_=wo_r[m].rearrange("p k c -> p (k c)")))(
                    m), waits=w, inc=s_w, k=16)
            pos_ = ([PRJ[:, 0:512], PRJ[:, 512:1024], PRJ[:, 1024:1536],
                     PRJ[:, 1536:2048]] if m % 2 == 0 else
                    [SC1[:], SC2[:], PO[:], RB[:]])

            def womm(e, m=m, pos_=pos_):
                last = None
                buf = WOB[m % 2]
                for k in range(KC):
                    for tc in range(4):
                        last = e.matmul(
                            pos_[tc], buf[:, k * 128:(k + 1) * 128],
                            YATT[:, k * T_ + tc * 512:k * T_ + (tc + 1) * 512],
                            start=(k == 0), stop=(k == KC - 1))
                return last
            w = [(s_w, wo_load[m]), (s_v, vint), (s_c, s_c.n)]
            if m >= 2:
                w.append((s_v, ep_m[m - 2]))
            wo_mm[m] = p.step("t", womm, waits=w, inc=s_t)
            xr = p.step("y", (lambda m: lambda e: e.dma_start(
                out=XRE[:, (m % 2) * T_:(m % 2 + 1) * T_],
                in_=xT[m * 128:(m + 1) * 128, :]))(m),
                waits=[(s_v, vint)] if m < 2 else [(s_v, ep_m[m - 2])],
                inc=s_x, k=16)

            def ep1(e, m=m, pos_=pos_):
                last = None
                for tc in range(4):
                    last = e.scalar_tensor_tensor(
                        OUT32[:, tc * 512:(tc + 1) * 512], pos_[tc],
                        BO[:, m:m + 1],
                        XRE[:, (m % 2) * T_ + tc * 512:
                            (m % 2) * T_ + (tc + 1) * 512],
                        ALU.add, ALU.add)
                return last
            ep_m[m] = p.step("v", ep1,
                             waits=[(s_t, wo_mm[m]), (s_x, xr),
                                    (s_st, 16 * m)], inc=s_v)
            p.step("y", (lambda m: lambda e: e.dma_start(
                out=yT[m * 128:(m + 1) * 128, :], in_=OUT32[:]))(m),
                waits=[(s_v, ep_m[m])], inc=s_st, k=16)

        def final(e):
            e.wait_ge(s_st.h, s_st.n)
        p.q["g"].append(final)
        p.run_block()
    return nc


def _gb(t, g):
    return t[:, g:g + 1]


def _sl16(srck, h, i):
    return srck[h * 64:(h + 1) * 64, i * 128:(i + 1) * 128]


def _id16h(ID16, h):
    return ID16[h * 64:(h + 1) * 64, h * 64:(h + 1) * 64]


def _pss4(a, b, c, d, tc):
    return [a, b, c, d][tc]


def l1attn_host_inputs(Wqkv, bqkv, gq, bq, gk, bk, Wo, bo):
    wq_r = np.ascontiguousarray(
        Wqkv.reshape(8, 128, 24, 128).transpose(2, 1, 0, 3))
    wo_r = np.ascontiguousarray(
        Wo.reshape(8, 128, 8, 128).transpose(2, 1, 0, 3))
    bqkv_c = np.ascontiguousarray(bqkv.reshape(24, 128).T)
    cols = lambda a: np.ascontiguousarray(a.reshape(8, 128).T)
    ut = np.triu(np.ones((128, 128), np.float32))
    return dict(wq_r=wq_r, wo_r=wo_r, bqkv_c=bqkv_c, gq_c=cols(gq),
                bq_c=cols(bq), gk_c=cols(gk), bk_c=cols(bk), bo_c=cols(bo),
                utri=ut, ident=np.eye(128, dtype=np.float32))


def kernel(**inputs):
    f32 = np.float32
    g = lambda k: np.asarray(inputs[k], f32)
    x = g("L0_attn_x")

    h_f0 = {k: np.ascontiguousarray(v, dtype=f32) for k, v in
            l0ffn_host_inputs(g("L0_ffn_W1"), g("L0_ffn_b1"),
                              g("L0_ffn_Wo"), g("L0_ffn_bo"),
                              g("L0_ffn_ln_gamma"), g("L0_ffn_ln_beta"),
                              g("L0_ffn_W_router")).items()}
    h_a0 = {k: np.ascontiguousarray(v, dtype=f32) for k, v in
            l0attn_host_inputs(g("L0_attn_W_qkv"),
                               g("L0_attn_W_o")).items()}
    h_f1 = dict(wg=g("L1_ffn_w_gate"), wu=g("L1_ffn_w_up"),
                wd=g("L1_ffn_w_down"), ident=np.eye(128, dtype=f32))
    h_a1 = {k: np.ascontiguousarray(v, dtype=f32) for k, v in
            l1attn_host_inputs(g("L1_attn_Wqkv"), g("L1_attn_bqkv"),
                               g("L1_attn_gamma_q"), g("L1_attn_beta_q"),
                               g("L1_attn_gamma_k"), g("L1_attn_beta_k"),
                               g("L1_attn_Wo"), g("L1_attn_bo")).items()}

    xT = [np.ascontiguousarray(x[c // 2, (c % 2) * T:(c % 2 + 1) * T, :].T)
          for c in range(NCORE)]
    ZP = np.zeros((D, T), f32)
    FNEG = np.full((1, 1), -1.0e38, f32)
    F0 = np.zeros((1, 1), f32)
    F1 = np.ones((1, 1), f32)

    def launch(nc, hosts):
        res = run_bass_kernel_spmd(nc, hosts, core_ids=list(range(NCORE)))
        return [np.ascontiguousarray(res.results[c]["yT"], dtype=f32)
                for c in range(NCORE)]

    def attn_hosts(hw, xT, fe, fo):
        hs = []
        for c in range(NCORE):
            if c % 2 == 0:
                hs.append(dict(hw, xT=xT[c], xpT=ZP, flag=fe))
            else:
                hs.append(dict(hw, xT=xT[c], xpT=xT[c - 1], flag=fo))
        return hs

    nc_f0 = build_l0ffn()
    nc_f1 = build_l1ffn()
    xT = launch(nc_f0, [dict(h_f0, xT=xT[c]) for c in range(NCORE)])
    xT = launch(build_l0attn(), attn_hosts(h_a0, xT, FNEG, F0))
    xT = launch(nc_f0, [dict(h_f0, xT=xT[c]) for c in range(NCORE)])
    xT = launch(nc_f1, [dict(h_f1, xT=xT[c]) for c in range(NCORE)])
    xT = launch(build_l1attn(), attn_hosts(h_a1, xT, F0, F1))
    xT = launch(nc_f1, [dict(h_f1, xT=xT[c]) for c in range(NCORE)])

    out = np.empty((4, 2 * T, D), f32)
    for c in range(NCORE):
        out[c // 2, (c % 2) * T:(c % 2 + 1) * T, :] = xT[c].T
    return out


# version 04
# speedup vs baseline: 1.0032x; 1.0032x over previous
import numpy as np
import ml_dtypes
from contextlib import ExitStack
import concourse.bass as bass
import concourse.mybir as mybir
from concourse.bass_utils import run_bass_kernel_spmd

F32 = mybir.dt.float32
BF16 = mybir.dt.bfloat16
I32 = mybir.dt.int32
AF = mybir.ActivationFunctionType
ALU = mybir.AluOpType
AX = mybir.AxisListType
BF = ml_dtypes.bfloat16

D = 1024
KC = 8
T = 2048
NCORE = 8


class Sem:
    __slots__ = ("h", "n", "name")

    def __init__(self, h, name):
        self.h = h
        self.n = 0
        self.name = name


class Prog:

    def __init__(self, nc, es):
        self.nc = nc
        self.es = es
        self.q = {k: [] for k in "tvcgy"}

    def sem(self, name):
        s = Sem(self.es.enter_context(self.nc.semaphore(name)), name)
        if not hasattr(self, "sem_handles"):
            self.sem_handles = []
        self.sem_handles.append(s.h)
        return s

    def sbuf(self, name, shape, dtype):
        return self.es.enter_context(self.nc.sbuf_tensor(name, shape, dtype))

    def psum(self, name, shape, dtype=F32):
        return self.es.enter_context(self.nc.psum_tensor(name, shape, dtype))

    def step(self, eng, build, waits=(), inc=None, k=1):
        wl = [(s.h, v) for (s, v) in waits]
        ih = inc.h if inc is not None else None

        def fn(e):
            for h, v in wl:
                e.wait_ge(h, v)
            last = build(e)
            if ih is not None:
                last.then_inc(ih, k)

        self.q[eng].append(fn)
        if inc is not None:
            inc.n += k
            return inc.n
        return None

    def run_block(self):
        self.nc.all_engine_barrier()
        for h in getattr(self, "sem_handles", []):
            self.nc.gpsimd.sem_clear(h)
        self.nc.all_engine_barrier()
        blk = self.es.enter_context(self.nc.Block())

        def body(key):
            def f(eng):
                for fn in self.q[key]:
                    fn(eng)
            return f

        blk.tensor(body("t"))
        blk.vector(body("v"))
        blk.scalar(body("c"))
        blk.gpsimd(body("g"))
        blk.sync(body("y"))


def emit_ln_stats(p, E, X, xh, TH, eps_ap, wait_first):
    if not isinstance(X, (list, tuple)):
        X = [X[:, k * TH:(k + 1) * TH] for k in range(KC)]
    if not isinstance(xh, (list, tuple)):
        xh = [xh[:, k * TH:(k + 1) * TH] for k in range(KC)]
    nTC = TH // 512
    s_t, s_v, s_c = E["s_t"], E["s_v"], E["s_c"]
    SQ, P1, P2, P3, P4 = E["SQ"], E["P1"], E["P2"], E["P3"], E["P4"]
    ONES = E["ONES"]
    R_MU, R_VAR, R_RSTD, R_TMP = E["R_MU"], E["R_VAR"], E["R_RSTD"], E["R_TMP"]
    R16_MU, R16_RSTD = E["R16_MU"], E["R16_RSTD"]
    MU_B, RSTD_B = E["MU_B"], E["RSTD_B"]
    mm_prev = s_t.n
    for k in range(KC):
        w = list(wait_first(k))
        if k >= 1:
            w.append((s_t, mm_prev))

        def sq_cast(e, k=k):
            e.activation(SQ[:, 0:TH], X[k], AF.Square)
            return e.activation(xh[k], X[k], AF.Copy)
        p.step("c", sq_cast, waits=w, inc=s_c)
        cw = s_c.n

        def mm(e, k=k):
            last = None
            for tc in range(nTC):
                sl = slice(tc * 512, (tc + 1) * 512)
                last = e.matmul(P1[0:1, sl], ONES[:, 0:1], xh[k][:, sl],
                                start=(k == 0), stop=(k == KC - 1))
                last = e.matmul(P2[0:1, sl], ONES[:, 0:1], SQ[:, sl],
                                start=(k == 0), stop=(k == KC - 1))
            return last
        mm_prev = p.step("t", mm, waits=[(s_c, cw)], inc=s_t)
    tmm = s_t.n

    def row_ops(e):
        e.tensor_single_scalar(R_MU[0:1, 0:TH], P1[0:1, 0:TH], 1.0 / D,
                               ALU.mult)
        e.tensor_tensor(R_TMP[0:1, 0:TH], R_MU[0:1, 0:TH], R_MU[0:1, 0:TH],
                        ALU.mult)
        return e.scalar_tensor_tensor(
            R_VAR[0:1, 0:TH], P2[0:1, 0:TH], 1.0 / D, R_TMP[0:1, 0:TH],
            ALU.mult, ALU.subtract)
    p.step("v", row_ops, waits=[(s_t, tmm)], inc=s_v)
    vr = s_v.n
    p.step("c", lambda e: e.activation(R_TMP[0:1, 0:TH], R_VAR[0:1, 0:TH],
                                       AF.Sqrt, bias=eps_ap),
           waits=[(s_v, vr)], inc=s_c)
    cr = s_c.n

    p.step("v", lambda e: e.reciprocal(R_RSTD[0:1, 0:TH], R_TMP[0:1, 0:TH]),
           waits=[(s_c, cr)], inc=s_v)
    vrc = s_v.n

    def r16(e):
        e.activation(R16_MU[0:1, 0:TH], R_MU[0:1, 0:TH], AF.Copy)
        return e.activation(R16_RSTD[0:1, 0:TH], R_RSTD[0:1, 0:TH], AF.Copy)
    p.step("c", r16, waits=[(s_v, vrc)], inc=s_c)
    vb = s_c.n

    def bcast(e):
        last = None
        for tc in range(nTC):
            sl = slice(tc * 512, (tc + 1) * 512)
            last = e.matmul(P3[:, sl], ONES[0:1, :], R16_MU[0:1, sl],
                            start=True, stop=True)
            last = e.matmul(P4[:, sl], ONES[0:1, :], R16_RSTD[0:1, sl],
                            start=True, stop=True)
        return last
    p.step("t", bcast, waits=[(s_c, vb)], inc=s_t)
    tb = s_t.n
    p.step("c", lambda e: e.activation(MU_B[:, 0:TH], P3[:, 0:TH], AF.Copy),
           waits=[(s_t, tb)], inc=s_c)
    p.step("c", lambda e: e.activation(RSTD_B[:, 0:TH], P4[:, 0:TH], AF.Copy),
           inc=s_c)
    return s_c.n


def build_l0ffn(T_=T, debug=False, router_passes=3):
    TH = T_ // 2
    nTC = TH // 512
    NN = 32
    nMT = TH // 128
    nc = bass.Bass("TRN2", target_bir_lowering=False)
    xT = nc.dram_tensor("xT", [D, T_], F32, kind="ExternalInput").ap()
    w1r = nc.dram_tensor("w1r", [4, NN, 128, KC, 128], F32,
                         kind="ExternalInput").ap()
    b1t = nc.dram_tensor("b1t", [128, 4 * NN], F32, kind="ExternalInput").ap()
    wor = nc.dram_tensor("wor", [KC, 128, NN, 128], F32,
                         kind="ExternalInput").ap()
    bo_c = nc.dram_tensor("bo_c", [128, KC], F32, kind="ExternalInput").ap()
    lng_c = nc.dram_tensor("lng_c", [128, KC], F32, kind="ExternalInput").ap()
    lnb_c = nc.dram_tensor("lnb_c", [128, KC], F32, kind="ExternalInput").ap()
    wrp = nc.dram_tensor("wr", [128, KC * 4], F32, kind="ExternalInput").ap()
    ident = nc.dram_tensor("ident", [128, 128], F32, kind="ExternalInput").ap()
    yT = nc.dram_tensor("yT", [D, T_], F32, kind="ExternalOutput").ap()
    if debug:
        TH_ = T_ // 2
        dbg_xlh = nc.dram_tensor("dbg_xlh", [D, TH_], F32, kind="ExternalOutput").ap()
        dbg_mask = nc.dram_tensor("dbg_mask", [128, 32], F32, kind="ExternalOutput").ap()
        dbg_maskb = nc.dram_tensor("dbg_maskb", [128, 4 * TH_], F32, kind="ExternalOutput").ap()
        dbg_inter = nc.dram_tensor("dbg_inter", [128, 32 * TH_], F32, kind="ExternalOutput").ap()
        dbg_y = nc.dram_tensor("dbg_y", [D, TH_], F32, kind="ExternalOutput").ap()
        dbg_mu = nc.dram_tensor("dbg_mu", [128, TH_], F32, kind="ExternalOutput").ap()
        dbg_rstd = nc.dram_tensor("dbg_rstd", [128, TH_], F32, kind="ExternalOutput").ap()
        dbg_x0 = nc.dram_tensor("dbg_x0", [D, TH_], F32, kind="ExternalOutput").ap()
        dbg_lg = nc.dram_tensor("dbg_lg", [128, 32], F32, kind="ExternalOutput").ap()
        dbg_xll = nc.dram_tensor("dbg_xll", [128, TH_], F32, kind="ExternalOutput").ap()
        dbg_wrl = nc.dram_tensor("dbg_wrl", [128, 64], F32, kind="ExternalOutput").ap()

    es = ExitStack()
    with es:
        p = Prog(nc, es)
        X = p.sbuf("X", [128, KC * TH], F32)
        XRE = p.sbuf("XRE", [128, 2 * TH], F32)
        XL1 = p.sbuf("XL1", [128, TH], F32)
        XLH = p.sbuf("XLH", [128, KC * TH], BF16)
        XLL = p.sbuf("XLL", [128, KC * TH], BF16)
        INTER = p.sbuf("INTER", [128, NN * TH], BF16)
        MASKB = p.sbuf("MASKB", [128, 4 * TH], BF16)
        MU_B = p.sbuf("MU_B", [128, TH], F32)
        RSTD_B = p.sbuf("RSTD_B", [128, TH], F32)
        SQ = p.sbuf("SQ", [128, TH], BF16)
        W1B = [p.sbuf(f"W1B{i}", [128, KC * 128], BF16) for i in range(2)]
        GE = [p.sbuf(f"GE{i}", [128, TH], BF16) for i in range(2)]
        GM = p.sbuf("GM", [128, 512], BF16)
        B1 = p.sbuf("B1", [128, 4 * NN], F32)
        BO = p.sbuf("BO", [128, KC], F32)
        LNG = p.sbuf("LNG", [128, KC], F32)
        LNB = p.sbuf("LNB", [128, KC], F32)
        WRH = p.sbuf("WRH", [128, KC * 4], BF16)
        WRL = p.sbuf("WRL", [128, KC * 4], BF16)
        WR32 = p.sbuf("WR32", [128, KC * 4], F32)
        ID16 = p.sbuf("ID16", [128, 128], BF16)
        ONES = p.sbuf("ONES", [128, 128], BF16)
        R_MU = p.sbuf("R_MU", [1, TH], F32)
        R_VAR = p.sbuf("R_VAR", [1, TH], F32)
        R_RSTD = p.sbuf("R_RSTD", [1, TH], F32)
        R_TMP = p.sbuf("R_TMP", [1, TH], F32)
        R16_MU = p.sbuf("R16_MU", [1, TH], BF16)
        R16_RSTD = p.sbuf("R16_RSTD", [1, TH], BF16)
        MROW = p.sbuf("MROW", [1, TH], BF16)
        MASK_TM = p.sbuf("MASK_TM", [128, nMT * 4], F32)
        MCOL16 = p.sbuf("MCOL16", [128, nMT * 4], BF16)
        LGS = p.sbuf("LGS", [128, 32], F32)
        RMX = p.sbuf("RMX", [128, 8], F32)
        WSEL = p.sbuf("WSEL", [128, 8], F32)
        EPS5 = p.sbuf("EPS5", [1, 1], F32)
        EPS12 = p.sbuf("EPS12", [1, 1], F32)
        WOB = [XLL[:, 0:4096], XLL[:, 4096:8192]]
        OUT32 = XLL[:].bitcast(F32)[:, 0:TH]

        P1 = p.psum("P1", [128, TH])
        P2 = p.psum("P2", [128, TH])
        P3 = p.psum("P3", [128, TH])
        P4 = p.psum("P4", [128, TH])

        s_t = p.sem("s_t")
        s_v = p.sem("s_v")
        s_c = p.sem("s_c")
        s_x = p.sem("s_x")
        s_w = p.sem("s_w")
        s_st = p.sem("s_st")
        s_m = p.sem("s_m")

        E = dict(s_t=s_t, s_v=s_v, s_c=s_c, SQ=SQ, P1=P1, P2=P2, P3=P3,
                 P4=P4, ONES=ONES, R_MU=R_MU, R_VAR=R_VAR, R_RSTD=R_RSTD,
                 R_TMP=R_TMP, R16_MU=R16_MU, R16_RSTD=R16_RSTD,
                 MU_B=MU_B, RSTD_B=RSTD_B)

        p.step("y", lambda e: e.dma_start(out=B1[:], in_=b1t), inc=s_x, k=16)
        p.step("y", lambda e: e.dma_start(out=BO[:], in_=bo_c), inc=s_x, k=16)
        p.step("y", lambda e: e.dma_start(out=LNG[:], in_=lng_c), inc=s_x, k=16)
        p.step("y", lambda e: e.dma_start(out=LNB[:], in_=lnb_c), inc=s_x, k=16)
        p.step("y", lambda e: e.dma_start(out=WR32[:], in_=wrp),
               inc=s_x, k=16)
        p.step("g", lambda e: e.dma_start(out=ID16[:], in_=ident),
               inc=s_w, k=16)
        cl = s_x.n
        def const_g(e):
            e.memset(ONES[:], 1.0)
            e.memset(EPS5[:], 1e-5)
            return e.memset(EPS12[:], 1e-12)
        p.step("g", const_g, waits=[(s_w, s_w.n)], inc=s_m)
        gmemset = s_m.n
        p.step("c", lambda e: e.activation(WRH[:], WR32[:], AF.Copy),
               waits=[(s_x, cl)], inc=s_c)
        cwrh = s_c.n
        p.step("v", lambda e: e.scalar_tensor_tensor(
            WRL[:], WRH[:], -1.0, WR32[:], ALU.mult, ALU.add),
            waits=[(s_c, cwrh)], inc=s_v)
        const_v = s_v.n
        const_c = s_c.n
        const_g = gmemset

        for half in range(2):
            t0 = half * TH
            xloads = []
            for k in range(KC):
                m = p.step("y", (lambda k, t0: lambda e: e.dma_start(
                    out=X[:, k * TH:(k + 1) * TH],
                    in_=xT[k * 128:(k + 1) * 128, t0:t0 + TH]))(k, t0),
                    waits=[(s_st, 16 * KC * half)], inc=s_x, k=16)
                xloads.append(m)

            def wait_first(k, xloads=xloads):
                w = [(s_x, xloads[k])]
                if k == 0:
                    w.append((s_v, const_v))
                    w.append((s_m, const_g))
                return w
            cmu = emit_ln_stats(p, E, X, XLH, TH, EPS5[0:1, 0:1], wait_first)
            for k in range(KC):
                sl = slice(k * TH, (k + 1) * TH)
                x1 = XL1[:, (k % 2) * 0:]

                def applya(e, sl=sl, k=k):
                    e.tensor_tensor(XL1[:], X[:, sl], MU_B[:], ALU.subtract)
                    return e.tensor_tensor(XL1[:], XL1[:], RSTD_B[:], ALU.mult)
                w = [(s_c, cmu)] if k == 0 else [(s_c, s_c.n)]
                p.step("v", applya, waits=w, inc=s_v)
                va = s_v.n
                p.step("c", (lambda sl: lambda e: e.activation(
                    XLH[:, sl], XL1[:], AF.Copy))(sl),
                    waits=[(s_v, va)], inc=s_c)
                ca = s_c.n
                p.step("v", (lambda sl: lambda e: e.scalar_tensor_tensor(
                    XLL[:, sl], XLH[:, sl], -1.0, XL1[:],
                    ALU.mult, ALU.add))(sl), waits=[(s_c, ca)], inc=s_v)
            vln = s_v.n
            cln = s_c.n
            if debug and half == 0:
                p.step("g", lambda e: e.dma_start(out=dbg_mu, in_=MU_B[:]),
                       waits=[(s_v, vln)], inc=s_w, k=16)
                p.step("g", lambda e: e.dma_start(out=dbg_rstd, in_=RSTD_B[:]),
                       waits=[(s_v, vln)], inc=s_w, k=16)
                for k in range(KC):
                    p.step("g", (lambda k: lambda e: e.dma_start(
                        out=dbg_x0[k * 128:(k + 1) * 128, :],
                        in_=X[:, k * TH:(k + 1) * TH]))(k),
                        waits=[(s_v, vln)], inc=s_w, k=16)
                for k in range(KC):
                    p.step("g", (lambda k: lambda e: e.dma_start(
                        out=dbg_xlh[k * 128:(k + 1) * 128, :],
                        in_=XLH[:, k * TH:(k + 1) * TH]))(k),
                        waits=[(s_v, vln)], inc=s_w, k=16)

            def router_mm(e):
                last = None
                for mt in range(nMT):
                    psl = P1[:, mt * 4:(mt + 1) * 4]
                    passes = ((XLH, WRH, True, False),
                              (XLH, WRL, False, False),
                              (XLL, WRH, False, True))[:router_passes]
                    passes = tuple(
                        (a, b, st, (i == len(passes) - 1))
                        for i, (a, b, st, _) in enumerate(passes))
                    for (lh, rh, st, fi) in passes:
                        for k in range(KC):
                            last = e.matmul(
                                psl,
                                lh[:, k * TH + mt * 128:k * TH + (mt + 1) * 128],
                                rh[:, k * 4:(k + 1) * 4],
                                start=(st and k == 0),
                                stop=(fi and k == KC - 1))
                return last
            p.step("t", router_mm, waits=[(s_v, vln), (s_c, cln)], inc=s_t)
            trt = s_t.n
            clg = p.step("c", lambda e: e.activation(LGS[:], P1[:, 0:32],
                                                     AF.Copy),
                         waits=[(s_t, trt)], inc=s_c)
            if debug and half == 0:
                p.step("g", lambda e: e.dma_start(out=dbg_lg, in_=LGS[:]),
                       waits=[(s_c, clg)], inc=s_w, k=16)
                p.step("g", lambda e: e.dma_start(out=dbg_xll, in_=XLL[:, 0:TH]),
                       waits=[(s_c, clg)], inc=s_w, k=16)
                p.step("g", lambda e: e.dma_start(out=dbg_wrl[:, 0:32], in_=WRH[:]),
                       waits=[(s_c, clg)], inc=s_w, k=16)
                p.step("g", lambda e: e.dma_start(out=dbg_wrl[:, 32:64], in_=WRL[:]),
                       waits=[(s_c, clg)], inc=s_w, k=16)
            def mask_ops(e):
                lg3 = LGS[:].rearrange("p (m e) -> p m e", e=4)
                e.tensor_reduce(RMX[:], lg3, AX.X, ALU.max)
                e.tensor_scalar(WSEL[:], RMX[:], 0.0, 1.0, ALU.mult, ALU.add)
                last = None
                for ex in range(4):
                    for mt in range(nMT):
                        e.tensor_tensor(MASK_TM[:, mt * 4 + ex:mt * 4 + ex + 1],
                                        LGS[:, mt * 4 + ex:mt * 4 + ex + 1],
                                        RMX[:, mt:mt + 1], ALU.is_ge)
                    for mt in range(nMT):
                        e.tensor_tensor(MASK_TM[:, mt * 4 + ex:mt * 4 + ex + 1],
                                        MASK_TM[:, mt * 4 + ex:mt * 4 + ex + 1],
                                        WSEL[:, mt:mt + 1], ALU.mult)
                    for mt in range(nMT):
                        last = e.tensor_tensor(
                            WSEL[:, mt:mt + 1], WSEL[:, mt:mt + 1],
                            MASK_TM[:, mt * 4 + ex:mt * 4 + ex + 1],
                            ALU.subtract)
                return last
            p.step("v", mask_ops, waits=[(s_c, clg)], inc=s_v)
            vmk0 = s_v.n
            p.step("c", lambda e: e.activation(MCOL16[:], MASK_TM[:], AF.Copy),
                   waits=[(s_v, vmk0)], inc=s_c)
            vmk_c = s_c.n
            vmk = s_v.n
            if debug and half == 0:
                p.step("g", lambda e: e.dma_start(out=dbg_mask, in_=MASK_TM[:]),
                       waits=[(s_v, vmk)], inc=s_w, k=16)
            cmask = s_c.n
            for ex in range(4):
                def mrow_mm(e, ex=ex):
                    last = None
                    for mt in range(nMT):
                        last = e.matmul(
                            P2[0:1, mt * 128:(mt + 1) * 128],
                            MCOL16[:, mt * 4 + ex:mt * 4 + ex + 1],
                            ID16[:], start=True, stop=True)
                    return last
                w = [(s_c, vmk_c)] if ex == 0 else [(s_c, cmask)]
                p.step("t", mrow_mm, waits=w, inc=s_t)
                tmr = s_t.n
                p.step("c", lambda e: e.activation(MROW[0:1, :],
                                                   P2[0:1, 0:TH], AF.Copy),
                       waits=[(s_t, tmr)], inc=s_c)
                vmr = s_c.n

                def mb_mm(e):
                    last = None
                    for tc in range(nTC):
                        sl = slice(tc * 512, (tc + 1) * 512)
                        last = e.matmul(P3[:, sl], ONES[0:1, :],
                                        MROW[0:1, sl], start=True, stop=True)
                    return last
                p.step("t", mb_mm, waits=[(s_c, vmr)], inc=s_t)
                tmb = s_t.n
                cmask = p.step("c", (lambda ex: lambda e: e.activation(
                    MASKB[:, ex * TH:(ex + 1) * TH], P3[:, 0:TH],
                    AF.Copy))(ex), waits=[(s_t, tmb)], inc=s_c)

            order = [(n, ex) for n in range(NN) for ex in range(4)]
            w1_load = [None] * len(order)
            w1_mm = [None] * len(order)
            gelu_m = [None] * len(order)
            merge_m = [None] * len(order)
            for i, (n, ex) in enumerate(order):
                buf = W1B[i % 2]
                w = []
                if i >= 2:
                    w.append((s_t, w1_mm[i - 2]))
                w1_load[i] = p.step("g", (lambda n, ex, buf: lambda e:
                    e.dma_start(out=buf[:],
                                in_=w1r[ex, n].rearrange("p k i -> p (k i)")))(
                        n, ex, buf), waits=w, inc=s_w, k=16)
                hp = P1 if (i % 2 == 0) else P2

                def w1mm(e, buf=buf, hp=hp):
                    last = None
                    for k in range(KC):
                        for tc in range(nTC):
                            last = e.matmul(
                                hp[:, tc * 512:(tc + 1) * 512],
                                buf[:, k * 128:(k + 1) * 128],
                                XLH[:, k * TH + tc * 512:k * TH + (tc + 1) * 512],
                                start=(k == 0), stop=(k == KC - 1))
                    return last
                w = [(s_w, w1_load[i]), (s_c, cmask)]
                if i >= 2:
                    w.append((s_c, gelu_m[i - 2]))
                w1_mm[i] = p.step("t", w1mm, waits=w, inc=s_t)
                w = [(s_t, w1_mm[i])]
                if i >= 2:
                    w.append((s_v, merge_m[i - 2]))
                gelu_m[i] = p.step("c", (lambda n, ex, hp, i: lambda e:
                    e.activation(GE[i % 2][:], hp[:, 0:TH], AF.Gelu,
                                 bias=B1[:, ex * NN + n:ex * NN + n + 1]))(
                        n, ex, hp, i), waits=w, inc=s_c)

                def merge(e, n=n, ex=ex, i=i):
                    last = None
                    for tc in range(nTC):
                        sl = slice(tc * 512, (tc + 1) * 512)
                        dst = INTER[:, n * TH + tc * 512:n * TH + (tc + 1) * 512]
                        msk = MASKB[:, ex * TH + tc * 512:
                                    ex * TH + (tc + 1) * 512]
                        if ex == 0:
                            last = e.tensor_tensor(dst, GE[i % 2][:, sl], msk,
                                                   ALU.mult)
                        else:
                            e.tensor_tensor(GM[:], GE[i % 2][:, sl], msk,
                                            ALU.mult)
                            last = e.tensor_tensor(dst, dst, GM[:], ALU.add)
                    return last
                merge_m[i] = p.step("v", merge, waits=[(s_c, gelu_m[i])],
                                    inc=s_v)
            vint = s_v.n
            if debug and half == 0:
                p.step("g", lambda e: e.dma_start(out=dbg_maskb, in_=MASKB[:]),
                       waits=[(s_c, cmask)], inc=s_w, k=16)
                p.step("g", lambda e: e.dma_start(out=dbg_inter, in_=INTER[:]),
                       waits=[(s_v, vint)], inc=s_w, k=16)

            wo_load = [None] * KC
            wo_mm = [None] * KC
            ep_m = [None] * KC
            for m in range(KC):
                w = [(s_t, trt), (s_v, vmk)]
                if m >= 2:
                    w.append((s_t, wo_mm[m - 2]))
                wo_load[m] = p.step("g", (lambda m: lambda e: e.dma_start(
                    out=WOB[m % 2], in_=wor[m].rearrange("p k c -> p (k c)")))(
                        m), waits=w, inc=s_w, k=16)
                po = P3 if (m % 2 == 0) else P4

                def womm(e, m=m, po=po):
                    last = None
                    buf = WOB[m % 2]
                    for k in range(NN):
                        for tc in range(nTC):
                            last = e.matmul(
                                po[:, tc * 512:(tc + 1) * 512],
                                buf[:, k * 128:(k + 1) * 128],
                                INTER[:, k * TH + tc * 512:k * TH + (tc + 1) * 512],
                                start=(k == 0), stop=(k == NN - 1))
                    return last
                w = [(s_w, wo_load[m]), (s_v, vint)]
                if m >= 2:
                    w.append((s_v, ep_m[m - 2]))
                wo_mm[m] = p.step("t", womm, waits=w, inc=s_t)

                def ep1(e, m=m, po=po):
                    sl = slice(m * TH, (m + 1) * TH)
                    e.tensor_tensor(XL1[:], X[:, sl], MU_B[:], ALU.subtract)
                    e.tensor_tensor(XL1[:], XL1[:], RSTD_B[:], ALU.mult)
                    return e.scalar_tensor_tensor(
                        X[:, sl], po[:, 0:TH], BO[:, m:m + 1], XL1[:],
                        ALU.add, ALU.add)
                ep_m[m] = p.step("v", ep1, waits=[(s_t, wo_mm[m])], inc=s_v)
            vy = s_v.n
            if debug and half == 0:
                for k in range(KC):
                    p.step("g", (lambda k: lambda e: e.dma_start(
                        out=dbg_y[k * 128:(k + 1) * 128, :],
                        in_=X[:, k * TH:(k + 1) * TH]))(k),
                        waits=[(s_v, vy)], inc=s_w, k=16)

            def wait_first2(k, vy=vy):
                return [(s_v, vy)]
            cmu2 = emit_ln_stats(p, E, X, XLH, TH, EPS12[0:1, 0:1],
                                 wait_first2)
            fin_marks = []
            for k in range(KC):
                xw = [(s_v, vy)] if k < 2 else [(s_v, fin_marks[k - 2])]
                xr = p.step("y", (lambda k, t0: lambda e: e.dma_start(
                    out=XRE[:, (k % 2) * TH:(k % 2 + 1) * TH],
                    in_=xT[k * 128:(k + 1) * 128, t0:t0 + TH]))(k, t0),
                    waits=xw, inc=s_x, k=16)

                def fin(e, k=k):
                    sl = slice(k * TH, (k + 1) * TH)
                    e.tensor_tensor(OUT32[:, 0:TH], X[:, sl], MU_B[:],
                                    ALU.subtract)
                    e.tensor_tensor(OUT32[:, 0:TH], OUT32[:, 0:TH],
                                    RSTD_B[:], ALU.mult)
                    e.tensor_scalar(OUT32[:, 0:TH], OUT32[:, 0:TH],
                                    LNG[:, k:k + 1], LNB[:, k:k + 1],
                                    ALU.mult, ALU.add)
                    e.tensor_single_scalar(OUT32[:, 0:TH], OUT32[:, 0:TH],
                                           0.5, ALU.mult)
                    return e.tensor_tensor(
                        OUT32[:, 0:TH], OUT32[:, 0:TH],
                        XRE[:, (k % 2) * TH:(k % 2 + 1) * TH], ALU.add)
                vout = p.step("v", fin,
                              waits=[(s_c, cmu2), (s_x, xr),
                                     (s_st, 16 * (KC * half + k))], inc=s_v)
                fin_marks.append(vout)
                p.step("y", (lambda k, t0: lambda e: e.dma_start(
                    out=yT[k * 128:(k + 1) * 128, t0:t0 + TH],
                    in_=OUT32[:, 0:TH]))(k, t0),
                    waits=[(s_v, vout)], inc=s_st, k=16)

        def final(e):
            e.wait_ge(s_st.h, s_st.n)
        p.q["g"].append(final)
        p.run_block()
    return nc


def l0ffn_host_inputs(W1, b1, Wo, bo, lng, lnb, Wr):
    w1r = np.ascontiguousarray(
        W1.reshape(4, 8, 128, 32, 128).transpose(0, 3, 2, 1, 4))
    b1t = np.ascontiguousarray(
        b1.reshape(4, 32, 128).transpose(2, 0, 1).reshape(128, 128))
    wor = np.ascontiguousarray(
        Wo.reshape(32, 128, 8, 128).transpose(2, 1, 0, 3))
    bo_c = np.ascontiguousarray(bo.reshape(8, 128).T)
    lng_c = np.ascontiguousarray(lng.reshape(8, 128).T)
    lnb_c = np.ascontiguousarray(lnb.reshape(8, 128).T)
    ident = np.eye(128, dtype=np.float32)
    return dict(w1r=w1r, b1t=b1t, wor=wor, bo_c=bo_c, lng_c=lng_c,
                lnb_c=lnb_c,
                wr=np.ascontiguousarray(
                    Wr.reshape(8, 128, 4).transpose(1, 0, 2).reshape(128, 32)),
                ident=ident)




def build_l0attn(T_=T, upto=None, ngroups=KC, nsets=5):
    TH = T_ // 2
    nc = bass.Bass("TRN2", target_bir_lowering=False)
    xT = nc.dram_tensor("xT", [D, T_], F32, kind="ExternalInput").ap()
    xpT = nc.dram_tensor("xpT", [D, T_], F32, kind="ExternalInput").ap()
    wq_r = nc.dram_tensor("wq_r", [24, 128, KC, 128], F32,
                          kind="ExternalInput").ap()
    wo_r = nc.dram_tensor("wo_r", [KC, 128, KC, 128], F32,
                          kind="ExternalInput").ap()
    dmask = nc.dram_tensor("dmask", [128, 4 * 512], F32,
                           kind="ExternalInput").ap()
    flag = nc.dram_tensor("flag", [1, 1], F32, kind="ExternalInput").ap()
    ident = nc.dram_tensor("ident", [128, 128], F32,
                           kind="ExternalInput").ap()
    yT = nc.dram_tensor("yT", [D, T_], F32, kind="ExternalOutput").ap()

    es = ExitStack()
    with es:
        p = Prog(nc, es)
        YATT = p.sbuf("YATT", [128, KC * T_], BF16)
        X = YATT[:].bitcast(F32)
        XLH = p.sbuf("XLH", [128, KC * T_], BF16)
        XPH = p.sbuf("XPH", [128, KC * T_], BF16)
        MU_B = p.sbuf("MU_B", [128, TH], F32)
        RSTD_B = p.sbuf("RSTD_B", [128, TH], F32)
        SQ = p.sbuf("SQ", [128, TH], BF16)
        XL1 = p.sbuf("XL1", [128, TH], F32)
        QA = [p.sbuf(f"QA{h}", [65, T_], BF16) for h in range(2)]
        KA = [p.sbuf(f"KA{h}", [65, T_], BF16) for h in range(2)]
        KP = [p.sbuf(f"KP{h}", [65, T_], BF16) for h in range(2)]
        VT = [p.sbuf(f"VT{h}", [128, 16 * 65], BF16) for h in range(2)]
        VPT = [p.sbuf(f"VPT{h}", [128, 16 * 65], BF16) for h in range(2)]
        VSTG = p.sbuf("VSTG", [128, T_], F32)
        PT = [p.sbuf(f"PT{i}", [128, 512], BF16) for i in range(2)]
        WQB = [p.sbuf(f"WQB{i}", [128, 3 * KC * 128], BF16) for i in range(2)]
        WOB = [p.sbuf(f"WOB{i}", [128, KC * 128], BF16) for i in range(2)]
        AB = p.sbuf("AB", [128, 4], F32)
        M3 = p.sbuf("M3", [4, 1], F32)
        R1 = p.sbuf("R1", [1, 4], F32)
        M1 = p.sbuf("M1", [1, 1], F32)
        CS = p.sbuf("CS", [1, 4], F32)
        RL32 = p.sbuf("RL32", [1, 512], F32)
        RL16 = p.sbuf("RL16", [1, 512], BF16)
        RBS = p.sbuf("RBS", [64, 512], F32)
        DM = p.sbuf("DM", [128, 4 * 512], F32)
        FLG = p.sbuf("FLG", [1, 1], F32)
        ONEROW = p.sbuf("ONEROW", [1, T_], BF16)
        ID32 = p.sbuf("ID32", [128, 128], F32)
        ID16 = p.sbuf("ID16", [128, 128], BF16)
        ONES = p.sbuf("ONES", [128, 128], BF16)
        XRE = XPH[:].bitcast(F32)[:, 0:2 * T_]
        OUT32 = XPH[:].bitcast(F32)[:, 2 * T_:3 * T_]
        R_MU = p.sbuf("R_MU", [1, TH], F32)
        R_VAR = p.sbuf("R_VAR", [1, TH], F32)
        R_RSTD = p.sbuf("R_RSTD", [1, TH], F32)
        R_TMP = p.sbuf("R_TMP", [1, TH], F32)
        R16_MU = p.sbuf("R16_MU", [1, TH], BF16)
        R16_RSTD = p.sbuf("R16_RSTD", [1, TH], BF16)
        EPS5 = p.sbuf("EPS5", [1, 1], F32)

        PRJ = p.psum("PRJ", [128, 2048])
        SC1 = p.psum("SC1", [128, 512])
        SC2 = p.psum("SC2", [128, 512])
        PO = p.psum("PO", [128, 512])
        RB = p.psum("RB", [128, 512])

        s_t = p.sem("s_t")
        s_v = p.sem("s_v")
        s_c = p.sem("s_c")
        s_x = p.sem("s_x")
        s_w = p.sem("s_w")
        s_st = p.sem("s_st")
        s_m = p.sem("s_m")

        E = dict(s_t=s_t, s_v=s_v, s_c=s_c, SQ=SQ,
                 P1=PRJ[:, 0:1024], P2=PRJ[:, 1024:2048],
                 P3=SC1, P4=SC2, ONES=ONES, R_MU=R_MU, R_VAR=R_VAR,
                 R_RSTD=R_RSTD, R_TMP=R_TMP, R16_MU=R16_MU,
                 R16_RSTD=R16_RSTD, MU_B=MU_B, RSTD_B=RSTD_B)
        E["P1"] = SC1
        E["P2"] = SC2
        E["P3"] = PRJ[:, 0:1024]
        E["P4"] = PRJ[:, 1024:2048]
        E["P1"] = PRJ[:, 0:1024]
        E["P2"] = PRJ[:, 1024:2048]
        E["P3"] = PRJ[:, 0:1024]
        E["P4"] = PRJ[:, 1024:2048]

        p.step("y", lambda e: e.dma_start(out=DM[:], in_=dmask),
               inc=s_x, k=16)
        p.step("y", lambda e: e.dma_start(out=FLG[:], in_=flag), inc=s_x, k=16)
        p.step("y", lambda e: e.dma_start(out=ID32[:], in_=ident),
               inc=s_x, k=16)
        p.step("g", lambda e: e.dma_start(out=ID16[:], in_=ident),
               inc=s_w, k=16)

        def const_g(e):
            e.memset(ONES[:], 1.0)
            e.memset(ONEROW[:], 1.0)
            e.memset(EPS5[:], 1e-5)
            for h in range(2):
                e.memset(QA[h][64:65, :], 1.0)
                e.memset(VT[h][:], 1.0)
                e.memset(VPT[h][:], 1.0)
            return e.memset(XL1[:, 0:1], 0.0)
        p.step("g", const_g, waits=[(s_w, s_w.n)], inc=s_m)
        const_w = s_m.n
        cl = s_x.n

        def do_ln(src_ap, dst, base_st):
            for th in range(2):
                t0 = th * TH
                vcur = s_v.n
                ccur = s_c.n
                xloads = []
                for k in range(KC):
                    m = p.step("y", (lambda k, t0: lambda e: e.dma_start(
                        out=X[:, k * TH:(k + 1) * TH],
                        in_=src_ap[k * 128:(k + 1) * 128, t0:t0 + TH]))(k, t0),
                        waits=[(s_v, vcur), (s_c, ccur)], inc=s_x, k=16)
                    xloads.append(m)

                def wait_first(k, xloads=xloads):
                    w = [(s_x, xloads[k])]
                    if k == 0:
                        w.append((s_m, const_w))
                    return w
                xh_chunks = [dst[:, k * T_ + t0: k * T_ + t0 + TH]
                             for k in range(KC)]
                cmu = emit_ln_stats(p, E, X, xh_chunks, TH, EPS5[0:1, 0:1],
                                    wait_first)
                for k in range(KC):
                    def applya(e, k=k, t0=t0):
                        e.tensor_tensor(XL1[:], X[:, k * TH:(k + 1) * TH],
                                        MU_B[:], ALU.subtract)
                        return e.tensor_tensor(XL1[:], XL1[:], RSTD_B[:],
                                               ALU.mult)
                    w = [(s_c, cmu)] if k == 0 else [(s_c, s_c.n)]
                    p.step("v", applya, waits=w, inc=s_v)
                    va = s_v.n
                    p.step("c", (lambda k, t0: lambda e: e.activation(
                        dst[:, k * T_ + t0: k * T_ + t0 + TH], XL1[:],
                        AF.Copy))(k, t0), waits=[(s_v, va)], inc=s_c)
            return s_c.n, s_v.n
        cln1, vln1 = do_ln(xT, XLH, 0)
        cln2, vln2 = do_ln(xpT, XPH, s_v.n)
        cln = s_c.n
        if upto == "ln":
            def final0(e):
                e.wait_ge(s_c.h, cln)
            p.q["g"].append(final0)
            p.run_block()
            return nc

        nKT = T_ // 128
        exp_marks = {}
        yatt_marks = []
        prj_read_m = [0]
        prev_cv = (s_c.n, s_v.n)
        for g in range(ngroups):
            sets = [("q", g, XLH), ("k", 8 + g, XLH), ("v", 16 + g, XLH),
                    ("kp", 8 + g, XPH), ("vp", 16 + g, XPH)][:nsets]
            wb = WQB[g % 2]
            w = [] if g < 2 else [(s_t, prj_read_m[0])]
            wl = p.step("g", (lambda g, wb: lambda e: e.dma_start(
                out=wb[:, 0:KC * 128],
                in_=wq_r[g, :, :, :].rearrange("p k c -> p (k c)")))(g, wb),
                waits=w, inc=s_w, k=16)
            wl2 = p.step("g", (lambda g, wb: lambda e: e.dma_start(
                out=wb[:, KC * 128:2 * KC * 128],
                in_=wq_r[8 + g, :, :, :].rearrange("p k c -> p (k c)")))(
                    g, wb), inc=s_w, k=16)
            wl3 = p.step("g", (lambda g, wb: lambda e: e.dma_start(
                out=wb[:, 2 * KC * 128:3 * KC * 128],
                in_=wq_r[16 + g, :, :, :].rearrange("p k c -> p (k c)")))(
                    g, wb), inc=s_w, k=16)
            wq_done = wl3
            if upto == "projw":
                continue
            prj_marks = []
            for si, (kind, mchunk, src) in enumerate(sets):
                woff = (0 if kind in ("q",) else
                        KC * 128 if kind in ("k", "kp") else 2 * KC * 128)

                def prjmm(e, woff=woff, src=src, wb=wb):
                    last = None
                    for k in range(KC):
                        lh32 = wb[:, woff + k * 128: woff + (k + 1) * 128]
                        for tc in range(4):
                            last = e.matmul(
                                PRJ[:, tc * 512:(tc + 1) * 512],
                                lh32, src[:, k * T_ + tc * 512:
                                          k * T_ + (tc + 1) * 512],
                                start=(k == 0), stop=(k == KC - 1))
                    return last
                w = [(s_w, wq_done), (s_c, cln)]
                if si > 0:
                    w.append((s_c, prj_marks[-1][0]))
                    w.append((s_v, prj_marks[-1][1]))
                else:
                    w.append((s_c, prev_cv[0]))
                    w.append((s_v, prev_cv[1]))
                tm = p.step("t", prjmm, waits=w, inc=s_t)
                if upto == "projm":
                    prj_marks.append((s_c.n, s_v.n, s_t.n))
                    continue
                vm = s_v.n
                if upto != "proje" and kind in ("q", "k", "kp"):
                    col = {"q": 0, "k": 1, "kp": 2}[kind]
                    vm = p.step("v", (lambda col: lambda e: e.tensor_reduce(
                        AB[:, col:col + 1], PRJ[:, 0:2048], AX.X, ALU.max,
                        apply_absolute_value=True))(col),
                        waits=[(s_t, tm)], inc=s_v)
                if upto == "projr":
                    prj_marks.append((s_c.n, s_v.n, s_t.n))
                    continue
                if kind in ("q", "k", "kp"):
                    dst = {"q": QA, "k": KA, "kp": KP}[kind]
                    qs = 0.125 if kind == "q" else 1.0

                    def evac(e, dst=dst, qs=qs):
                        e.activation(dst[0][0:64, :], PRJ[0:64, 0:2048],
                                     AF.Copy, scale=qs)
                        return e.activation(dst[1][0:64, :],
                                            PRJ[64:128, 0:2048], AF.Copy,
                                            scale=qs)
                    cm = p.step("c", evac, waits=[(s_t, tm), (s_v, vm)],
                                inc=s_c)
                    tm2 = s_t.n
                else:
                    cm = p.step("c", lambda e: e.activation(
                        VSTG[:], PRJ[:, 0:2048], AF.Copy),
                        waits=[(s_t, tm), (s_v, vm)], inc=s_c)
                    dstv = VT if kind == "v" else VPT

                    def vtr(e, dstv=dstv):
                        last = None
                        for i in range(16):
                            for h in range(2):
                                last = e.transpose(
                                    RB[:, 0:64].bitcast(BF16)[:, 0:64],
                                    VSTG[h * 64:(h + 1) * 64,
                                         i * 128:(i + 1) * 128], ID16[:])
                        return last
                    cm2 = cm
                    for i in range(16):
                        for h in range(2):
                            tmx = p.step("t", (lambda i, h: lambda e:
                                e.transpose(RB[:, 0:64],
                                            VSTG[h * 64:(h + 1) * 64,
                                                 i * 128:(i + 1) * 128],
                                            ID32[h * 64:(h + 1) * 64,
                                                 h * 64:(h + 1) * 64]))(i, h),
                                waits=[(s_c, cm2)] if (i == 0 and h == 0)
                                else [(s_c, s_c.n)], inc=s_t)
                            p.step("c", (lambda i, h, dstv: lambda e:
                                e.activation(
                                    dstv[h][:, i * 65:i * 65 + 64],
                                    RB[:, 0:64], AF.Copy))(i, h, dstv),
                                waits=[(s_t, tmx)], inc=s_c)
                    cm = s_c.n
                prj_marks.append((s_c.n, s_v.n, s_t.n))
            prj_read_m[0] = s_t.n
            prev_cv = (s_c.n, s_v.n)
            if upto in ("proj", "projm", "proje", "projr"):
                continue
            vab = s_v.n

            def abred(e):
                e.tensor_scalar(AB[:, 3:4], AB[:, 0:1], 0.0, 0.0,
                                ALU.mult, ALU.add)
                return e.engine_nop() if False else e.tensor_reduce(
                    AB[:, 3:4], AB[:, 0:3], AX.X, ALU.max)
            p.step("v", lambda e: e.tensor_reduce(
                AB[:, 3:4], AB[:, 0:3], AX.X, ALU.max),
                waits=[(s_v, vab)], inc=s_v)
            vab2 = s_v.n
            ttr = p.step("t", lambda e: e.transpose(
                RB[0:1, 0:128], AB[:, 3:4], ID32[:]),
                waits=[(s_v, vab2), (s_c, s_c.n)], inc=s_t)
            cr1 = p.step("c", lambda e: e.activation(
                ONEROW[0:1, 0:128] if False else R_TMP[0:1, 0:128],
                RB[0:1, 0:128], AF.Copy), waits=[(s_t, ttr)], inc=s_c)
            vmx = p.step("v", lambda e: e.tensor_reduce(
                M1[:], R_TMP[0:1, 0:128], AX.X, ALU.max),
                waits=[(s_c, cr1)], inc=s_v)

            csq = p.step("c", lambda e: e.activation(
                CS[0:1, 0:1], M1[:], AF.Square, scale=0.3535533906),
                waits=[(s_v, vmx)], inc=s_c)

            def crow_v(e):
                e.tensor_single_scalar(CS[0:1, 1:2], CS[0:1, 0:1], -1.0,
                                       ALU.mult)
                e.scalar_tensor_tensor(CS[0:1, 2:3], CS[0:1, 0:1], -1.0,
                                       FLG[0:1, 0:1], ALU.mult, ALU.add)
                e.tensor_single_scalar(KA[0][64:65, :], ONEROW[:],
                                       CS[0:1, 1:2], ALU.mult)
                e.tensor_single_scalar(KA[1][64:65, :], ONEROW[:],
                                       CS[0:1, 1:2], ALU.mult)
                e.tensor_single_scalar(KP[0][64:65, :], ONEROW[:],
                                       CS[0:1, 2:3], ALU.mult)
                return e.tensor_single_scalar(KP[1][64:65, :], ONEROW[:],
                                              CS[0:1, 2:3], ALU.mult)
            cc = p.step("v", crow_v, waits=[(s_c, csq), (s_x, cl)], inc=s_v)
            if upto == "bound":
                continue

            for h in range(2):
                for qc in range(4):
                    kts = [("p", kt) for kt in range(nKT)] +                           [("o", kt) for kt in range(4 * qc + 4)]
                    av_marks = []
                    for idx, (reg, kt) in enumerate(kts):
                        sc = SC1 if idx % 2 == 0 else SC2
                        ksrc = KP[h] if reg == "p" else KA[h]
                        vsrc = VPT[h] if reg == "p" else VT[h]

                        def smm(e, sc=sc, ksrc=ksrc, kt=kt, h=h, qc=qc):
                            return e.matmul(
                                sc[:], ksrc[:, kt * 128:(kt + 1) * 128]
                                if False else ksrc[0:65,
                                                   kt * 128:(kt + 1) * 128],
                                QA[h][0:65, qc * 512:(qc + 1) * 512],
                                start=True, stop=True)
                        w = [(s_v, cc)]
                        if idx >= 2:
                            w.append((s_c, exp_marks[(g, h, qc, idx - 2)]))
                        tsm = p.step("t", smm, waits=w, inc=s_t)
                        diag = (reg == "o" and kt >= 4 * qc)
                        vm = None
                        if diag:
                            j = kt - 4 * qc
                            vm = p.step("v", (lambda sc, j: lambda e:
                                e.tensor_tensor(
                                    sc[:], sc[:],
                                    DM[:, j * 512:(j + 1) * 512], ALU.add))(
                                        sc, j), waits=[(s_t, tsm)], inc=s_v)
                        wex = [(s_t, tsm)] if vm is None else [(s_v, vm)]
                        if idx >= 2:
                            wex.append((s_t, av_marks[idx - 2]))
                        cex = p.step("c", (lambda sc, pt: lambda e:
                            e.activation(PT[pt][:], sc[:], AF.Exp))(
                                sc, idx % 2), waits=wex, inc=s_c)
                        exp_marks[(g, h, qc, idx)] = cex

                        def avmm(e, vsrc=vsrc, kt=kt, first=(idx == 0),
                                 last_=(idx == len(kts) - 1), pt=idx % 2):
                            return e.matmul(
                                PO[0:65, :], vsrc[:, kt * 65:(kt + 1) * 65],
                                PT[pt][:], start=first, stop=last_)
                        wav = [(s_c, cex)]
                        if idx == 0 and yatt_marks:
                            wav.append((s_v, yatt_marks[-1]))
                        tav = p.step("t", avmm, waits=wav, inc=s_t)
                        av_marks.append(tav)
                    vrl = p.step("v", lambda e: e.reciprocal(
                        RL32[:], PO[64:65, 0:512]),
                        waits=[(s_t, av_marks[-1])], inc=s_v)
                    crl = p.step("c", lambda e: e.activation(
                        RL16[:], RL32[:], AF.Copy), waits=[(s_v, vrl)],
                        inc=s_c)
                    trb = p.step("t", lambda e: e.matmul(
                        RB[0:64, :], ONES[0:1, 0:64], RL16[:],
                        start=True, stop=True), waits=[(s_c, crl)], inc=s_t)
                    crb = p.step("c", lambda e: e.activation(
                        RBS[:], RB[0:64, :], AF.Copy), waits=[(s_t, trb)],
                        inc=s_c)
                    ym = p.step("v", (lambda g, h, qc: lambda e:
                        e.tensor_tensor(
                            YATT[h * 64:(h + 1) * 64 + 0,
                                 g * T_ + qc * 512: g * T_ + (qc + 1) * 512]
                            if h == 0 else
                            YATT[64:128,
                                 g * T_ + qc * 512: g * T_ + (qc + 1) * 512],
                            PO[0:64, :], RBS[:], ALU.mult))(g, h, qc),
                        waits=[(s_c, crb)], inc=s_v)
                    yatt_marks.append(ym)
        vint = s_v.n
        if upto in ("ln", "proj", "bound", "attn", "projw", "projm", "proje", "projr"):
            def final1(e):
                e.wait_ge(s_v.h, s_v.n)
            p.q["g"].append(final1)
            p.run_block()
            return nc

        wo_load = [None] * KC
        wo_mm = [None] * KC
        ep_m = [None] * KC
        for m in range(KC):
            w = []
            if m >= 2:
                w.append((s_t, wo_mm[m - 2]))
            wo_load[m] = p.step("g", (lambda m: lambda e: e.dma_start(
                out=WOB[m % 2][:], in_=wo_r[m].rearrange("p k c -> p (k c)")))(
                    m), waits=w, inc=s_w, k=16)
            pos_ = ([PRJ[:, 0:512], PRJ[:, 512:1024], PRJ[:, 1024:1536],
                     PRJ[:, 1536:2048]] if m % 2 == 0 else
                    [SC1[:], SC2[:], PO[:], RB[:]])

            def womm(e, m=m, pos_=pos_):
                last = None
                buf = WOB[m % 2]
                for k in range(KC):
                    for tc in range(4):
                        last = e.matmul(
                            pos_[tc], buf[:, k * 128:(k + 1) * 128],
                            YATT[:, k * T_ + tc * 512:k * T_ + (tc + 1) * 512],
                            start=(k == 0), stop=(k == KC - 1))
                return last
            w = [(s_w, wo_load[m]), (s_v, vint)]
            if m >= 2:
                w.append((s_v, ep_m[m - 2]))
            wo_mm[m] = p.step("t", womm, waits=w, inc=s_t)
            xr = p.step("y", (lambda m: lambda e: e.dma_start(
                out=XRE[:, (m % 2) * T_:(m % 2 + 1) * T_],
                in_=xT[m * 128:(m + 1) * 128, :]))(m),
                waits=[(s_v, vint)] if m < 2 else [(s_v, ep_m[m - 2])],
                inc=s_x, k=16)

            def ep1(e, m=m, pos_=pos_):
                last = None
                for tc in range(4):
                    last = e.tensor_tensor(
                        OUT32[:, tc * 512:(tc + 1) * 512], pos_[tc],
                        XRE[:, (m % 2) * T_ + tc * 512:
                            (m % 2) * T_ + (tc + 1) * 512], ALU.add)
                return last
            ep_m[m] = p.step("v", ep1,
                             waits=[(s_t, wo_mm[m]), (s_x, xr),
                                    (s_st, 16 * m)], inc=s_v)
            p.step("y", (lambda m: lambda e: e.dma_start(
                out=yT[m * 128:(m + 1) * 128, :], in_=OUT32[:]))(m),
                waits=[(s_v, ep_m[m])], inc=s_st, k=16)

        def final(e):
            e.wait_ge(s_st.h, s_st.n)
        p.q["g"].append(final)
        p.run_block()
    return nc


def l0attn_host_inputs(W_qkv, W_o):
    wqT = np.ascontiguousarray(W_qkv.T)
    wq_r = np.ascontiguousarray(
        wqT.reshape(8, 128, 24, 128).transpose(2, 1, 0, 3))
    woT = np.ascontiguousarray(W_o.T)
    wo_r = np.ascontiguousarray(
        woT.reshape(8, 128, 8, 128).transpose(2, 1, 0, 3))
    dm = np.full((4, 128, 512), -1.0e38, np.float32)
    for j in range(4):
        for kk in range(128):
            dm[j, kk, j * 128 + kk:] = 0.0
    ident = np.eye(128, dtype=np.float32)
    dm = np.ascontiguousarray(dm.transpose(1, 0, 2).reshape(128, 2048))
    return dict(wq_r=wq_r, wo_r=wo_r, dmask=dm, ident=ident)




def emit_quant(p, E2, XIN, QOUT, W, s_v, waits):
    BM, RS = E2["BM"], E2["RS"]
    XN, GS, ZR = E2["XN"][:, 0:W], E2["GS"][:, 0:W], E2["ZR"][:, 0:W]
    SGN = E2["SGN"][:, 0:W]
    nb = W // 16

    def q(e):
        x3 = XIN.rearrange("p (w k) -> p w k", k=16)
        e.tensor_reduce(BM[:, 0:nb], x3, AX.X, ALU.max,
                        apply_absolute_value=True)
        e.tensor_scalar(RS[:, 0:nb], BM[:, 0:nb], 1.0 / 6.0, 1e-12,
                        ALU.mult, ALU.max)
        e.reciprocal(BM[:, 0:nb], RS[:, 0:nb])
        rrep = BM[:, 0:nb].unsqueeze(2).broadcast_to((128, nb, 16))
        e.tensor_tensor(XN.rearrange("p (w k) -> p w k", k=16),
                        XIN.rearrange("p (w k) -> p w k", k=16), rrep,
                        ALU.mult)
        e.tensor_single_scalar(ZR.bitcast(I32), XN.bitcast(I32), 0x7FFFFFFF,
                               ALU.bitwise_and)
        e.tensor_single_scalar(GS, ZR, 0.25, ALU.is_gt)
        e.tensor_single_scalar(SGN, ZR, 0.75, ALU.is_gt)
        e.tensor_scalar(ZR.bitcast(I32), ZR.bitcast(I32),
                        0x3F800000, 0x1FFFFF, ALU.max, ALU.add)
        e.tensor_single_scalar(ZR.bitcast(I32), ZR.bitcast(I32), -0x400000,
                               ALU.bitwise_and)
        e.scalar_tensor_tensor(ZR, ZR, -0.5, SGN, ALU.add, ALU.mult)
        e.scalar_tensor_tensor(ZR, GS, 0.5, ZR, ALU.mult, ALU.add)
        e.tensor_single_scalar(SGN.bitcast(I32), XN.bitcast(I32),
                               -0x80000000, ALU.bitwise_and)
        e.tensor_tensor(ZR.bitcast(I32), ZR.bitcast(I32), SGN.bitcast(I32),
                        ALU.bitwise_or)
        srep = RS[:, 0:nb].unsqueeze(2).broadcast_to((128, nb, 16))
        return e.tensor_tensor(
            ZR.rearrange("p (w k) -> p w k", k=16),
            ZR.rearrange("p (w k) -> p w k", k=16), srep, ALU.mult)
    return p.step("v", q, waits=waits, inc=s_v)


def build_l1ffn(T_=T, debug=False):
    TH = T_ // 2
    NI = 2816
    NIC = NI // 128
    NCH = 6
    nch_w = [512] * 5 + [256]
    nch_off = [0, 512, 1024, 1536, 2048, 2560]
    nc = bass.Bass("TRN2", target_bir_lowering=False)
    xT = nc.dram_tensor("xT", [D, T_], F32, kind="ExternalInput").ap()
    wg = nc.dram_tensor("wg", [NI, D], F32, kind="ExternalInput").ap()
    wu = nc.dram_tensor("wu", [NI, D], F32, kind="ExternalInput").ap()
    wd = nc.dram_tensor("wd", [D, NI], F32, kind="ExternalInput").ap()
    ident = nc.dram_tensor("ident", [128, 128], F32,
                           kind="ExternalInput").ap()
    yT = nc.dram_tensor("yT", [D, T_], F32, kind="ExternalOutput").ap()
    wgq_t = nc.dram_tensor("wgq_t", [128, KC, NI], BF16, kind="Internal").ap()
    wuq_t = nc.dram_tensor("wuq_t", [128, KC, NI], BF16, kind="Internal").ap()
    wdq_t = nc.dram_tensor("wdq_t", [KC, 128, NIC, 128], BF16,
                           kind="Internal").ap()
    h2q_d = nc.dram_tensor("h2q_d", [128, NIC * TH], BF16,
                           kind="Internal").ap()
    if debug:
        dbg_xqt = nc.dram_tensor("dbg_xqt", [128, KC * (T_ // 2)], F32,
                                 kind="ExternalOutput").ap()
        dbg_h2f = nc.dram_tensor("dbg_h2f", [128, 22 * (T_ // 2)], F32,
                                 kind="ExternalOutput").ap()
        dbg_wg = nc.dram_tensor("dbg_wg", [128, KC * 2816], F32,
                                kind="ExternalOutput").ap()
        dbg_wmv = nc.dram_tensor("dbg_wmv", [128, KC * 512], F32,
                                 kind="ExternalOutput").ap()
        dbg_gate = nc.dram_tensor("dbg_gate", [128, 512], F32,
                                  kind="ExternalOutput").ap()

    es = ExitStack()
    with es:
        p = Prog(nc, es)
        X = p.sbuf("X", [128, KC * TH], F32)
        XQT = p.sbuf("XQT", [128, KC * TH], BF16)
        H2F = p.sbuf("H2F", [128, NIC * TH], BF16)
        WSTG = p.sbuf("WSTG", [128, NI], F32)
        QS = p.sbuf("QS", [128, NI], BF16)
        XTM = p.sbuf("XTM", [128, 1024], F32)
        XLN = p.sbuf("XLN", [128, 1024], F32)
        BM = p.sbuf("BM", [128, NI // 16], F32)
        RS = p.sbuf("RS", [128, NI // 16], F32)
        H2F32 = H2F[:].bitcast(F32)
        XN = H2F32[:, 0:NI]
        GS = H2F32[:, NI:2 * NI]
        ZR = H2F32[:, 2 * NI:3 * NI]
        SGN = H2F32[:, 3 * NI:4 * NI]
        MUC = p.sbuf("MUC", [128, 16], F32)
        RSC = p.sbuf("RSC", [128, 16], F32)
        SG = p.sbuf("SG", [128, 512], F32)
        H2 = p.sbuf("H2", [128, 1024], F32)
        WMV = [p.sbuf(f"WMV{i}", [128, KC * 512], BF16) for i in range(2)]
        WDB = [p.sbuf(f"WDB{i}", [128, NIC * 128], BF16) for i in range(2)]
        STG = p.sbuf("STG", [128, 256], BF16)
        ID32 = p.sbuf("ID32", [128, 128], F32)
        ID16 = p.sbuf("ID16", [128, 128], BF16)
        ONES = p.sbuf("ONES", [128, 128], BF16)
        R_MU = p.sbuf("R_MU", [1, TH], F32)
        R_VAR = p.sbuf("R_VAR", [1, TH], F32)
        R_RSTD = p.sbuf("R_RSTD", [1, TH], F32)
        R_TMP = p.sbuf("R_TMP", [1, TH], F32)
        R16_MU = p.sbuf("R16_MU", [1, TH], BF16)
        R16_RSTD = p.sbuf("R16_RSTD", [1, TH], BF16)
        SQ = p.sbuf("SQ", [128, TH], BF16)
        XH1 = p.sbuf("XH1", [128, TH], BF16)
        MU_B = p.sbuf("MU_B", [128, TH], F32)
        RSTD_B = p.sbuf("RSTD_B", [128, TH], F32)
        EPS5 = p.sbuf("EPS5", [1, 1], F32)

        P1 = p.psum("P1", [128, 1024])
        P2 = p.psum("P2", [128, 1024])
        P3 = p.psum("P3", [128, 1024])
        P4 = p.psum("P4", [128, 1024])

        s_t = p.sem("s_t")
        s_v = p.sem("s_v")
        s_c = p.sem("s_c")
        s_x = p.sem("s_x")
        s_w = p.sem("s_w")
        s_st = p.sem("s_st")
        s_m = p.sem("s_m")

        E = dict(s_t=s_t, s_v=s_v, s_c=s_c, SQ=SQ, P1=P1, P2=P2, P3=P3,
                 P4=P4, ONES=ONES, R_MU=R_MU, R_VAR=R_VAR, R_RSTD=R_RSTD,
                 R_TMP=R_TMP, R16_MU=R16_MU, R16_RSTD=R16_RSTD,
                 MU_B=MU_B, RSTD_B=RSTD_B)
        E2 = dict(BM=BM, RS=RS, XN=XN, GS=GS, ZR=ZR, SGN=SGN)

        p.step("y", lambda e: e.dma_start(out=ID32[:], in_=ident),
               inc=s_x, k=16)
        p.step("g", lambda e: e.dma_start(out=ID16[:], in_=ident),
               inc=s_w, k=16)

        def const_g(e):
            e.memset(ONES[:], 1.0)
            e.memset(H2[:], 1.0)
            return e.memset(EPS5[:], 1e-5)
        p.step("g", const_g, waits=[(s_w, s_w.n)], inc=s_m)
        const_w = s_m.n

        def quant_weight(wsrc, nrows, freew, dst_kind):
            ntile = nrows // 128
            for wt in range(ntile):
                lw = p.step("y", (lambda wt: lambda e: e.dma_start(
                    out=WSTG[:, 0:freew],
                    in_=wsrc[wt * 128:(wt + 1) * 128, :]))(wt),
                    waits=[(s_v, s_v.n)], inc=s_x, k=16)
                qm = emit_quant(p, E2, WSTG[:, 0:freew], None,
                                freew, s_v,
                                [(s_x, lw), (s_m, const_w), (s_t, s_t.n)])
                ZRv = E2["ZR"]
                for j in range(freew // 128):
                    st_sl = STG[:, (j % 2) * 128:(j % 2) * 128 + 128]
                    w = [(s_v, qm)] if j < 2 else [(s_w, s_w.n)]
                    tmark = p.step("t", (lambda j, ZRv: lambda e: e.transpose(
                        P1[:, 0:128],
                        ZRv[:, j * 128:(j + 1) * 128], ID32[:]))(j, ZRv),
                        waits=w + ([(s_c, s_c.n)] if j > 0 else []),
                        inc=s_t)
                    cm = p.step("c", (lambda st_sl: lambda e: e.activation(
                        st_sl, P1[:, 0:128], AF.Copy))(
                            st_sl), waits=[(s_t, tmark)], inc=s_c)
                    if dst_kind == "g":
                        dap = wgq_t[:, j, wt * 128:(wt + 1) * 128]
                    elif dst_kind == "u":
                        dap = wuq_t[:, j, wt * 128:(wt + 1) * 128]
                    else:
                        dap = wdq_t[wt, :, j, :]
                    p.step("g", (lambda dap, st_sl: lambda e: e.dma_start(
                        out=dap, in_=st_sl))(dap, st_sl),
                        waits=[(s_c, cm)], inc=s_w, k=16)
        quant_weight(wg, NI, 1024, "g")
        quant_weight(wu, NI, 1024, "u")
        quant_weight(wd, D, NI, "d")
        w_spill = s_w.n

        for half in range(2):
            t0 = half * TH
            xloads = []
            for k in range(KC):
                m = p.step("y", (lambda k, t0: lambda e: e.dma_start(
                    out=X[:, k * TH:(k + 1) * TH],
                    in_=xT[k * 128:(k + 1) * 128, t0:t0 + TH]))(k, t0),
                    waits=[(s_st, 16 * KC * half), (s_v, s_v.n)],
                    inc=s_x, k=16)
                xloads.append(m)
            X_chunks = [X[:, k * TH:(k + 1) * TH] for k in range(KC)]
            xh_chunks = [XH1[:]] * KC

            def wait_first(k, xloads=xloads):
                w = [(s_x, xloads[k])]
                if k == 0:
                    w.append((s_m, const_w))
                return w
            cmu = emit_ln_stats(p, E, X_chunks, xh_chunks, TH,
                                EPS5[0:1, 0:1], wait_first)
            for tt in range(TH // 128):
                tm1 = p.step("t", (lambda tt: lambda e: e.transpose(
                    P2[:, 0:128], _row(MU_B, tt), ID32[0:1, 0:128]))(tt),
                    waits=[(s_c, cmu), (s_c, s_c.n)], inc=s_t)
                cm1 = p.step("c", (lambda tt: lambda e: e.activation(
                    MUC[:, tt:tt + 1], P2[:, 0:1], AF.Copy))(tt),
                    waits=[(s_t, tm1)], inc=s_c)
                tm2 = p.step("t", (lambda tt: lambda e: e.transpose(
                    P2[:, 0:128], _row(RSTD_B, tt), ID32[0:1, 0:128]))(tt),
                    waits=[(s_c, cm1)], inc=s_t)
                cm2 = p.step("c", (lambda tt: lambda e: e.activation(
                    RSC[:, tt:tt + 1], P2[:, 0:1], AF.Copy))(tt),
                    waits=[(s_t, tm2)], inc=s_c)
            cmurs = s_c.n

            for tt in range(TH // 128):
                evs = []
                for k in range(KC):
                    w = ([(s_c, cmurs)] if (tt == 0 and k == 0)
                         else [(s_v, s_v.n), (s_c, s_c.n)])
                    tmx = p.step("t", (lambda k, tt: lambda e:
                        e.transpose(P3[:, 0:128],
                                    X[:, k * TH + tt * 128:
                                      k * TH + (tt + 1) * 128],
                                    ID32[:]))(k, tt), waits=w, inc=s_t)
                    cmx = p.step("c", (lambda k: lambda e: e.activation(
                        XTM[:, k * 128:(k + 1) * 128], P3[:, 0:128],
                        AF.Copy))(k), waits=[(s_t, tmx)], inc=s_c)
                    evs.append(cmx)

                def applyq(e, tt=tt):
                    return e.tensor_scalar(
                        XLN[:], XTM[:], MUC[:, tt:tt + 1], RSC[:, tt:tt + 1],
                        ALU.subtract, ALU.mult)
                va = p.step("v", applyq, waits=[(s_c, evs[-1])], inc=s_v)
                qm = emit_quant(p, E2, XLN[:], None, 1024, s_v,
                                [(s_v, va), (s_t, s_t.n)])
                ZRv = E2["ZR"]
                for k in range(KC):
                    w = [(s_v, qm)] if k == 0 else [(s_c, s_c.n)]
                    tmy = p.step("t", (lambda k, ZRv: lambda e: e.transpose(
                        P3[:, 0:128],
                        ZRv[:, k * 128:(k + 1) * 128], ID32[:]))(k, ZRv),
                        waits=w, inc=s_t)
                    p.step("c", (lambda k, tt: lambda e: e.activation(
                        XQT[:, k * TH + tt * 128:k * TH + (tt + 1) * 128],
                        P3[:, 0:128], AF.Copy))(k, tt),
                        waits=[(s_t, tmy)], inc=s_c)
            cxq = s_c.n
            vxq = s_v.n
            if debug and half == 0:
                p.step("g", lambda e: e.dma_start(out=dbg_xqt, in_=XQT[:]),
                       waits=[(s_c, cxq)], inc=s_w, k=16)
                p.step("g", lambda e: e.dma_start(
                    out=dbg_wg, in_=wgq_t[:, :, :]), waits=[(s_w, w_spill)],
                    inc=s_w, k=16)

            spill_marks = []
            for nch in range(NCH):
                wnn = nch_w[nch]
                lg_ = p.step("g", (lambda nch, wnn: lambda e: e.dma_start(
                    out=WMV[0][:, 0:KC * wnn],
                    in_=wgq_t[:, :, nch_off[nch]:nch_off[nch] + wnn]))(
                        nch, wnn),
                    waits=[(s_t, s_t.n), (s_w, w_spill)], inc=s_w, k=16)
                lu_ = p.step("g", (lambda nch, wnn: lambda e: e.dma_start(
                    out=WMV[1][:, 0:KC * wnn],
                    in_=wuq_t[:, :, nch_off[nch]:nch_off[nch] + wnn]))(
                        nch, wnn),
                    inc=s_w, k=16)
                if debug and half == 0 and nch == 0:
                    p.step("g", lambda e: e.dma_start(
                        out=dbg_wmv, in_=WMV[0][:]),
                        waits=[(s_w, lg_)], inc=s_w, k=16)
                for mt in range(TH // 128):
                    def gmm(e, mt=mt, wnn=wnn):
                        last = None
                        for k in range(KC):
                            last = e.matmul(
                                P1[:, 0:wnn],
                                XQT[:, k * TH + mt * 128:
                                    k * TH + (mt + 1) * 128],
                                WMV[0][:, k * wnn:(k + 1) * wnn],
                                start=(k == 0), stop=(k == KC - 1))
                        return last
                    w = [(s_w, lg_), (s_c, cxq), (s_v, vxq)]
                    if spill_marks:
                        w.append((s_w, spill_marks[-1]))
                    tg = p.step("t", gmm, waits=w, inc=s_t)

                    def umm(e, mt=mt, wnn=wnn):
                        last = None
                        for k in range(KC):
                            last = e.matmul(
                                P2[:, 0:wnn],
                                XQT[:, k * TH + mt * 128:
                                    k * TH + (mt + 1) * 128],
                                WMV[1][:, k * wnn:(k + 1) * wnn],
                                start=(k == 0), stop=(k == KC - 1))
                        return last
                    tu = p.step("t", umm, waits=[(s_w, lu_)], inc=s_t)
                    cs = p.step("c", (lambda wnn: lambda e: e.activation(
                        SG[:, 0:wnn], P1[:, 0:wnn], AF.Silu))(wnn),
                        waits=[(s_t, tg)], inc=s_c)
                    if debug and half == 0 and nch == 0 and mt == 0:
                        cdg = p.step("c", lambda e: e.activation(
                            XTM[:, 0:512], P1[:, 0:512], AF.Copy),
                            waits=[(s_t, tg)], inc=s_c)
                        p.step("g", lambda e: e.dma_start(
                            out=dbg_gate, in_=XTM[:, 0:512]),
                            waits=[(s_c, cdg)], inc=s_w, k=16)
                    vh2 = p.step("v", (lambda wnn: lambda e: e.tensor_tensor(
                        H2[:, 0:wnn], SG[:, 0:wnn], P2[:, 0:wnn],
                        ALU.mult))(wnn), waits=[(s_c, cs), (s_t, tu)],
                        inc=s_v)
                    qm2 = emit_quant(p, E2, H2[:, 0:1024], None,
                                     1024, s_v, [(s_v, vh2), (s_t, s_t.n)])
                    ZRv = E2["ZR"]
                    for j in range(wnn // 128):
                        st_sl = STG[:, (j % 2) * 128:(j % 2) * 128 + 128]
                        w2 = [(s_v, qm2)] if j < 2 else [(s_w, s_w.n)]
                        tmz = p.step("t", (lambda j, ZRv: lambda e:
                            e.transpose(P3[:, 0:128],
                                        ZRv[:, j * 128:(j + 1) * 128],
                                        ID32[:]))(j, ZRv),
                            waits=w2 + ([(s_c, s_c.n)] if j > 0 else []),
                            inc=s_t)
                        cmz = p.step("c", (lambda st_sl: lambda e:
                            e.activation(st_sl, P3[:, 0:128],
                                         AF.Copy))(st_sl),
                            waits=[(s_t, tmz)], inc=s_c)
                        kti = (nch_off[nch] + j * 128) // 128
                        hm = p.step("g", (lambda kti, mt, st_sl: lambda e:
                            e.dma_start(
                                out=h2q_d[:, kti * TH + mt * 128:
                                          kti * TH + (mt + 1) * 128],
                                in_=st_sl))(kti, mt, st_sl),
                            waits=[(s_c, cmz)], inc=s_w, k=16)
                        spill_marks.append(hm)
            h2_done = s_w.n

            h2f_m = p.step("y", lambda e: e.dma_start(out=H2F[:], in_=h2q_d),
                           waits=[(s_w, h2_done)], inc=s_x, k=16)
            if debug and half == 0:
                p.step("g", lambda e: e.dma_start(out=dbg_h2f, in_=H2F[:]),
                       waits=[(s_x, h2f_m)], inc=s_w, k=16)
            ep_marks = []
            for m in range(KC):
                w = [(s_w, w_spill)]
                if m >= 2:
                    w.append((s_t, ep_marks[m - 2][1]))
                lw = p.step("g", (lambda m: lambda e: e.dma_start(
                    out=WDB[m % 2][:],
                    in_=wdq_t[m].rearrange("p k c -> p (k c)")))(m),
                    waits=w, inc=s_w, k=16)
                po = P3 if m % 2 == 0 else P4

                def dmm(e, m=m, po=po):
                    last = None
                    buf = WDB[m % 2]
                    for k in range(NIC):
                        for tc in range(TH // 512):
                            last = e.matmul(
                                po[:, tc * 512:(tc + 1) * 512],
                                buf[:, k * 128:(k + 1) * 128],
                                H2F[:, k * TH + tc * 512:
                                    k * TH + (tc + 1) * 512],
                                start=(k == 0), stop=(k == NIC - 1))
                    return last
                w = [(s_w, lw), (s_x, h2f_m)]
                if m >= 2:
                    w.append((s_v, ep_marks[m - 2][0]))
                tdm = p.step("t", dmm, waits=w, inc=s_t)

                def ep(e, m=m, po=po):
                    sl = slice(m * TH, (m + 1) * TH)
                    return e.scalar_tensor_tensor(
                        X[:, sl], po[:, 0:TH], 0.5, X[:, sl],
                        ALU.mult, ALU.add)
                em = p.step("v", ep, waits=[(s_t, tdm)], inc=s_v)
                ep_marks.append((em, tdm))
                p.step("y", (lambda m, t0: lambda e: e.dma_start(
                    out=yT[m * 128:(m + 1) * 128, t0:t0 + TH],
                    in_=X[:, m * TH:(m + 1) * TH]))(m, t0),
                    waits=[(s_v, em)], inc=s_st, k=16)

        def final(e):
            e.wait_ge(s_st.h, s_st.n)
        p.q["g"].append(final)
        p.run_block()
    return nc


def _row(t, tt):
    return t[0:1, tt * 128:(tt + 1) * 128]


def _h2ap(h):
    return h.rearrange("k p t -> p (k t)")




def build_l1attn(T_=T):
    TH = T_ // 2
    nKT = T_ // 128
    nc = bass.Bass("TRN2", target_bir_lowering=False)
    xT = nc.dram_tensor("xT", [D, T_], F32, kind="ExternalInput").ap()
    xpT = nc.dram_tensor("xpT", [D, T_], F32, kind="ExternalInput").ap()
    wq_r = nc.dram_tensor("wq_r", [24, 128, KC, 128], F32,
                          kind="ExternalInput").ap()
    wo_r = nc.dram_tensor("wo_r", [KC, 128, KC, 128], F32,
                          kind="ExternalInput").ap()
    bqkv_c = nc.dram_tensor("bqkv_c", [128, 24], F32,
                            kind="ExternalInput").ap()
    gq_c = nc.dram_tensor("gq_c", [128, KC], F32, kind="ExternalInput").ap()
    bq_c = nc.dram_tensor("bq_c", [128, KC], F32, kind="ExternalInput").ap()
    gk_c = nc.dram_tensor("gk_c", [128, KC], F32, kind="ExternalInput").ap()
    bk_c = nc.dram_tensor("bk_c", [128, KC], F32, kind="ExternalInput").ap()
    bo_c = nc.dram_tensor("bo_c", [128, KC], F32, kind="ExternalInput").ap()
    utri = nc.dram_tensor("utri", [128, 128], F32, kind="ExternalInput").ap()
    flag = nc.dram_tensor("flag", [1, 1], F32, kind="ExternalInput").ap()
    ident = nc.dram_tensor("ident", [128, 128], F32,
                           kind="ExternalInput").ap()
    yT = nc.dram_tensor("yT", [D, T_], F32, kind="ExternalOutput").ap()

    es = ExitStack()
    with es:
        p = Prog(nc, es)
        YATT = p.sbuf("YATT", [128, KC * T_], BF16)
        X = YATT[:].bitcast(F32)
        XLH = p.sbuf("XLH", [128, KC * T_], BF16)
        XPH = p.sbuf("XPH", [128, KC * T_], BF16)
        XRE = XPH[:].bitcast(F32)[:, 0:2 * T_]
        OUT32 = XPH[:].bitcast(F32)[:, 2 * T_:3 * T_]
        MUBC = p.sbuf("MUBC", [128, T_], F32)
        WBC = p.sbuf("WBC", [128, T_], F32)
        KERQ = p.sbuf("KERQ", [128, T_], BF16)
        KERK = p.sbuf("KERK", [128, T_], BF16)
        KERKF = p.sbuf("KERKF", [128, T_], F32)
        VSTG = p.sbuf("VSTG", [128, T_], F32)
        KTOK = [p.sbuf(f"KTOK{h}", [128, nKT * 64], BF16) for h in range(2)]
        KPTOK = [p.sbuf(f"KPTOK{h}", [128, nKT * 64], BF16) for h in range(2)]
        VT = [p.sbuf(f"VT{h}", [128, nKT * 65], BF16) for h in range(2)]
        VPT = [p.sbuf(f"VPT{h}", [128, nKT * 65], BF16) for h in range(2)]
        KVB = p.sbuf("KVB", [128, 65], BF16)
        YU = [p.sbuf(f"YU{h}", [65, T_], BF16) for h in range(2)]
        PTB = p.sbuf("PTB", [128, 128], BF16)
        RBS = p.sbuf("RBS", [64, 512], BF16)
        UTRI = p.sbuf("UTRI", [128, 128], F32)
        FLG16 = p.sbuf("FLG16", [1, 1], BF16)
        FLGC = p.sbuf("FLGC", [128, 1], F32)
        WQB = [p.sbuf(f"WQB{i}", [128, 3 * KC * 128], BF16) for i in range(2)]
        BQKV = p.sbuf("BQKV", [128, 24], F32)
        BQ16 = p.sbuf("BQ16", [128, 24], BF16)
        BSR = p.sbuf("BSR", [1, 24], F32)
        BSQ = p.sbuf("BSQ", [1, 1], F32)
        BSK = p.sbuf("BSK", [1, 1], F32)
        GQ = p.sbuf("GQ", [128, KC], F32)
        BQ = p.sbuf("BQ", [128, KC], F32)
        GK = p.sbuf("GK", [128, KC], F32)
        BK = p.sbuf("BK", [128, KC], F32)
        BO = p.sbuf("BO", [128, KC], F32)
        ID32 = p.sbuf("ID32", [128, 128], F32)
        ONES = p.sbuf("ONES", [128, 128], BF16)
        ONES32 = p.sbuf("ONES32", [1, 128], F32)
        EPS5 = p.sbuf("EPS5", [1, 1], F32)
        EPS6 = p.sbuf("EPS6", [1, 1], F32)
        MQ = [p.sbuf(f"MQ{i}", [1, T_], BF16) for i in range(2)]
        MK = [p.sbuf(f"MK{i}", [1, T_], BF16) for i in range(2)]
        MP = [p.sbuf(f"MP{i}", [1, T_], BF16) for i in range(2)]
        SQ = KERK[:, 0:TH]
        XH1 = KERK[:, TH:2 * TH]
        MU_B = KERQ[:].bitcast(F32)[:, 0:TH]
        RSTD_B = VSTG[:, 0:TH]
        R_MU = MUBC[0:1, 0:TH]
        R_VAR = MUBC[0:1, TH:2 * TH]
        R_RSTD = WBC[0:1, 0:TH]
        R_TMP = WBC[0:1, TH:2 * TH]
        R16_MU = KTOK[0][0:1, 0:TH]
        R16_RSTD = KTOK[1][0:1, 0:TH]
        XL1 = KERKF[:, 0:TH]
        QSC = MUBC[:].bitcast(BF16)[:, 0:T_]
        SACC = WBC[0:1, 0:T_]
        RL32 = KERKF[0:1, 0:T_]
        TMPR = VSTG[0:1, 0:T_]
        WOB = [VSTG[:].bitcast(BF16)[:, i * KC * 128:(i + 1) * KC * 128]
               for i in range(2)]

        PRJ = p.psum("PRJ", [128, 2048])
        SC1 = p.psum("SC1", [128, 512])
        SC2 = p.psum("SC2", [128, 512])
        PO = p.psum("PO", [128, 512])
        RB = p.psum("RB", [128, 512])

        s_t = p.sem("s_t")
        s_v = p.sem("s_v")
        s_c = p.sem("s_c")
        s_x = p.sem("s_x")
        s_w = p.sem("s_w")
        s_st = p.sem("s_st")
        s_m = p.sem("s_m")

        E = dict(s_t=s_t, s_v=s_v, s_c=s_c, SQ=SQ,
                 P1=PRJ[:, 0:1024], P2=PRJ[:, 1024:2048],
                 P3=PRJ[:, 0:1024], P4=PRJ[:, 1024:2048],
                 ONES=ONES, R_MU=R_MU, R_VAR=R_VAR, R_RSTD=R_RSTD,
                 R_TMP=R_TMP, R16_MU=R16_MU, R16_RSTD=R16_RSTD,
                 MU_B=MU_B, RSTD_B=RSTD_B)

        p.step("y", lambda e: e.dma_start(out=UTRI[:], in_=utri), inc=s_x, k=16)
        p.step("y", lambda e: e.dma_start(out=ID32[:], in_=ident), inc=s_x, k=16)
        p.step("y", lambda e: e.dma_start(out=BQKV[:], in_=bqkv_c), inc=s_x, k=16)
        p.step("y", lambda e: e.dma_start(out=GQ[:], in_=gq_c), inc=s_x, k=16)
        p.step("y", lambda e: e.dma_start(out=BQ[:], in_=bq_c), inc=s_x, k=16)
        p.step("y", lambda e: e.dma_start(out=GK[:], in_=gk_c), inc=s_x, k=16)
        p.step("y", lambda e: e.dma_start(out=BK[:], in_=bk_c), inc=s_x, k=16)
        p.step("y", lambda e: e.dma_start(out=BO[:], in_=bo_c), inc=s_x, k=16)
        p.step("g", lambda e: e.dma_start(out=FLG16[:], in_=flag), inc=s_w, k=16)

        def const_g(e):
            e.memset(ONES[:], 1.0)
            e.memset(ONES32[:], 1.0)
            e.memset(EPS5[:], 1e-5)
            for h in range(2):
                e.memset(VT[h][:], 1.0)
                e.memset(VPT[h][:], 1.0)
            return e.memset(EPS6[:], 1e-6)
        p.step("g", const_g, waits=[(s_w, s_w.n)], inc=s_m)
        const_w = s_m.n
        cl = s_x.n
        tf = p.step("t", lambda e: e.matmul(RB[:, 0:1], ONES[0:1, :],
                                            FLG16[0:1, 0:1], start=True,
                                            stop=True),
                    waits=[(s_m, const_w)], inc=s_t)
        cf = p.step("c", lambda e: e.activation(FLGC[:], RB[:, 0:1], AF.Copy),
                    waits=[(s_t, tf)], inc=s_c)
        cb16 = p.step("c", lambda e: e.activation(BQ16[:], BQKV[:], AF.Copy),
                      waits=[(s_x, cl)], inc=s_c)
        tbs = p.step("t", lambda e: e.matmul(RB[0:1, 0:24], ONES[:, 0:1],
                                             BQ16[:, 0:24], start=True,
                                             stop=True),
                     waits=[(s_c, cb16), (s_c, cf)], inc=s_t)
        cbs = p.step("c", lambda e: e.activation(BSR[0:1, 0:24],
                                                 RB[0:1, 0:24], AF.Copy),
                     waits=[(s_t, tbs)], inc=s_c)
        vbs = p.step("v", lambda e: (
            e.tensor_reduce(BSQ[0:1, 0:1], BSR[0:1, 0:8], AX.X, ALU.add),
            e.tensor_reduce(BSK[0:1, 0:1], BSR[0:1, 8:16], AX.X,
                            ALU.add))[-1], waits=[(s_c, cbs)], inc=s_v)

        def do_ln(src_ap, dst):
            for th in range(2):
                t0 = th * TH
                vcur, ccur = s_v.n, s_c.n
                xloads = []
                for k in range(KC):
                    m = p.step("y", (lambda k, t0: lambda e: e.dma_start(
                        out=X[:, k * TH:(k + 1) * TH],
                        in_=src_ap[k * 128:(k + 1) * 128, t0:t0 + TH]))(k, t0),
                        waits=[(s_v, vcur), (s_c, ccur)], inc=s_x, k=16)
                    xloads.append(m)

                def wait_first(k, xloads=xloads):
                    w = [(s_x, xloads[k])]
                    if k == 0:
                        w.append((s_m, const_w))
                    return w
                X_chunks = [X[:, k * TH:(k + 1) * TH] for k in range(KC)]
                xh_chunks = [XH1[:]] * KC
                cmu = emit_ln_stats(p, E, X_chunks, xh_chunks, TH,
                                    EPS5[0:1, 0:1], wait_first)
                for k in range(KC):
                    def applya(e, k=k):
                        e.tensor_tensor(XL1[:], X[:, k * TH:(k + 1) * TH],
                                        MU_B[:], ALU.subtract)
                        return e.tensor_tensor(XL1[:], XL1[:], RSTD_B[:],
                                               ALU.mult)
                    w = [(s_c, cmu)] if k == 0 else [(s_c, s_c.n)]
                    p.step("v", applya, waits=w, inc=s_v)
                    va = s_v.n
                    p.step("c", (lambda k, t0: lambda e: e.activation(
                        dst[:, k * T_ + t0:k * T_ + t0 + TH], XL1[:],
                        AF.Copy))(k, t0), waits=[(s_v, va)], inc=s_c)
        do_ln(xT, XLH)
        do_ln(xpT, XPH)
        cln = s_c.n

        def qkv_weight_load(g, wb, extra_w):
            wl = p.step("g", (lambda g, wb: lambda e: e.dma_start(
                out=wb[:, 0:KC * 128],
                in_=wq_r[g, :, :, :].rearrange("p k c -> p (k c)")))(g, wb),
                waits=extra_w, inc=s_w, k=16)
            p.step("g", (lambda g, wb: lambda e: e.dma_start(
                out=wb[:, KC * 128:2 * KC * 128],
                in_=wq_r[8 + g, :, :, :].rearrange("p k c -> p (k c)")))(
                    g, wb), inc=s_w, k=16)
            return p.step("g", (lambda g, wb: lambda e: e.dma_start(
                out=wb[:, 2 * KC * 128:3 * KC * 128],
                in_=wq_r[16 + g, :, :, :].rearrange("p k c -> p (k c)")))(
                    g, wb), inc=s_w, k=16)

        def stat_pass(which, src, dst_rows):
            base = 0 if which == 0 else 8
            pss = [SC1, SC2, PO, RB]
            for mc in range(KC):
                wb = WQB[mc % 2]
                wl = p.step("g", (lambda mc, wb: lambda e: e.dma_start(
                    out=wb[:, 0:KC * 128],
                    in_=wq_r[base + mc, :, :, :]
                    .rearrange("p k c -> p (k c)")))(mc, wb),
                    waits=[(s_t, s_t.n)], inc=s_w, k=16)

                def prj(e, wb=wb, src=src):
                    last = None
                    for k in range(KC):
                        for tc in range(4):
                            last = e.matmul(
                                PRJ[:, tc * 512:(tc + 1) * 512],
                                wb[:, k * 128:(k + 1) * 128],
                                src[:, k * T_ + tc * 512:
                                    k * T_ + (tc + 1) * 512],
                                start=(k == 0), stop=(k == KC - 1))
                    return last
                w = [(s_w, wl), (s_c, cln), (s_c, s_c.n), (s_v, s_v.n)]
                tm = p.step("t", prj, waits=w, inc=s_t)
                cm1 = p.step("c", lambda e: e.activation(
                    QSC[:], PRJ[:, 0:2048], AF.Copy),
                    waits=[(s_t, tm)], inc=s_c)
                cm2 = p.step("c", (lambda mc, base: lambda e: e.activation(
                    KERQ[:], PRJ[:, 0:2048], AF.Square,
                    bias=BQKV[:, base + mc:base + mc + 1]))(mc, base),
                    waits=[(s_c, cm1)], inc=s_c)

                def summs(e):
                    last = None
                    for tc in range(4):
                        sl = slice(tc * 512, (tc + 1) * 512)
                        last = e.matmul(pss[tc][0:1, :], ONES[:, 0:1],
                                        QSC[:, sl], start=True, stop=True)
                    return last
                ts1 = p.step("t", summs, waits=[(s_c, cm2)], inc=s_t)

                def acc_sum(e, mc=mc):
                    last = None
                    for tc in range(4):
                        sl = slice(tc * 512, (tc + 1) * 512)
                        if mc == 0:
                            last = e.tensor_single_scalar(
                                SACC[0:1, sl], pss[tc][0:1, :],
                                1.0, ALU.mult)
                        else:
                            last = e.tensor_tensor(
                                SACC[0:1, sl], SACC[0:1, sl],
                                pss[tc][0:1, :], ALU.add)
                    return last
                va1 = p.step("v", acc_sum, waits=[(s_t, ts1)], inc=s_v)

                def sqmms(e):
                    last = None
                    for tc in range(4):
                        sl = slice(tc * 512, (tc + 1) * 512)
                        last = e.matmul(pss[tc][0:1, :], ONES[:, 0:1],
                                        KERQ[:, sl], start=True, stop=True)
                    return last
                ts2 = p.step("t", sqmms, waits=[(s_v, va1)], inc=s_t)

                def acc_sq(e, mc=mc):
                    last = None
                    for tc in range(4):
                        sl = slice(tc * 512, (tc + 1) * 512)
                        if mc == 0:
                            last = e.tensor_single_scalar(
                                RL32[0:1, sl], pss[tc][0:1, :], 1.0,
                                ALU.mult)
                        else:
                            last = e.tensor_tensor(
                                RL32[0:1, sl], RL32[0:1, sl],
                                pss[tc][0:1, :], ALU.add)
                    return last
                p.step("v", acc_sq, waits=[(s_t, ts2)], inc=s_v)
            bs = BSQ if which == 0 else BSK

            def rows5(e, bs=bs):
                e.tensor_single_scalar(SACC[0:1, :], SACC[0:1, :],
                                       bs[0:1, 0:1], ALU.add)
                e.tensor_single_scalar(SACC[0:1, :], SACC[0:1, :], 1.0 / D,
                                       ALU.mult)
                e.tensor_tensor(TMPR[0:1, :], SACC[0:1, :], SACC[0:1, :],
                                ALU.mult)
                e.scalar_tensor_tensor(RL32[0:1, :], RL32[0:1, :], 1.0 / D,
                                       TMPR[0:1, :], ALU.mult, ALU.subtract)
                return e.tensor_single_scalar(RL32[0:1, :], RL32[0:1, :],
                                              1024.0 / 1023.0, ALU.mult)
            vr = p.step("v", rows5, waits=[(s_v, s_v.n)], inc=s_v)
            cq = p.step("c", lambda e: e.activation(
                RL32[0:1, :], RL32[0:1, :], AF.Sqrt), waits=[(s_v, vr)],
                inc=s_c)

            def fin_w(e):
                e.tensor_single_scalar(RL32[0:1, :], RL32[0:1, :], 1e-6,
                                       ALU.add)
                return e.reciprocal(RL32[0:1, :], RL32[0:1, :])
            vf = p.step("v", fin_w, waits=[(s_c, cq)], inc=s_v)

            def cast16(e, dst_rows=dst_rows):
                e.activation(dst_rows[0][0:1, :], SACC[0:1, :], AF.Copy)
                return e.activation(dst_rows[1][0:1, :], RL32[0:1, :],
                                    AF.Copy)
            return p.step("c", cast16, waits=[(s_v, vf)], inc=s_c)

        sq_m = stat_pass(0, XLH, MQ)
        sk_m = stat_pass(1, XLH, MK)
        sp_m = stat_pass(1, XPH, MP)
        stats_done = (s_c.n, s_v.n, s_t.n)

        yatt_marks = []
        prev_cv = (s_c.n, s_v.n)
        for g in range(KC):
            wb = WQB[g % 2]
            w = [(s_c, sp_m)]
            if g >= 2:
                w.append((s_t, prev_prj[0]))
            wq_done = qkv_weight_load(g, wb, w)
            sets = [("q", 0, XLH, MQ, GQ, BQ), ("k", KC * 128, XLH, MK, GK, BK),
                    ("v", 2 * KC * 128, XLH, None, None, None),
                    ("kp", KC * 128, XPH, MP, GK, BK),
                    ("vp", 2 * KC * 128, XPH, None, None, None)]
            prj_marks = []
            for si, (kind, woff, src, rows, gam, bet) in enumerate(sets):
                def prjmm(e, woff=woff, src=src, wb=wb):
                    last = None
                    for k in range(KC):
                        for tc in range(4):
                            last = e.matmul(
                                PRJ[:, tc * 512:(tc + 1) * 512],
                                wb[:, woff + k * 128:woff + (k + 1) * 128],
                                src[:, k * T_ + tc * 512:
                                    k * T_ + (tc + 1) * 512],
                                start=(k == 0), stop=(k == KC - 1))
                    return last
                w = [(s_w, wq_done), (s_c, s_c.n), (s_v, s_v.n)]
                tm = p.step("t", prjmm, waits=w, inc=s_t)
                bcol = {"q": g, "k": 8 + g, "v": 16 + g,
                        "kp": 8 + g, "vp": 16 + g}[kind]
                if kind in ("v", "vp"):
                    vm = p.step("v", (lambda bcol: lambda e:
                        e.tensor_single_scalar(
                            VSTG[:], PRJ[:, 0:2048],
                            BQKV[:, bcol:bcol + 1], ALU.add))(bcol),
                        waits=[(s_t, tm), (s_c, s_c.n)], inc=s_v)
                    dstv = VT if kind == "v" else VPT
                    for i in range(nKT):
                        for h in range(2):
                            tmx = p.step("t", (lambda i, h: lambda e:
                                e.transpose(RB[:, 0:64],
                                            VSTG[h * 64:(h + 1) * 64,
                                                 i * 128:(i + 1) * 128],
                                            ID32[h * 64:(h + 1) * 64,
                                                 h * 64:(h + 1) * 64]))(i, h),
                                waits=[(s_v, vm)] if (i == 0 and h == 0)
                                else [(s_c, s_c.n)], inc=s_t)
                            p.step("c", (lambda i, h, dstv: lambda e:
                                e.activation(dstv[h][:, i * 65:i * 65 + 64],
                                             RB[:, 0:64], AF.Copy))(
                                    i, h, dstv), waits=[(s_t, tmx)], inc=s_c)
                else:
                    def bc_mu(e, rows=rows):
                        last = None
                        pss = [SC1, SC2, PO, RB]
                        for tc in range(4):
                            sl = slice(tc * 512, (tc + 1) * 512)
                            last = e.matmul(pss[tc][:], ONES[0:1, :],
                                            rows[0][0:1, sl], start=True,
                                            stop=True)
                        return last
                    tb1 = p.step("t", bc_mu,
                                 waits=[(s_c, s_c.n), (s_v, s_v.n)], inc=s_t)

                    def ev_mu(e):
                        pss = [SC1, SC2, PO, RB]
                        last = None
                        for tc in range(4):
                            sl = slice(tc * 512, (tc + 1) * 512)
                            last = e.activation(MUBC[:, sl], pss[tc][:],
                                                AF.Copy)
                        return last
                    cb1 = p.step("c", ev_mu, waits=[(s_t, tb1)], inc=s_c)

                    def bc_w(e, rows=rows):
                        last = None
                        pss = [SC1, SC2, PO, RB]
                        for tc in range(4):
                            sl = slice(tc * 512, (tc + 1) * 512)
                            last = e.matmul(pss[tc][:], ONES[0:1, :],
                                            rows[1][0:1, sl], start=True,
                                            stop=True)
                        return last
                    tb2 = p.step("t", bc_w, waits=[(s_c, cb1)], inc=s_t)

                    def ev_w(e):
                        pss = [SC1, SC2, PO, RB]
                        last = None
                        for tc in range(4):
                            sl = slice(tc * 512, (tc + 1) * 512)
                            last = e.activation(WBC[:, sl], pss[tc][:],
                                                AF.Copy)
                        return last
                    cb2 = p.step("c", ev_w, waits=[(s_t, tb2)], inc=s_c)

                    def appk(e, bcol=bcol):
                        e.tensor_single_scalar(KERKF[:], PRJ[:, 0:2048],
                                               BQKV[:, bcol:bcol + 1],
                                               ALU.add)
                        e.tensor_tensor(KERKF[:], KERKF[:], MUBC[:],
                                        ALU.subtract)
                        return e.tensor_tensor(KERKF[:], KERKF[:], WBC[:],
                                               ALU.mult)
                    vap = p.step("v", appk, waits=[(s_c, cb2), (s_t, tm)],
                                 inc=s_v)

                    def sq2(e, gam=gam, bet=bet, kind=kind, g=g):
                        if kind == "q":
                            return e.activation(KERQ[:], KERKF[:], AF.Square,
                                                bias=_gb(bet, g),
                                                scale=_gb(gam, g))
                        if kind == "k":
                            e.activation(KERK[:], KERKF[:], AF.Square,
                                         bias=_gb(bet, g), scale=_gb(gam, g))
                        return e.activation(VSTG[:], KERKF[:], AF.Square,
                                            bias=_gb(bet, g),
                                            scale=_gb(gam, g))
                    csq = p.step("c", sq2, waits=[(s_v, vap)], inc=s_c)
                    if kind == "kp":
                        def gate(e):
                            return e.tensor_single_scalar(
                                VSTG[:], VSTG[:], FLGC[:, 0:1], ALU.mult)
                        csq_v = p.step("v", gate,
                                       waits=[(s_c, csq), (s_c, cf)],
                                       inc=s_v)
                    if kind in ("k", "kp"):
                        dstt = KTOK if kind == "k" else KPTOK
                        for i in range(nKT):
                            for h in range(2):
                                wtt = [(s_c, s_c.n)]
                                if i == 0 and h == 0:
                                    wtt = ([(s_v, csq_v)] if kind == "kp"
                                           else [(s_c, csq)])
                                tmx = p.step("t", (lambda i, h: lambda e:
                                    e.transpose(
                                        RB[:, 0:64],
                                        VSTG[h * 64:(h + 1) * 64,
                                             i * 128:(i + 1) * 128],
                                        ID32[h * 64:(h + 1) * 64,
                                             h * 64:(h + 1) * 64]))(i, h),
                                    waits=wtt, inc=s_t)
                                p.step("c", (lambda i, h, dstt: lambda e:
                                    e.activation(
                                        dstt[h][:, i * 64:(i + 1) * 64],
                                        RB[:, 0:64], AF.Copy))(i, h, dstt),
                                    waits=[(s_t, tmx)], inc=s_c)
                prj_marks.append((s_c.n, s_v.n))
            prev_prj = (s_t.n,)
            prev_cv = (s_c.n, s_v.n)

            for h in range(2):
                def kvinit(e, h=h):
                    last = None
                    for i in range(nKT):
                        last = e.matmul(
                            SC1[0:64, 0:65],
                            KPTOK[h][:, i * 64:(i + 1) * 64],
                            VPT[h][:, i * 65:(i + 1) * 65],
                            start=(i == 0), stop=(i == nKT - 1))
                    return last
                tki = p.step("t", kvinit, waits=[(s_c, s_c.n), (s_v, s_v.n)],
                             inc=s_t)
                ckv = p.step("c", (lambda h: lambda e: e.activation(
                    KVB[h * 64:(h + 1) * 64, :], SC1[0:64, 0:65],
                    AF.Copy))(h), waits=[(s_t, tki)], inc=s_c)
                for c_ in range(nKT):
                    def amm(e, h=h, c_=c_):
                        return e.matmul(
                            SC2[:, 0:128],
                            KERK[h * 64:(h + 1) * 64,
                                 c_ * 128:(c_ + 1) * 128],
                            KERQ[h * 64:(h + 1) * 64,
                                 c_ * 128:(c_ + 1) * 128],
                            start=True, stop=True)
                    w = [(s_c, s_c.n), (s_v, s_v.n)]
                    tam = p.step("t", amm, waits=w, inc=s_t)
                    vmk = p.step("v", (lambda: lambda e: e.tensor_tensor(
                        SC2[:, 0:128], SC2[:, 0:128], UTRI[:], ALU.mult))(),
                        waits=[(s_t, tam), (s_x, cl)], inc=s_v)
                    cpt = p.step("c", lambda e: e.activation(
                        PTB[:], SC2[:, 0:128], AF.Copy),
                        waits=[(s_v, vmk)], inc=s_c)
                    def outc(e, h=h, c_=c_):
                        e.matmul(PO[0:65, 0:128],
                                 VT[h][:, c_ * 65:(c_ + 1) * 65], PTB[:],
                                 start=True, stop=False)
                        return e.matmul(
                            PO[0:65, 0:128], KVB[h * 64:(h + 1) * 64, :],
                            KERQ[h * 64:(h + 1) * 64,
                                 c_ * 128:(c_ + 1) * 128],
                            start=False, stop=True)
                    toc = p.step("t", outc, waits=[(s_c, cpt), (s_c, ckv)],
                                 inc=s_t)
                    cev = p.step("c", (lambda h, c_: lambda e: e.activation(
                        YU[h][0:65, c_ * 128:(c_ + 1) * 128],
                        PO[0:65, 0:128], AF.Copy))(h, c_),
                        waits=[(s_t, toc)], inc=s_c)
                    if c_ < nKT - 1:
                        def kvup(e, h=h, c_=c_):
                            return e.matmul(
                                SC1[0:64, 0:65],
                                KTOK[h][:, c_ * 64:(c_ + 1) * 64],
                                VT[h][:, c_ * 65:(c_ + 1) * 65],
                                start=False, stop=True)
                        tku = p.step("t", kvup, waits=[(s_t, toc)], inc=s_t)
                        ckv = p.step("c", (lambda h: lambda e: e.activation(
                            KVB[h * 64:(h + 1) * 64, :], SC1[0:64, 0:65],
                            AF.Copy))(h), waits=[(s_t, tku)], inc=s_c)
                cve = p.step("c", (lambda h: lambda e: e.activation(
                    RL32[0:1, :], YU[h][64:65, :], AF.Copy))(h),
                    waits=[(s_t, s_t.n)], inc=s_c)
                vrc = p.step("v", lambda e: (
                    e.tensor_single_scalar(RL32[0:1, :], RL32[0:1, :], 1e-6,
                                           ALU.add),
                    e.reciprocal(RL32[0:1, :], RL32[0:1, :]))[-1],
                    waits=[(s_c, cve)], inc=s_v)
                for qc in range(4):
                    trb = p.step("t", (lambda qc: lambda e: e.matmul(
                        RB[0:64, :], ONES32[0:1, 0:64],
                        RL32[0:1, qc * 512:(qc + 1) * 512], start=True,
                        stop=True))(qc), waits=[(s_v, s_v.n)],
                        inc=s_t)
                    crb = p.step("c", lambda e: e.activation(
                        RBS[:], RB[0:64, :], AF.Copy), waits=[(s_t, trb)],
                        inc=s_c)
                    ym = p.step("v", (lambda g, h, qc: lambda e:
                        e.tensor_tensor(
                            YATT[h * 64:(h + 1) * 64 if h == 0 else 128,
                                 g * T_ + qc * 512:g * T_ + (qc + 1) * 512],
                            YU[h][0:64, qc * 512:(qc + 1) * 512], RBS[:],
                            ALU.mult))(g, h, qc), waits=[(s_c, crb)],
                        inc=s_v)
                    yatt_marks.append(ym)
        vint = s_v.n

        wo_load = [None] * KC
        wo_mm = [None] * KC
        ep_m = [None] * KC
        for m in range(KC):
            w = [(s_t, prev_prj[0])] if m < 2 else [(s_t, wo_mm[m - 2])]
            wo_load[m] = p.step("g", (lambda m: lambda e: e.dma_start(
                out=WOB[m % 2], in_=wo_r[m].rearrange("p k c -> p (k c)")))(
                    m), waits=w, inc=s_w, k=16)
            pos_ = ([PRJ[:, 0:512], PRJ[:, 512:1024], PRJ[:, 1024:1536],
                     PRJ[:, 1536:2048]] if m % 2 == 0 else
                    [SC1[:], SC2[:], PO[:], RB[:]])

            def womm(e, m=m, pos_=pos_):
                last = None
                buf = WOB[m % 2]
                for k in range(KC):
                    for tc in range(4):
                        last = e.matmul(
                            pos_[tc], buf[:, k * 128:(k + 1) * 128],
                            YATT[:, k * T_ + tc * 512:k * T_ + (tc + 1) * 512],
                            start=(k == 0), stop=(k == KC - 1))
                return last
            w = [(s_w, wo_load[m]), (s_v, vint), (s_c, s_c.n)]
            if m >= 2:
                w.append((s_v, ep_m[m - 2]))
            wo_mm[m] = p.step("t", womm, waits=w, inc=s_t)
            xr = p.step("y", (lambda m: lambda e: e.dma_start(
                out=XRE[:, (m % 2) * T_:(m % 2 + 1) * T_],
                in_=xT[m * 128:(m + 1) * 128, :]))(m),
                waits=[(s_v, vint)] if m < 2 else [(s_v, ep_m[m - 2])],
                inc=s_x, k=16)

            def ep1(e, m=m, pos_=pos_):
                last = None
                for tc in range(4):
                    last = e.scalar_tensor_tensor(
                        OUT32[:, tc * 512:(tc + 1) * 512], pos_[tc],
                        BO[:, m:m + 1],
                        XRE[:, (m % 2) * T_ + tc * 512:
                            (m % 2) * T_ + (tc + 1) * 512],
                        ALU.add, ALU.add)
                return last
            ep_m[m] = p.step("v", ep1,
                             waits=[(s_t, wo_mm[m]), (s_x, xr),
                                    (s_st, 16 * m)], inc=s_v)
            p.step("y", (lambda m: lambda e: e.dma_start(
                out=yT[m * 128:(m + 1) * 128, :], in_=OUT32[:]))(m),
                waits=[(s_v, ep_m[m])], inc=s_st, k=16)

        def final(e):
            e.wait_ge(s_st.h, s_st.n)
        p.q["g"].append(final)
        p.run_block()
    return nc


def _gb(t, g):
    return t[:, g:g + 1]


def _sl16(srck, h, i):
    return srck[h * 64:(h + 1) * 64, i * 128:(i + 1) * 128]


def _id16h(ID16, h):
    return ID16[h * 64:(h + 1) * 64, h * 64:(h + 1) * 64]


def _pss4(a, b, c, d, tc):
    return [a, b, c, d][tc]


def l1attn_host_inputs(Wqkv, bqkv, gq, bq, gk, bk, Wo, bo):
    wq_r = np.ascontiguousarray(
        Wqkv.reshape(8, 128, 24, 128).transpose(2, 1, 0, 3))
    wo_r = np.ascontiguousarray(
        Wo.reshape(8, 128, 8, 128).transpose(2, 1, 0, 3))
    bqkv_c = np.ascontiguousarray(bqkv.reshape(24, 128).T)
    cols = lambda a: np.ascontiguousarray(a.reshape(8, 128).T)
    ut = np.triu(np.ones((128, 128), np.float32))
    return dict(wq_r=wq_r, wo_r=wo_r, bqkv_c=bqkv_c, gq_c=cols(gq),
                bq_c=cols(bq), gk_c=cols(gk), bk_c=cols(bk), bo_c=cols(bo),
                utri=ut, ident=np.eye(128, dtype=np.float32))


def kernel(**inputs):
    f32 = np.float32
    g = lambda k: np.asarray(inputs[k], f32)
    x = g("L0_attn_x")

    h_f0 = {k: np.ascontiguousarray(v, dtype=f32) for k, v in
            l0ffn_host_inputs(g("L0_ffn_W1"), g("L0_ffn_b1"),
                              g("L0_ffn_Wo"), g("L0_ffn_bo"),
                              g("L0_ffn_ln_gamma"), g("L0_ffn_ln_beta"),
                              g("L0_ffn_W_router")).items()}
    h_a0 = {k: np.ascontiguousarray(v, dtype=f32) for k, v in
            l0attn_host_inputs(g("L0_attn_W_qkv"),
                               g("L0_attn_W_o")).items()}
    h_f1 = dict(wg=g("L1_ffn_w_gate"), wu=g("L1_ffn_w_up"),
                wd=g("L1_ffn_w_down"), ident=np.eye(128, dtype=f32))
    h_a1 = {k: np.ascontiguousarray(v, dtype=f32) for k, v in
            l1attn_host_inputs(g("L1_attn_Wqkv"), g("L1_attn_bqkv"),
                               g("L1_attn_gamma_q"), g("L1_attn_beta_q"),
                               g("L1_attn_gamma_k"), g("L1_attn_beta_k"),
                               g("L1_attn_Wo"), g("L1_attn_bo")).items()}

    xT = [np.ascontiguousarray(x[c // 2, (c % 2) * T:(c % 2 + 1) * T, :].T)
          for c in range(NCORE)]
    ZP = np.zeros((D, T), f32)
    FNEG = np.full((1, 1), -1.0e38, f32)
    F0 = np.zeros((1, 1), f32)
    F1 = np.ones((1, 1), f32)

    def launch(nc, hosts):
        res = run_bass_kernel_spmd(nc, hosts, core_ids=list(range(NCORE)))
        return [np.ascontiguousarray(res.results[c]["yT"], dtype=f32)
                for c in range(NCORE)]

    def attn_hosts(hw, xT, fe, fo):
        hs = []
        for c in range(NCORE):
            if c % 2 == 0:
                hs.append(dict(hw, xT=xT[c], xpT=ZP, flag=fe))
            else:
                hs.append(dict(hw, xT=xT[c], xpT=xT[c - 1], flag=fo))
        return hs

    nc_f0 = build_l0ffn()
    nc_f1 = build_l1ffn()
    xT = launch(nc_f0, [dict(h_f0, xT=xT[c]) for c in range(NCORE)])
    xT = launch(build_l0attn(), attn_hosts(h_a0, xT, FNEG, F0))
    xT = launch(nc_f0, [dict(h_f0, xT=xT[c]) for c in range(NCORE)])
    xT = launch(nc_f1, [dict(h_f1, xT=xT[c]) for c in range(NCORE)])
    xT = launch(build_l1attn(), attn_hosts(h_a1, xT, F0, F1))
    xT = launch(nc_f1, [dict(h_f1, xT=xT[c]) for c in range(NCORE)])

    out = np.empty((4, 2 * T, D), f32)
    for c in range(NCORE):
        out[c // 2, (c % 2) * T:(c % 2 + 1) * T, :] = xT[c].T
    return out
